# version 07
# baseline (speedup 1.0000x reference)
import numpy as np
import ml_dtypes
from contextlib import ExitStack

import concourse.bass as bass
import concourse.mybir as mybir
from concourse.bass_utils import run_bass_kernel_spmd

F32 = mybir.dt.float32
BF16 = mybir.dt.bfloat16
AF = mybir.ActivationFunctionType
ALU = mybir.AluOpType
AX = mybir.AxisListType

NCORES = 8
NSEQ = 4
T = 2048
D = 1024
NCT = 8
H = 16
HD = 64
CH = 512
NCHUNK = T // CH
NTILE = T // 128

ENGS = ("tensor", "vector", "scalar", "gpsimd", "sync")
DEBUG = False


class Prog:

    def __init__(self, nc):
        self.nc = nc
        self.q = {e: [] for e in ENGS}
        self.cn = {e: 0 for e in ENGS}
        self.dn = {e: 0 for e in ENGS}
        self.hw = {}
        self.sems = {}

    def make_sems(self, stack):
        for e in ENGS:
            self.sems[("c", e)] = stack.enter_context(self.nc.semaphore(f"c_{e}"))
            self.sems[("d", e)] = stack.enter_context(self.nc.semaphore(f"d_{e}"))

    def _wait(self, eng_name, tok):
        kind, peng, n = tok
        val = n if kind == "c" else 16 * n
        key = (eng_name, (kind, peng))
        if self.hw.get(key, -1) >= val:
            return
        self.hw[key] = val
        sem = self.sems[(kind, peng)]
        self.q[eng_name].append(lambda eng, sem=sem, val=val: eng.wait_ge(sem, val))

    def op(self, eng_name, fn, after=(), inc=False):
        for tok in after:
            if tok is not None:
                self._wait(eng_name, tok)
        if inc:
            self.cn[eng_name] += 1
            n = self.cn[eng_name]
            key = ("c", eng_name)

            def wrapped(eng, fn=fn, key=key):
                fn(eng).then_inc(self.sems[key], 1)
            self.q[eng_name].append(wrapped)
            return ("c", eng_name, n)
        self.q[eng_name].append(fn)
        return None

    def dma(self, eng_name, out_ap, in_ap, after=()):
        for tok in after:
            if tok is not None:
                self._wait(eng_name, tok)
        self.dn[eng_name] += 1
        n = self.dn[eng_name]
        key = ("d", eng_name)

        def wrapped(eng, out_ap=out_ap, in_ap=in_ap, key=key):
            eng.dma_start(out_ap, in_ap).then_inc(self.sems[key], 16)
        self.q[eng_name].append(wrapped)
        return ("d", eng_name, n)

    def barrier(self):
        toks = [("c", e, self.cn[e]) for e in ENGS if self.cn[e] > 0]
        dtoks = [("d", e, self.dn[e]) for e in ENGS if self.dn[e] > 0]
        for e in ENGS:
            for t in toks:
                if t[1] != e:
                    self._wait(e, t)
            for t in dtoks:
                self._wait(e, t)

    def finalize(self, block):
        for e in ENGS:
            ops = self.q[e]

            def runner(eng, ops=ops):
                for f in ops:
                    f(eng)
            getattr(block, e)(runner)



def build_l0a(nc, P, stack, io):
    E = stack.enter_context
    sb = lambda name, shape, dt=F32: E(nc.sbuf_tensor(name, shape, dt))

    wp = sb("l0a_wp", [128, NCT, D], BF16)
    wm = sb("l0a_wm", [128, NCT, D], BF16)
    metric = sb("l0a_metric", [128, 8, 64], BF16)
    ident = sb("sb_ident_f32", [128, 128], F32)
    ones_b = sb("sb_ones_bf16", [128, 128], BF16)
    ones_f = sb("sb_ones_f32", [128, 128], F32)
    maskT = sb("sb_maskT_f32", [128, 128], F32)
    masklk = sb("sb_masklk_f32", [64, 64], F32)
    sel = sb("sb_sel_bf16", [16, NCT * 128], BF16)
    eps_t = sb("sb_eps_t", [128, 1], F32)
    ln8_t = sb("sb_ln8_t", [128, 1], F32)
    xb = [sb(f"l0a_xb{i}", [128, NCT, CH], BF16) for i in range(2)]
    xf = [sb(f"l0a_xf{i}", [128, NCT, CH], F32) for i in range(2)]
    pfm = sb("l0a_pfm", [128, NCT, T], BF16)
    ptok = sb("l0a_ptok", [128, NTILE, H * 65], BF16)
    qm = sb("l0a_qm", [128, 8, CH], BF16)
    probs = [sb(f"l0a_pr{i}", [128, CH], BF16) for i in range(4)]
    oraw = sb("l0a_oraw", [128, NCT, CH], BF16)
    dend = sb("l0a_dend", [16, CH], F32)
    dendb = sb("l0a_dendb", [16, CH], BF16)
    negB = sb("l0a_negB", [128, 1], F32)
    tinyB = sb("l0a_tinyB", [2, 1], F32)
    tinyC = sb("l0a_tinyC", [1, 1], F32)
    pn2run = sb("l0a_pn2run", [2, 1], F32)
    eb = sb("sb_eb_bf16", [128, 2], BF16)
    msum = sb("l0a_msum", [128, 8], F32)
    ebf = sb("sb_eb_f32", [128, 2], F32)
    lnc = sb("l0a_lnc", [1, 1], F32)
    pmstat = sb("l0a_pmstat", [64, H], F32)
    negmax = sb("l0a_negmax", [64, H], F32)
    rb = [sb(f"l0a_rb{i}", [128, CH], BF16) for i in range(2)]
    r2b = [sb(f"l0a_r2b{i}", [128, CH], BF16) for i in range(2)]
    ssum = sb("l0a_ssum", [1, CH], F32)
    ssq = sb("l0a_ssq", [1, CH], F32)
    u1_t = sb("l0a_u1", [128, 2 * CH], F32)
    u2_t = sb("l0a_u2", [128, 2 * CH], F32)
    mb = u1_t[:, 0:CH]
    m2 = u1_t[:, CH:2 * CH]
    vb = u2_t[:, 0:CH]
    rstd = u2_t[:, CH:2 * CH]
    pnacc = u1_t[0:2, 0:CH]
    pm = u1_t[0:64, :].rearrange("p (h l) -> p h l", h=H, l=64)
    pmT = u2_t[0:64, :].rearrange("p (h l) -> p h l", h=H, l=64)
    xo_t = sb("l0a_xo", [128, NCT, CH], F32)
    xo = xo_t[:, :, :]
    den = xo_t[:, :, :].rearrange("p a b -> p (a b)")

    psA = [E(nc.psum_tensor(f"l0a_psA{i}", [128, CH], F32)) for i in range(2)]
    psS = [E(nc.psum_tensor(f"l0a_psS{i}", [128, CH], F32)) for i in range(2)]
    psO = [E(nc.psum_tensor(f"l0a_psO{i}", [65, CH], F32)) for i in range(2)]
    psE1 = E(nc.psum_tensor("l0a_psE1", [128, CH], F32))
    psE2 = E(nc.psum_tensor("l0a_psE2", [128, CH], F32))

    t_id = P.dma("sync", ident[:, :], io["ident_f32"])
    t_ob = P.dma("sync", ones_b[:, :], io["ones_bf16"])
    t_of = P.dma("sync", ones_f[:, :], io["ones_f32"])
    t_mk = P.dma("sync", maskT[:, :], io["maskT_f32"])
    t_ml = P.dma("sync", masklk[:, :], io["maskLK_f32"])
    t_se = P.dma("sync", sel[:, :], io["sel_bf16"])
    t_eb = P.dma("sync", eb[:, :], io["eb_bf16"])
    t_ebf = P.dma("sync", ebf[:, :], io["eb_f32"])
    t_pm = P.dma("sync", pm[:, :, :], io["pre_metric"])
    t_wp = P.dma("gpsimd", wp[:, :, :], io["wproj_t"].transpose([1, 0, 2]))
    t_wm = P.dma("gpsimd", wm[:, :, :], io["wmix_t"].transpose([1, 0, 2]))

    t_eps0 = P.op("vector", lambda v: v.memset(eps_t[:, :], 1e-5), inc=True)
    t_eps = P.op("vector", lambda v: v.memset(ln8_t[:, :], float(np.log(8.0))), inc=True)
    ones_ap = ptok[:, :, :].rearrange("p j (h c) -> p j h c", h=H, c=65)[:, :, :, 64:65]
    t_po = P.op("vector", lambda v: v.memset(ones_ap, 1.0), inc=True)

    t_add = []
    for h in range(H):
        t_add.append(P.op("vector", lambda v, h=h: v.tensor_add(
            pm[:, h, :], pm[:, h, :], masklk[:, :]), after=(t_pm, t_ml), inc=True))
    t1 = P.op("vector", lambda v: v.tensor_reduce(pmstat[:, :], pm[:, :, :], AX.X, ALU.max),
              after=t_add, inc=True)
    t2 = P.op("vector", lambda v: v.tensor_scalar(negmax[:, :], pmstat[:, :], -0.125, None, ALU.mult),
              after=(t1,), inc=True)
    t_exp = []
    for h in range(H):
        t_exp.append(P.op("scalar", lambda s, h=h: s.activation(
            pm[:, h, :], pm[:, h, :], AF.Exp, bias=negmax[:, h:h + 1], scale=0.125),
            after=(t2,), inc=True))
    t3 = P.op("vector", lambda v: v.tensor_reduce(pmstat[:, :], pm[:, :, :], AX.X, ALU.add),
              after=t_exp, inc=True)
    t4 = P.op("vector", lambda v: v.reciprocal(pmstat[:, :], pmstat[:, :]), after=(t3,), inc=True)
    t_nrm = []
    for h in range(H):
        t_nrm.append(P.op("vector", lambda v, h=h: v.tensor_scalar(
            pm[:, h, :], pm[:, h, :], pmstat[:, h:h + 1], None, ALU.mult),
            after=(t4,), inc=True))
    t_tp = []
    for h in range(H):
        tk = P.op("tensor", lambda t, h=h: t.transpose(
            psA[0][0:64, 0:64], pm[:, h, :], ident[0:64, 0:64]),
            after=(t_nrm[h], t_id, t_tp[h - 1] if h else None), inc=True)
        t_tp.append(P.op("vector", lambda v, h=h: v.tensor_copy(
            pmT[:, h, :], psA[0][0:64, 0:64]), after=(tk,), inc=True))
    t_met = []
    for h in range(H):
        base = (h % 2) * 64
        tk = P.op("tensor", lambda t, h=h: t.matmul(
            psA[1][0:64, 0:64], pmT[:, h, :], pmT[:, h, :], start=True, stop=True),
            after=(t_tp[h], t_met[h - 1] if h else None), inc=True)
        t_met.append(P.op("vector", lambda v, h=h, base=base: v.tensor_copy(
            metric[base:base + 64, h // 2, :], psA[1][0:64, 0:64]), after=(tk,), inc=True))

    mf1 = P.op("scalar", lambda s_: s_.activation(
        probs[0][:, :].rearrange("p (a b) -> p a b", a=8, b=64), metric[:, :, :], AF.Square),
        after=t_met, inc=True)
    mf2 = P.op("vector", lambda v: v.tensor_reduce(
        msum[:, :], probs[0][:, :].rearrange("p (a b) -> p a b", a=8, b=64),
        AX.X, ALU.add), after=(mf1,), inc=True)
    mf3 = P.op("tensor", lambda t: t.matmul(
        psE1[0:2, 0:8], ebf[:, :], msum[:, :], start=True, stop=True),
        after=(mf2, t_ebf), inc=True)
    mf4 = P.op("vector", lambda v: v.tensor_reduce(
        tinyB[:, :], psE1[0:2, 0:8], AX.X, ALU.max), after=(mf3,), inc=True)
    mf5 = P.op("gpsimd", lambda g: g.tensor_reduce(
        tinyC[:, :], tinyB[:, :], AX.C, ALU.max), after=(mf4,), inc=True)
    mf6 = P.op("scalar", lambda s_: s_.activation(tinyC[:, :], tinyC[:, :], AF.Ln),
               after=(mf5,), inc=True)
    mf7 = P.op("vector", lambda v: v.tensor_scalar(
        lnc[:, :], tinyC[:, :], 0.5, -float(np.log(8.0)), ALU.mult, ALU.add),
        after=(mf6,), inc=True)
    t_lnc = mf7
    fr_probs0_init = mf3

    fr_xb = [None, None]
    fr_xf = [None, None]
    fr_psA = [t_tp[-1], t_met[-1]]
    fr_psS = [None, None]
    fr_psSA = [None, None, None, None]
    fr_psO = [None, None]
    fr_psE1 = None
    fr_psE2 = None
    fr_probs = [fr_probs0_init, None, None, None]
    fr_den = None
    fr_dend = None
    fr_dendb = None
    fr_qm = None
    fr_oraw = None
    fr_xo = None
    fr_rb = [None, None]
    fr_r2b = [None, None]

    pend = None
    for s in range(NSEQ):
        for c in range(NCHUNK):
            buf = c % 2
            cb = c * CH
            t_xb = P.dma("gpsimd", xb[buf][:, :, :],
                         io["xin"][s, :, :, cb:cb + CH].transpose([1, 0, 2]),
                         after=(fr_xb[buf],))
            t_xf = P.dma("sync", xf[buf][:, :, :],
                         io["xin"][s, :, :, cb:cb + CH].transpose([1, 0, 2]),
                         after=(fr_xf[buf],))

            ev_fm = []
            for mt in range(NCT):
                slot = mt % 2
                dep = ev_fm[-2] if len(ev_fm) >= 2 else fr_psA[slot]
                for ci in range(NCT):
                    tk = P.op("tensor", lambda t, mt=mt, ci=ci, slot=slot, buf=buf: t.matmul(
                        psA[slot][:, :], wp[:, ci, mt * 128:(mt + 1) * 128],
                        xb[buf][:, ci, :], start=(ci == 0), stop=(ci == NCT - 1)),
                        after=(t_xb, t_wp, dep) if ci == 0 else (), inc=(ci == NCT - 1))
                ev_fm.append(P.op("scalar", lambda v, mt=mt, slot=slot, cb=cb: v.copy(
                    pfm[:, mt, cb:cb + CH], psA[slot][:, :]), after=(tk,), inc=True))

            ev_tok = []
            last_tokmm = None
            for tt4 in range(4):
                jt = c * 4 + tt4
                for nch in range(2):
                    i = tt4 * 2 + nch
                    slot = i % 2
                    dep = ev_tok[-2] if len(ev_tok) >= 2 else ev_fm[-2 + slot]
                    for ci in range(NCT):
                        tk = P.op("tensor", lambda t, tt4=tt4, nch=nch, ci=ci, slot=slot, buf=buf: t.matmul(
                            psA[slot][:, :], xb[buf][:, ci, tt4 * 128:(tt4 + 1) * 128],
                            wp[:, ci, nch * 512:(nch + 1) * 512],
                            start=(ci == 0), stop=(ci == NCT - 1)),
                            after=(dep,) if ci == 0 else (), inc=(ci == NCT - 1))
                    last_tokmm = tk
                    dst = ptok[:, jt, :].rearrange("p (h c) -> p h c", h=H, c=65)[
                        :, nch * 8:(nch + 1) * 8, 0:64]
                    src = psA[slot][:, :].rearrange("p (h k) -> p h k", h=8, k=64)
                    ev_tok.append(P.op("scalar", lambda v, dst=dst, src=src: v.copy(dst, src),
                                       after=(tk, t_po), inc=True))
            fr_xb[buf] = last_tokmm

            ev_qm = []
            for h in range(H):
                base = (h % 2) * 64
                slot = h % 2
                dep = ev_qm[-2] if len(ev_qm) >= 2 else ev_tok[-2 + slot]
                tk = P.op("tensor", lambda t, h=h, base=base, slot=slot, cb=cb: t.matmul(
                    psA[slot][base:base + 64, :], metric[base:base + 64, h // 2, :],
                    pfm[base:base + 64, h // 2, cb:cb + CH], start=True, stop=True),
                    after=(t_met[h], ev_fm[h // 2], dep, fr_qm), inc=True)
                ev_qm.append(P.op("scalar", lambda v, h=h, base=base, slot=slot: v.copy(
                    qm[base:base + 64, h // 2, :], psA[slot][base:base + 64, :]),
                    after=(tk,), inc=True))
            fr_psA[0] = ev_qm[-2]
            fr_psA[1] = ev_qm[-1]

            if pend is not None:
                pend()
                pend = None
            sq1 = P.op("scalar", lambda s_, cb=cb: s_.activation(
                oraw[:, :, :], pfm[:, :, cb:cb + CH], AF.Square),
                after=ev_fm + [fr_oraw], inc=True)
            nb_toks = []
            for blk in range(NCT):
                eslot = blk % 2
                psE = psE1 if eslot == 0 else psE2
                mmn = P.op("tensor", lambda t, blk=blk, psE=psE: t.matmul(
                    psE[0:2, :], eb[:, :], oraw[:, blk, :], start=True, stop=True),
                    after=(sq1, fr_psE1 if blk == 0 else None,
                           fr_psE2 if blk == 1 else None,
                           nb_toks[-2] if len(nb_toks) >= 2 else None), inc=True)
                if blk == 0:
                    nb_toks.append(P.op("vector", lambda v, psE=psE: v.tensor_copy(
                        pnacc[:, :], psE[0:2, :]), after=(mmn,), inc=True))
                else:
                    nb_toks.append(P.op("vector", lambda v, psE=psE: v.tensor_tensor(
                        pnacc[:, :], pnacc[:, :], psE[0:2, :], ALU.max),
                        after=(mmn,), inc=True))
            fr_psE1 = nb_toks[-2]
            fr_psE2 = nb_toks[-1]
            nb1 = P.op("vector", lambda v: v.tensor_reduce(
                tinyB[:, :], pnacc[:, :], AX.X, ALU.max), after=(nb_toks[-1],), inc=True)
            if c == 0:
                nb2 = P.op("vector", lambda v: v.tensor_copy(pn2run[:, :], tinyB[:, :]),
                           after=(nb1,), inc=True)
            else:
                nb2 = P.op("vector", lambda v: v.tensor_tensor(
                    pn2run[:, :], pn2run[:, :], tinyB[:, :], ALU.max),
                    after=(nb1,), inc=True)
            nb3 = P.op("gpsimd", lambda g: g.tensor_reduce(
                tinyC[:, :], pn2run[:, :], AX.C, ALU.max), after=(nb2,), inc=True)
            nb4 = P.op("scalar", lambda s_: s_.activation(
                tinyC[:, :], tinyC[:, :], AF.Ln), after=(nb3,), inc=True)
            nb5 = P.op("scalar", lambda s_: s_.activation(
                tinyC[:, :], tinyC[:, :], AF.Exp, bias=lnc[:, :]),
                after=(nb4, t_lnc), inc=True)
            nb6 = P.op("tensor", lambda t: t.matmul(
                psE2[:, 0:1], ones_f[0:1, :], tinyC[:, :], start=True, stop=True),
                after=(nb5, fr_psE2, t_of), inc=True)
            t_negB = P.op("vector", lambda v: v.tensor_scalar(
                negB[:, :], psE2[:, 0:1], -1.0, None, ALU.mult), after=(nb6,), inc=True)
            fr_psE2 = t_negB

            if DEBUG and s == 0 and c == 0:
                P.dma("sync", io["dbg_qm"], qm[:, :, :], after=ev_qm)
                P.dma("sync", io["dbg_negB"], negB[:, :], after=(t_negB,))
            psSS = [psS[0], psS[1], psA[0], psA[1]]
            fr_psSA = [fr_psS[0], fr_psS[1], ev_qm[-2], ev_qm[-1]]
            itc = 0
            ev_or = []
            dn_tok = []
            t_dd = []
            fr_den_gate = fr_den
            njt = 4 * c + 4
            for hp in range(H // 2):
                pair = (2 * hp, 2 * hp + 1)
                o_deps = {}
                for h in pair:
                    oslot = h % 2
                    if len(ev_or) >= 2:
                        o_deps[h] = (ev_or[-2 + oslot], dn_tok[-2 + oslot])
                    else:
                        o_deps[h] = fr_psO[oslot] or (None, None)
                avs = {pair[0]: None, pair[1]: None}
                pending = []

                def emit_av(args, o_deps=o_deps, avs=avs):
                    h, jt, off, ncols, pslot, tk_ex = args
                    oslot = h % 2
                    o_dep, o_dep2 = o_deps[h]
                    avs[h] = P.op("tensor", lambda t, h=h, jt=jt, off=off, ncols=ncols, pslot=pslot, oslot=oslot, njt=njt: t.matmul(
                        psO[oslot][:, off:off + ncols],
                        ptok[:, jt, h * 65:(h + 1) * 65],
                        probs[pslot][:, 0:ncols],
                        start=(jt == 0), stop=(jt == njt - 1)),
                        after=((tk_ex,) + ((ev_tok[-1], o_dep, o_dep2) if jt == 0 else ())),
                        inc=(jt == njt - 1))
                    fr_probs[pslot] = avs[h]

                for jt in range(njt):
                    off = max(0, jt * 128 - cb)
                    ncols = CH - off
                    exs = []
                    for h in pair:
                        base = (h % 2) * 64
                        sslot = itc % 4
                        itc += 1
                        tk_sc = P.op("tensor", lambda t, h=h, base=base, jt=jt, off=off, ncols=ncols, sslot=sslot: t.matmul(
                            psSS[sslot][:, off:off + ncols],
                            pfm[base:base + 64, h // 2, jt * 128:(jt + 1) * 128],
                            qm[base:base + 64, h // 2, off:off + ncols],
                            start=True, stop=True),
                            after=(fr_psSA[sslot], ev_qm[h]), inc=True)
                        if jt * 128 >= cb:
                            tk_sc = P.op("vector", lambda v, off=off, sslot=sslot: v.tensor_add(
                                psSS[sslot][:, off:off + 128], psSS[sslot][:, off:off + 128],
                                maskT[:, :]), after=(tk_sc, t_mk), inc=True)
                        exs.append((h, sslot, tk_sc))
                    for h, sslot, tk_sc in exs:
                        pslot = sslot
                        tk_ex = P.op("scalar", lambda s_, off=off, ncols=ncols, sslot=sslot, pslot=pslot: s_.activation(
                            probs[pslot][:, 0:ncols], psSS[sslot][:, off:off + ncols],
                            AF.Exp, bias=negB[:, :], scale=0.125),
                            after=(tk_sc, t_negB, fr_probs[pslot]), inc=True)
                        fr_psSA[sslot] = tk_ex
                        pending.append((h, jt, off, ncols, pslot, tk_ex))
                        if len(pending) > 2:
                            emit_av(pending.pop(0))
                while pending:
                    emit_av(pending.pop(0))
                for h in pair:
                    base = (h % 2) * 64
                    oslot = h % 2
                    half = h // 8
                    ev_or.append(P.op("vector", lambda v, h=h, base=base, oslot=oslot: v.tensor_copy(
                        oraw[base:base + 64, h // 2, :], psO[oslot][0:64, :]),
                        after=(avs[h], fr_oraw), inc=True))
                    dn_tok.append(P.op("scalar", lambda s_, h=h, oslot=oslot, half=half: s_.copy(
                        den[64:65][:, (h - half * 8) * CH:(h - half * 8 + 1) * CH],
                        psO[oslot][64:65, :]), after=(avs[h], fr_den_gate), inc=True))
                    if h == 7 or h == 15:
                        half_dma = P.dma("sync",
                                         dend[half * 8:(half + 1) * 8, :].unsqueeze(1),
                                         den[64:65][:, :].rearrange("one (hh t) -> one hh t", hh=8, t=CH),
                                         after=dn_tok[half * 8:(half + 1) * 8] + [fr_dend])
                        t_dd.append(half_dma)
                        fr_den_gate = half_dma
            fr_psO[0] = (ev_or[-2], dn_tok[-2])
            fr_psO[1] = (ev_or[-1], dn_tok[-1])
            fr_psA[0] = fr_psSA[2]
            fr_psA[1] = fr_psSA[3]
            fr_psS[0] = fr_psSA[0]
            fr_psS[1] = fr_psSA[1]

            t_rec = P.op("vector", lambda v: v.reciprocal(dend[:, :], dend[:, :]),
                         after=t_dd, inc=True)
            t_recb = P.op("vector", lambda v: v.tensor_copy(dendb[:, :], dend[:, :]),
                          after=(t_rec, fr_dendb), inc=True)
            if DEBUG and s == 0 and c == 0:
                P.dma("sync", io["dbg_dend"], dend[:, :], after=(t_recb,))
            fr_dend = t_recb

            def backB(s=s, c=c, buf=buf, cb=cb, t_xf=t_xf, t_recb=t_recb,
                      ev_or=ev_or, ev_fm=ev_fm, ev_tok=ev_tok, ev_qm=ev_qm):
                nonlocal fr_psS, fr_psE1, fr_psE2, fr_oraw, fr_xf, fr_xo, fr_den
                nonlocal fr_dendb, fr_rb, fr_r2b
                mul_toks = []
                res_toks = []
                scl_prev = [fr_psE1, fr_psE2]
                psE = [psE1, psE2]
                for ct in range(NCT):
                    eslot = ct % 2
                    t_sc = P.op("tensor", lambda t, ct=ct, eslot=eslot: t.matmul(
                        psE[eslot][:, :], sel[:, ct * 128:(ct + 1) * 128], dendb[:, :],
                        start=True, stop=True),
                        after=(t_recb, t_se, scl_prev[eslot],
                               mul_toks[-2] if len(mul_toks) >= 2 else None), inc=True)
                    scl_prev[eslot] = None
                    mul_toks.append(P.op("vector", lambda v, ct=ct, eslot=eslot: v.tensor_tensor(
                        oraw[:, ct, :], oraw[:, ct, :], psE[eslot][:, :], ALU.mult),
                        after=(t_sc, ev_or[-1]), inc=True))
                fr_dendb = t_sc
                fr_psE1 = mul_toks[-2]
                fr_psE2 = mul_toks[-1]

                if DEBUG and s == 0 and c == 0:
                    dbg_tok = P.dma("sync", io["dbg_oraw"], oraw[:, :, :], after=mul_toks)
                for mt in range(NCT):
                    slot = mt % 2
                    dep = res_toks[-2] if len(res_toks) >= 2 else fr_psS[slot]
                    for ci in range(NCT):
                        tk = P.op("tensor", lambda t, mt=mt, ci=ci, slot=slot: t.matmul(
                            psS[slot][:, :], wm[:, ci, mt * 128:(mt + 1) * 128],
                            oraw[:, ci, :], start=(ci == 0), stop=(ci == NCT - 1)),
                            after=(mul_toks[-1], t_wm, dep) if ci == 0 else (),
                            inc=(ci == NCT - 1))
                    res_toks.append(P.op("vector", lambda v, mt=mt, slot=slot, buf=buf: v.tensor_add(
                        xf[buf][:, mt, :], xf[buf][:, mt, :], psS[slot][:, :]),
                        after=(tk, t_xf), inc=True))
                if DEBUG and s == 0 and c == 0:
                    dbg_r_tok = P.dma("sync", io["dbg_r"], xf[buf][:, :, :], after=res_toks)
                    P.dma("sync", io["dbg_oraw2"], oraw[:, :, :], after=res_toks)
                    res_toks = res_toks + [dbg_r_tok]
                fr_oraw = res_toks[-1]
                fr_psS[0] = res_toks[-2]
                fr_psS[1] = res_toks[-1]

                st_toks = []
                sq_toks = []
                cast_toks = []
                for ct in range(NCT):
                    rs = ct % 2
                    t_rb = P.op("vector", lambda v, ct=ct, rs=rs, buf=buf: v.tensor_copy(
                        rb[rs][:, :], xf[buf][:, ct, :]),
                        after=(res_toks[ct], st_toks[-2] if len(st_toks) >= 2 else None,
                               fr_rb[rs]), inc=True)
                    t_r2 = P.op("scalar", lambda s_, ct=ct, rs=rs, buf=buf: s_.activation(
                        r2b[rs][:, :], xf[buf][:, ct, :], AF.Square),
                        after=(res_toks[ct], sq_toks[-2] if len(sq_toks) >= 2 else None,
                               fr_r2b[rs]), inc=True)
                    st_toks.append(P.op("tensor", lambda t, ct=ct, rs=rs: t.matmul(
                        psE1[0:1, :], ones_b[:, 0:1], rb[rs][:, :],
                        start=(ct == 0), stop=(ct == NCT - 1)),
                        after=(t_rb, t_ob) + ((fr_psE1,) if ct == 0 else ()), inc=True))
                    sq_toks.append(P.op("tensor", lambda t, ct=ct, rs=rs: t.matmul(
                        psE2[0:1, :], ones_b[:, 0:1], r2b[rs][:, :],
                        start=(ct == 0), stop=(ct == NCT - 1)),
                        after=(t_r2,) + ((fr_psE2,) if ct == 0 else ()), inc=True))
                    fr_rb[rs] = st_toks[-1]
                    fr_r2b[rs] = sq_toks[-1]
                e1 = P.op("vector", lambda v: v.tensor_copy(ssum[:, :], psE1[0:1, :]),
                          after=(st_toks[-1],), inc=True)
                e2 = P.op("vector", lambda v: v.tensor_copy(ssq[:, :], psE2[0:1, :]),
                          after=(sq_toks[-1],), inc=True)
                e3 = P.op("tensor", lambda t: t.matmul(
                    psE1[:, :], ones_f[0:1, :], ssum[:, :], start=True, stop=True),
                    after=(e1,), inc=True)
                e4 = P.op("tensor", lambda t: t.matmul(
                    psE2[:, :], ones_f[0:1, :], ssq[:, :], start=True, stop=True),
                    after=(e2,), inc=True)
                e5 = P.op("vector", lambda v: v.tensor_scalar(
                    mb, psE1[:, :], 1.0 / D, None, ALU.mult),
                    after=(e3, t_tp[-1], t_met[-1]), inc=True)
                fr_psE1 = e5
                e6 = P.op("scalar", lambda s_: s_.activation(m2, mb, AF.Square),
                          after=(e5,), inc=True)
                e7 = P.op("vector", lambda v: v.scalar_tensor_tensor(
                    vb, psE2[:, :], 1.0 / D, m2, ALU.mult, ALU.subtract),
                    after=(e4, e6, t_met[-1]), inc=True)
                fr_psE2 = e7
                e8 = P.op("scalar", lambda s_: s_.activation(vb, vb, AF.Ln,
                                                             bias=eps_t[:, :]),
                          after=(e7, t_eps0), inc=True)
                e9 = P.op("scalar", lambda s_: s_.activation(rstd, vb, AF.Exp,
                                                             scale=-0.5),
                          after=(e8,), inc=True)
                out_toks = []
                for ct in range(NCT):
                    u1 = P.op("vector", lambda v, ct=ct, buf=buf: v.tensor_sub(
                        xo[:, ct, :], xf[buf][:, ct, :], mb),
                        after=(e5, res_toks[ct], fr_xo), inc=True)
                    out_toks.append(P.op("vector", lambda v, ct=ct: v.tensor_mul(
                        xo[:, ct, :], xo[:, ct, :], rstd), after=(u1, e9), inc=True))
                fr_xf[buf] = out_toks[-1]
                t_out = P.dma("sync", io["xout"][s, :, :, cb:cb + CH].transpose([1, 0, 2]),
                              xo, after=out_toks)
                fr_xo = t_out
                fr_den = t_out
                if DEBUG and s == 0 and c == NCHUNK - 1:
                    P.dma("sync", io["dbg_pfm"], pfm[:, :, :], after=ev_fm)
                    P.dma("sync", io["dbg_ptok"], ptok[:, :, :], after=ev_tok)

            pend = backB
        pend()
        pend = None
    final_tok = ("d", "sync", P.dn["sync"])
    P.op("sync", lambda e: e.nop(), after=(final_tok,))
    return final_tok



def _host_consts():
    ident = np.eye(128, dtype=np.float32)
    ones_b = np.ones((128, 128), dtype=ml_dtypes.bfloat16)
    ones_f = np.ones((128, 128), dtype=np.float32)
    p = np.arange(128)
    maskT = np.where(p[:, None] <= p[None, :], 0.0, -1e9).astype(np.float32)
    k = np.arange(64)
    maskLK = np.where(k[None, :] <= k[:, None], 0.0, -1e9).astype(np.float32)
    sel = np.zeros((16, NCT * 128), dtype=ml_dtypes.bfloat16)
    for ct in range(NCT):
        for pp in range(128):
            sel[2 * ct + (pp >= 64), ct * 128 + pp] = 1.0
    eb = np.zeros((128, 2), dtype=ml_dtypes.bfloat16)
    eb[0:64, 0] = 1.0
    eb[64:128, 1] = 1.0
    return ident, ones_b, ones_f, maskT, maskLK, sel, eb


def _fm(x):
    B = x.shape[0]
    return np.ascontiguousarray(x.reshape(B, T, NCT, 128).transpose(0, 2, 3, 1))


def _fm_inv(x):
    B = x.shape[0]
    return np.ascontiguousarray(x.transpose(0, 3, 1, 2).reshape(B, T, D))


def _wT(w):
    return np.ascontiguousarray(w.T.reshape(w.shape[1] // 128, 128, w.shape[0]))


def build_l0a_program():
    nc = bass.Bass("TRN2", target_bir_lowering=False, debug=False)
    io = {}
    io["xin"] = nc.dram_tensor("xin", [NSEQ, NCT, 128, T], F32, kind="ExternalInput").ap()
    io["xout"] = nc.dram_tensor("xout", [NSEQ, NCT, 128, T], F32, kind="ExternalOutput").ap()
    io["wproj_t"] = nc.dram_tensor("wproj_t", [NCT, 128, D], F32, kind="ExternalInput").ap()
    io["wmix_t"] = nc.dram_tensor("wmix_t", [NCT, 128, D], F32, kind="ExternalInput").ap()
    io["pre_metric"] = nc.dram_tensor("pre_metric", [64, H, 64], F32, kind="ExternalInput").ap()
    io["ident_f32"] = nc.dram_tensor("ident_f32", [128, 128], F32, kind="ExternalInput").ap()
    io["ones_bf16"] = nc.dram_tensor("ones_bf16", [128, 128], BF16, kind="ExternalInput").ap()
    io["ones_f32"] = nc.dram_tensor("ones_f32", [128, 128], F32, kind="ExternalInput").ap()
    io["maskT_f32"] = nc.dram_tensor("maskT_f32", [128, 128], F32, kind="ExternalInput").ap()
    io["maskLK_f32"] = nc.dram_tensor("maskLK_f32", [64, 64], F32, kind="ExternalInput").ap()
    io["sel_bf16"] = nc.dram_tensor("sel_bf16", [16, NCT * 128], BF16, kind="ExternalInput").ap()
    io["eb_bf16"] = nc.dram_tensor("eb_bf16", [128, 2], BF16, kind="ExternalInput").ap()
    io["eb_f32"] = nc.dram_tensor("eb_f32", [128, 2], F32, kind="ExternalInput").ap()
    if DEBUG:
        io["dbg_pfm"] = nc.dram_tensor("dbg_pfm", [128, NCT, T], BF16, kind="ExternalOutput").ap()
        io["dbg_ptok"] = nc.dram_tensor("dbg_ptok", [128, NTILE, H * 65], BF16, kind="ExternalOutput").ap()
        io["dbg_qm"] = nc.dram_tensor("dbg_qm", [128, 8, CH], BF16, kind="ExternalOutput").ap()
        io["dbg_oraw"] = nc.dram_tensor("dbg_oraw", [128, NCT, CH], BF16, kind="ExternalOutput").ap()
        io["dbg_dend"] = nc.dram_tensor("dbg_dend", [16, CH], F32, kind="ExternalOutput").ap()
        io["dbg_negB"] = nc.dram_tensor("dbg_negB", [128, 1], F32, kind="ExternalOutput").ap()
        io["dbg_r"] = nc.dram_tensor("dbg_r", [128, NCT, CH], F32, kind="ExternalOutput").ap()
        io["dbg_oraw2"] = nc.dram_tensor("dbg_oraw2", [128, NCT, CH], BF16, kind="ExternalOutput").ap()

    stack = ExitStack()
    P = Prog(nc)
    P.make_sems(stack)
    build_l0a(nc, P, stack, io)
    block = stack.enter_context(nc.Block())
    P.finalize(block)
    stack.close()
    return nc


def l0a_in_maps(inputs, core):
    x = inputs["L0_attn_in_sequence_bwc"]
    ident, ones_b, ones_f, maskT, maskLK, sel, eb = _host_consts()
    m = {
        "xin": _fm(np.asarray(x[core * NSEQ:(core + 1) * NSEQ])),
        "wproj_t": _wT(np.asarray(inputs["L0_attn_W_proj"])),
        "wmix_t": _wT(np.asarray(inputs["L0_attn_W_mix"])),
        "pre_metric": np.ascontiguousarray(
            np.asarray(inputs["L0_attn_pre_metric"]).transpose(1, 0, 2)),
        "ident_f32": ident, "ones_bf16": ones_b, "ones_f32": ones_f,
        "maskT_f32": maskT, "maskLK_f32": maskLK, "sel_bf16": sel, "eb_bf16": eb, "eb_f32": eb.astype(np.float32),
    }
    return m


def build_fused_program():
    nc = bass.Bass("TRN2", target_bir_lowering=False, debug=False)
    AD = [NSEQ, NCT, 128, T]
    xin = nc.dram_tensor("xin", AD, F32, kind="ExternalInput").ap()
    x1 = nc.dram_tensor("x1_scratch", AD, F32, kind="Internal").ap()
    x2 = nc.dram_tensor("x2_scratch", AD, F32, kind="Internal").ap()
    x3 = nc.dram_tensor("x3_scratch", AD, F32, kind="Internal").ap()
    xout = nc.dram_tensor("xout", AD, F32, kind="ExternalOutput").ap()
    dt = lambda n, sh, d=F32: nc.dram_tensor(n, sh, d, kind="ExternalInput").ap()
    cm = {
        "ident_f32": dt("ident_f32", [128, 128]),
        "ones_bf16": dt("ones_bf16", [128, 128], BF16),
        "ones_f32": dt("ones_f32", [128, 128]),
        "eb_bf16": dt("eb_bf16", [128, 2], BF16),
    }
    io1 = dict(cm)
    io1.update(xin=xin, xout=x1,
               wproj_t=dt("wproj_t", [NCT, 128, D]),
               wmix_t=dt("wmix_t", [NCT, 128, D]),
               pre_metric=dt("pre_metric", [64, H, 64]),
               maskT_f32=dt("maskT_f32", [128, 128]),
               maskLK_f32=dt("maskLK_f32", [64, 64]),
               sel_bf16=dt("sel_bf16", [16, NCT * 128], BF16),
               eb_f32=dt("eb_f32", [128, 2]))
    io2 = dict(cm)
    io2.update(xin=x1, xout=x2,
               wfc_t=dt("wfc_t", [NCT, 128, 4096]),
               wproj2_t=dt("wproj2_t", [32, 128, 1024]),
               lnw=dt("lnw", [128, 32]), lnb=dt("lnb", [128, 32]))
    io3 = dict(cm)
    io3.update(xin=x2, xout=x3,
               conv3=dt("conv3", [NCT, 128, 3]),
               pe_t=dt("pe_t", [NCT, 128, D]),
               qkv_t=dt("qkv_t", [NCT, 128, 3072]),
               outw_t=dt("outw_t", [NCT, 128, D]),
               mask_strict=dt("mask_strict", [128, 128], BF16),
               mask_incl=dt("mask_incl", [128, 128], BF16),
               sel4=dt("sel4", [4, 2 * 128], BF16))
    io4 = dict(cm)
    io4.update(xin=x3, xout=xout,
               router_t=dt("router_t", [NCT, 128, 8]),
               ebias_b=dt("ebias_b", [128, 8]),
               w1=dt("w1", [8, NCT, 128, 1024]), w3=dt("w3", [8, NCT, 128, 1024]),
               w2=dt("w2", [8, NCT, 128, 1024]),
               sw1=dt("sw1", [NCT, 128, 1024]), sw3=dt("sw3", [NCT, 128, 1024]),
               sw2=dt("sw2", [NCT, 128, 1024]),
               e8sel=dt("e8sel", [8, 8 * 128], BF16))

    outer = ExitStack()
    P = Prog(nc)
    P.make_sems(outer)
    stages = ((build_l0a, io1), (build_l0f, io2), (build_l1a, io3), (build_l1f, io4))[:N_FUSE_STAGES]
    for build, io_ in stages:
        stage_stack = ExitStack()
        build(nc, P, stage_stack, io_)
        stage_stack.close()
        P.barrier()
    block = outer.enter_context(nc.Block())
    P.finalize(block)
    outer.close()
    return nc


def fused_in_maps(inputs, core):
    m = l0a_in_maps(inputs, core)
    m2 = l0f_in_maps(inputs, None); m2.pop("xin")
    m3 = l1a_in_maps(inputs, None); m3.pop("xin")
    m4 = l1f_in_maps(inputs, None); m4.pop("xin")
    for d in (m2, m3, m4):
        m.update(d)
    return m


FUSED = True
N_FUSE_STAGES = 4


def kernel(**inputs):
    cores = list(range(NCORES))
    if FUSED:
        nc = build_fused_program()
        res = run_bass_kernel_spmd(nc, [fused_in_maps(inputs, c) for c in cores],
                                   core_ids=cores)
        return np.concatenate([_fm_inv(r["xout"]) for r in res.results], axis=0)
    nc1 = build_l0a_program()
    res = run_bass_kernel_spmd(nc1, [l0a_in_maps(inputs, c) for c in cores], core_ids=cores)
    x = [r["xout"] for r in res.results]
    nc2 = build_l0f_program()
    res = run_bass_kernel_spmd(nc2, [l0f_in_maps(inputs, x[c]) for c in cores], core_ids=cores)
    x = [r["xout"] for r in res.results]
    nc3 = build_l1a_program()
    res = run_bass_kernel_spmd(nc3, [l1a_in_maps(inputs, x[c]) for c in cores], core_ids=cores)
    x = [r["xout"] for r in res.results]
    nc4 = build_l1f_program()
    res = run_bass_kernel_spmd(nc4, [l1f_in_maps(inputs, x[c]) for c in cores], core_ids=cores)
    return np.concatenate([_fm_inv(r["xout"]) for r in res.results], axis=0)



def build_l0f(nc, P, stack, io):
    E = stack.enter_context
    sb = lambda name, shape, dt=F32: E(nc.sbuf_tensor(name, shape, dt))
    NH = 32

    wq1 = sb("l0f_wq1", [128, NCT, 4096], BF16)
    wq2 = sb("l0f_wq2", [128, NH, 1024], BF16)
    hb = sb("l0f_hb", [128, NH, CH], BF16)
    xb_t = sb("l0f_xb0", [128, NCT, CH], BF16)
    xf_t = sb("l0f_xf", [128, NCT, CH], F32)
    xf = xf_t[:, :, :]
    ones_b = sb("l0f_ones_b", [128, 128], BF16)
    ones_f = sb("l0f_ones_f", [128, 128], F32)
    eps_t = sb("l0f_eps", [128, 1], F32)
    lnw = sb("l0f_lnw", [128, NH], F32)
    lnb = sb("l0f_lnb", [128, NH], F32)
    absacc = sb("l0f_absacc", [128, 2], F32)
    thr = sb("l0f_thr", [128, 2], F32)
    red_t = sb("l0f_red", [128, 1], F32)
    tiny = sb("l0f_tiny", [1, 1], F32)
    u1_t = sb("l0f_u1", [128, 2 * CH], F32)
    u2_t = sb("l0f_u2", [128, 2 * CH], F32)
    mb, m2 = u1_t[:, 0:CH], u1_t[:, CH:2 * CH]
    vb, rstd = u2_t[:, 0:CH], u2_t[:, CH:2 * CH]
    ssum = sb("l0f_ssum", [1, CH], F32)
    ssq = sb("l0f_ssq", [1, CH], F32)
    rb = [sb(f"l0f_rb{i}", [128, CH], BF16) for i in range(2)]
    r2b = [sb(f"l0f_r2b{i}", [128, CH], BF16) for i in range(2)]
    tmp1 = sb("l0f_tmp1", [128, CH], F32)
    tmp2 = sb("l0f_tmp2", [128, CH], F32)

    wld = [hb[:, :, :].bitcast(F32).rearrange("p a b -> p (a b)")[:, i * 4096:(i + 1) * 4096]
           for i in range(2)]
    abs_t = xf_t[:, :, :].rearrange("p a b -> p (a b)")
    sgn_t = xb_t[:, :, :].rearrange("p a b -> p (a b)")

    psA = [E(nc.psum_tensor(f"l0f_psA{i}", [128, CH], F32)) for i in range(2)]
    psB = [E(nc.psum_tensor(f"l0f_psB{i}", [128, CH], F32)) for i in range(4)]
    psE1 = E(nc.psum_tensor("l0f_psE1", [128, CH], F32))
    psE2 = E(nc.psum_tensor("l0f_psE2", [128, CH], F32))

    t_ob = P.dma("sync", ones_b[:, :], io["ones_bf16"])
    t_of = P.dma("sync", ones_f[:, :], io["ones_f32"])
    t_lw = P.dma("sync", lnw[:, :], io["lnw"])
    t_lb = P.dma("sync", lnb[:, :], io["lnb"])
    t_eps0 = P.op("vector", lambda v: v.memset(eps_t[:, :], 1e-5), inc=True)
    t_aa = P.op("vector", lambda v: v.memset(absacc[:, :], 0.0), inc=True)

    tile_srcs = []
    for ci in range(NCT):
        tile_srcs.append((0, io["wfc_t"][ci], wq1[:, ci, :], 4096))
    for g in range(8):
        src = io["wproj2_t"][g * 4:(g + 1) * 4].transpose([1, 0, 2])
        dst = wq2[:, g * 4:(g + 1) * 4, :].rearrange("p a b -> p (a b)")
        tile_srcs.append((1, src, dst, 4096))

    acc_tok = [t_aa, t_aa]
    ld_prev = [None, None]
    for i, (mi, src, dst, ncols) in enumerate(tile_srcs):
        slot = i % 2
        t_ld = P.dma("sync", wld[slot][:, 0:ncols], src, after=(ld_prev[slot],))
        t_red = P.op("vector", lambda v, slot=slot, ncols=ncols: v.tensor_reduce(
            red_t[:, :], wld[slot][:, 0:ncols].rearrange("p (a b) -> p a b", b=512),
            AX.XY, ALU.add, apply_absolute_value=True), after=(t_ld, acc_tok[0]), inc=True)
        acc_tok[0] = P.op("vector", lambda v, mi=mi: v.tensor_add(
            absacc[:, mi:mi + 1], absacc[:, mi:mi + 1], red_t[:, :]),
            after=(t_red,), inc=True)
        ld_prev[slot] = t_red
    thr_toks = []
    for mi in range(2):
        m1 = P.op("tensor", lambda t, mi=mi: t.matmul(
            psE1[0:1, 0:1], ones_f[:, 0:1], absacc[:, mi:mi + 1], start=True, stop=True),
            after=(acc_tok[0], t_of, thr_toks[-1] if thr_toks else None), inc=True)
        m2_ = P.op("vector", lambda v, mi=mi: v.tensor_scalar(
            tiny[:, :], psE1[0:1, 0:1], 0.5 / (4096 * 1024), 0.5e-6, ALU.mult, ALU.add),
            after=(m1,), inc=True)
        m3 = P.op("tensor", lambda t, mi=mi: t.matmul(
            psE2[:, 0:1], ones_f[0:1, :], tiny[:, :], start=True, stop=True),
            after=(m2_,), inc=True)
        thr_toks.append(P.op("vector", lambda v, mi=mi: v.tensor_copy(
            thr[:, mi:mi + 1], psE2[:, 0:1]), after=(m3,), inc=True))

    qz_prev = [None, None]
    last_q = None
    for i, (mi, src, dst, ncols) in enumerate(tile_srcs):
        slot = i % 2
        t_ld = P.dma("sync", wld[slot][:, 0:ncols], src, after=(qz_prev[slot], thr_toks[-1]))
        t_ab = P.op("scalar", lambda s_, slot=slot, ncols=ncols: s_.activation(
            abs_t[:, 0:ncols], wld[slot][:, 0:ncols], AF.Abs), after=(t_ld, last_q), inc=True)
        t_sg = P.op("scalar", lambda s_, slot=slot, ncols=ncols: s_.sign(
            sgn_t[:, 0:ncols], wld[slot][:, 0:ncols]), after=(t_ab,), inc=True)
        last_q = P.op("vector", lambda v, mi=mi, dst=dst, ncols=ncols: v.scalar_tensor_tensor(
            dst, abs_t[:, 0:ncols], thr[:, mi:mi + 1], sgn_t[:, 0:ncols],
            ALU.is_ge, ALU.mult), after=(t_sg,), inc=True)
        qz_prev[slot] = t_sg
    t_wq = last_q
    if DEBUG:
        P.dma("sync", io["dbg_wq1"], wq1[:, :, :], after=(t_wq,))
        P.dma("sync", io["dbg_wq2"], wq2[:, :, :], after=(t_wq,))
        P.dma("sync", io["dbg_thr"], thr[:, :], after=(t_wq,))

    fr_xb0 = None
    fr_xf = None
    fr_psA = [None, None]
    fr_psB = [None, None, None, None]
    fr_psH = [None, None, None, None]
    fr_psE1 = thr_toks[-1]
    fr_psE2 = thr_toks[-1]
    fr_rb = [None, None]
    fr_r2b = [None, None]
    fr_hb = t_wq
    fr_tmp = None
    for s in range(NSEQ):
        for c in range(NCHUNK):
            buf = c % 2
            cb = c * CH
            t_xb = P.dma("gpsimd", xb_t[:, :, :],
                         io["xin"][s, :, :, cb:cb + CH].transpose([1, 0, 2]),
                         after=(fr_xb0, t_wq if s == 0 and c == 0 else None))
            t_xf = P.dma("sync", xf_t[:, :, :],
                         io["xin"][s, :, :, cb:cb + CH].transpose([1, 0, 2]),
                         after=(fr_xf, t_wq if (s == 0 and c == 0) else None))

            gel_toks = []
            st_toks = []
            sq_toks = []
            psH = [psA[0], psA[1], psB[0], psB[1]]
            for ht in range(NH):
                slot = ht % 4
                dep = gel_toks[-4] if len(gel_toks) >= 4 else fr_psH[slot]
                for ci in range(NCT):
                    tk = P.op("tensor", lambda t, ht=ht, ci=ci, slot=slot: t.matmul(
                        psH[slot][:, :], wq1[:, ci, ht * 128:(ht + 1) * 128],
                        xb_t[:, ci, :], start=(ci == 0), stop=(ci == NCT - 1)),
                        after=(t_xb, dep) if ci == 0 else (), inc=(ci == NCT - 1))
                g1 = P.op("scalar", lambda s_, ht=ht, slot=slot: s_.activation(
                    hb[:, ht, :], psH[slot][:, :], AF.Gelu),
                    after=(tk, fr_hb), inc=True)
                gel_toks.append(g1)
                rs = ht % 2
                t_r2 = P.op("scalar", lambda s_, ht=ht, rs=rs: s_.activation(
                    r2b[rs][:, :], hb[:, ht, :], AF.Square),
                    after=(g1, fr_r2b[rs]), inc=True)
                st_toks.append(P.op("tensor", lambda t, ht=ht: t.matmul(
                    psE1[0:1, :], ones_b[:, 0:1], hb[:, ht, :],
                    start=(ht == 0), stop=(ht == NH - 1)),
                    after=(g1, t_ob) + ((fr_psE1,) if ht == 0 else ()), inc=True))
                sq_toks.append(P.op("tensor", lambda t, ht=ht, rs=rs: t.matmul(
                    psE2[0:1, :], ones_b[:, 0:1], r2b[rs][:, :],
                    start=(ht == 0), stop=(ht == NH - 1)),
                    after=(t_r2,) + ((fr_psE2,) if ht == 0 else ()), inc=True))
                fr_r2b[rs] = sq_toks[-1]
            fr_xb0 = tk

            if DEBUG and s == 0 and c == 0:
                P.dma("sync", io["dbg_hb"], hb[:, :, :], after=gel_toks)
            e1 = P.op("vector", lambda v: v.tensor_copy(ssum[:, :], psE1[0:1, :]),
                      after=(st_toks[-1],), inc=True)
            e2 = P.op("vector", lambda v: v.tensor_copy(ssq[:, :], psE2[0:1, :]),
                      after=(sq_toks[-1],), inc=True)
            e3 = P.op("tensor", lambda t: t.matmul(
                psE1[:, :], ones_f[0:1, :], ssum[:, :], start=True, stop=True),
                after=(e1,), inc=True)
            e4 = P.op("tensor", lambda t: t.matmul(
                psE2[:, :], ones_f[0:1, :], ssq[:, :], start=True, stop=True),
                after=(e2,), inc=True)
            e5 = P.op("vector", lambda v: v.tensor_scalar(
                mb, psE1[:, :], 1.0 / 4096, None, ALU.mult), after=(e3,), inc=True)
            fr_psE1 = e5
            e6 = P.op("scalar", lambda s_: s_.activation(m2, mb, AF.Square),
                      after=(e5,), inc=True)
            e7 = P.op("vector", lambda v: v.scalar_tensor_tensor(
                vb, psE2[:, :], 1.0 / 4096, m2, ALU.mult, ALU.subtract),
                after=(e4, e6), inc=True)
            fr_psE2 = e7
            e8 = P.op("scalar", lambda s_: s_.activation(vb, vb, AF.Ln, bias=eps_t[:, :]),
                      after=(e7, t_eps0), inc=True)
            e9 = P.op("scalar", lambda s_: s_.activation(rstd, vb, AF.Exp, scale=-0.5),
                      after=(e8,), inc=True)

            nrm_toks = []
            res_toks = []
            sweep_mm_last = [None, None, None, None]
            for half in range(2):
                for ht in range(NH):
                    if half == 0:
                        ts_ = ht % 2
                        tmp = (tmp1 if ts_ == 0 else tmp2)[:, :]
                        n1 = P.op("vector", lambda v, ht=ht, tmp=tmp: v.tensor_sub(
                            tmp, hb[:, ht, :], mb),
                            after=(e5, gel_toks[ht], sq_toks[ht],
                                   nrm_toks[-2] if len(nrm_toks) >= 2 else fr_tmp), inc=True)
                        n2 = P.op("vector", lambda v, tmp=tmp: v.tensor_mul(tmp, tmp, rstd),
                                  after=(n1, e9), inc=True)
                        nrm_toks.append(P.op("vector", lambda v, ht=ht, tmp=tmp: v.tensor_scalar(
                            hb[:, ht, :], tmp, lnw[:, ht:ht + 1], lnb[:, ht:ht + 1],
                            ALU.mult, ALU.add), after=(n2, t_lw, t_lb), inc=True))
                    for dti in range(4):
                        dt = half * 4 + dti
                        tk = P.op("tensor", lambda t, dt=dt, ht=ht, dti=dti: t.matmul(
                            psB[dti][:, :], wq2[:, ht, dt * 128:(dt + 1) * 128],
                            hb[:, ht, :], start=(ht == 0), stop=(ht == NH - 1)),
                            after=(nrm_toks[ht],
                                   res_toks[dti] if half == 1 else
                                   (gel_toks[-2 + dti] if dti < 2 else fr_psB[dti]))
                                  if ht == 0 else
                                  ((nrm_toks[ht],) if half == 0 else ()),
                            inc=(ht == NH - 1))
                        sweep_mm_last[dti] = tk
                for dti in range(4):
                    dt = half * 4 + dti
                    tok_r = P.op("vector", lambda v, dt=dt, dti=dti: v.tensor_add(
                        xf[:, dt, :], xf[:, dt, :], psB[dti][:, :]),
                        after=(sweep_mm_last[dti], t_xf), inc=True)
                    if half == 0:
                        res_toks.append(tok_r)
                    else:
                        res_toks.append(tok_r)
            fr_tmp = nrm_toks[-1]
            for dti in range(4):
                fr_psB[dti] = res_toks[4 + dti]
            fr_psH[0] = gel_toks[-4]
            fr_psH[1] = gel_toks[-3]
            fr_psH[2] = res_toks[4]
            fr_psH[3] = res_toks[5]
            fr_hb = tk

            st2 = []
            sq2 = []
            for ct in range(NCT):
                rs = ct % 2
                t_rb = P.op("vector", lambda v, ct=ct, rs=rs: v.tensor_copy(
                    rb[rs][:, :], xf[:, ct, :]),
                    after=(res_toks[ct], st2[-2] if len(st2) >= 2 else None,
                           fr_rb[rs]), inc=True)
                t_r2 = P.op("scalar", lambda s_, ct=ct, rs=rs: s_.activation(
                    r2b[rs][:, :], xf[:, ct, :], AF.Square),
                    after=(res_toks[ct], sq2[-2] if len(sq2) >= 2 else None,
                           fr_r2b[rs]), inc=True)
                st2.append(P.op("tensor", lambda t, ct=ct, rs=rs: t.matmul(
                    psE1[0:1, :], ones_b[:, 0:1], rb[rs][:, :],
                    start=(ct == 0), stop=(ct == NCT - 1)),
                    after=(t_rb,) + ((fr_psE1,) if ct == 0 else ()), inc=True))
                sq2.append(P.op("tensor", lambda t, ct=ct, rs=rs: t.matmul(
                    psE2[0:1, :], ones_b[:, 0:1], r2b[rs][:, :],
                    start=(ct == 0), stop=(ct == NCT - 1)),
                    after=(t_r2,) + ((fr_psE2,) if ct == 0 else ()), inc=True))
                fr_rb[rs] = st2[-1]
                fr_r2b[rs] = sq2[-1]
            f1 = P.op("vector", lambda v: v.tensor_copy(ssum[:, :], psE1[0:1, :]),
                      after=(st2[-1],), inc=True)
            f2 = P.op("vector", lambda v: v.tensor_copy(ssq[:, :], psE2[0:1, :]),
                      after=(sq2[-1],), inc=True)
            f3 = P.op("tensor", lambda t: t.matmul(
                psE1[:, :], ones_f[0:1, :], ssum[:, :], start=True, stop=True),
                after=(f1,), inc=True)
            f4 = P.op("tensor", lambda t: t.matmul(
                psE2[:, :], ones_f[0:1, :], ssq[:, :], start=True, stop=True),
                after=(f2,), inc=True)
            f5 = P.op("vector", lambda v: v.tensor_scalar(
                mb, psE1[:, :], 1.0 / D, None, ALU.mult), after=(f3,), inc=True)
            fr_psE1 = f5
            f6 = P.op("scalar", lambda s_: s_.activation(m2, mb, AF.Square),
                      after=(f5,), inc=True)
            f7 = P.op("vector", lambda v: v.scalar_tensor_tensor(
                vb, psE2[:, :], 1.0 / D, m2, ALU.mult, ALU.subtract),
                after=(f4, f6), inc=True)
            fr_psE2 = f7
            f8 = P.op("scalar", lambda s_: s_.activation(vb, vb, AF.Ln, bias=eps_t[:, :]),
                      after=(f7,), inc=True)
            f9 = P.op("scalar", lambda s_: s_.activation(rstd, vb, AF.Exp, scale=-0.5),
                      after=(f8,), inc=True)
            out_toks = []
            for ct in range(NCT):
                u1 = P.op("vector", lambda v, ct=ct: v.tensor_sub(
                    xf[:, ct, :], xf[:, ct, :], mb), after=(f5, res_toks[ct]), inc=True)
                out_toks.append(P.op("vector", lambda v, ct=ct: v.tensor_mul(
                    xf[:, ct, :], xf[:, ct, :], rstd), after=(u1, f9), inc=True))
            t_out = P.dma("sync", io["xout"][s, :, :, cb:cb + CH].transpose([1, 0, 2]),
                          xf, after=out_toks)
            fr_xf = t_out
    final_tok = ("d", "sync", P.dn["sync"])
    P.op("sync", lambda e: e.nop(), after=(final_tok,))
    return final_tok


def build_l0f_program():
    nc = bass.Bass("TRN2", target_bir_lowering=False, debug=False)
    io = {}
    io["xin"] = nc.dram_tensor("xin", [NSEQ, NCT, 128, T], F32, kind="ExternalInput").ap()
    io["xout"] = nc.dram_tensor("xout", [NSEQ, NCT, 128, T], F32, kind="ExternalOutput").ap()
    io["wfc_t"] = nc.dram_tensor("wfc_t", [NCT, 128, 4096], F32, kind="ExternalInput").ap()
    io["wproj2_t"] = nc.dram_tensor("wproj2_t", [32, 128, 1024], F32, kind="ExternalInput").ap()
    io["lnw"] = nc.dram_tensor("lnw", [128, 32], F32, kind="ExternalInput").ap()
    io["lnb"] = nc.dram_tensor("lnb", [128, 32], F32, kind="ExternalInput").ap()
    io["ones_bf16"] = nc.dram_tensor("ones_bf16", [128, 128], BF16, kind="ExternalInput").ap()
    io["ones_f32"] = nc.dram_tensor("ones_f32", [128, 128], F32, kind="ExternalInput").ap()
    if DEBUG:
        io["dbg_wq1"] = nc.dram_tensor("dbg_wq1", [128, NCT, 4096], BF16, kind="ExternalOutput").ap()
        io["dbg_wq2"] = nc.dram_tensor("dbg_wq2", [128, 32, 1024], BF16, kind="ExternalOutput").ap()
        io["dbg_thr"] = nc.dram_tensor("dbg_thr", [128, 2], F32, kind="ExternalOutput").ap()
        io["dbg_hb"] = nc.dram_tensor("dbg_hb", [128, 32, CH], BF16, kind="ExternalOutput").ap()
        io["dbg_hn"] = nc.dram_tensor("dbg_hn", [128, 32, CH], BF16, kind="ExternalOutput").ap()
    stack = ExitStack()
    P = Prog(nc)
    P.make_sems(stack)
    build_l0f(nc, P, stack, io)
    block = stack.enter_context(nc.Block())
    P.finalize(block)
    stack.close()
    return nc


def l0f_in_maps(inputs, x1_core):
    ident, ones_b, ones_f, maskT, maskLK, sel, eb = _host_consts()
    wfc = np.asarray(inputs["L0_ffn_W_fc"])
    wpr = np.asarray(inputs["L0_ffn_W_proj"])
    return {
        "xin": x1_core,
        "wfc_t": np.ascontiguousarray(wfc.T.reshape(NCT, 128, 4096)),
        "wproj2_t": np.ascontiguousarray(wpr.T.reshape(32, 128, 1024)),
        "lnw": np.ascontiguousarray(np.asarray(inputs["L0_ffn_ln_w"]).reshape(32, 128).T),
        "lnb": np.ascontiguousarray(np.asarray(inputs["L0_ffn_ln_b"]).reshape(32, 128).T),
        "ones_bf16": ones_b, "ones_f32": ones_f,
    }



def build_l1a(nc, P, stack, io):
    E = stack.enter_context
    sb = lambda name, shape, dt=F32: E(nc.sbuf_tensor(name, shape, dt))

    qkvT = sb("l1a_qkvT", [128, NCT, 3072], BF16)
    peT = sb("l1a_peT", [128, NCT, D], BF16)
    cw = sb("l1a_cw", [128, NCT, 3], F32)
    ident = sb("l1a_ident", [128, 128], F32)
    ones_b = sb("l1a_ones_b", [128, 128], BF16)
    ones_f = sb("l1a_ones_f", [128, 128], F32)
    m_strict = sb("l1a_mstrict", [128, 128], BF16)
    m_incl = sb("l1a_mincl", [128, 128], BF16)
    sel4 = sb("l1a_sel4", [4, 2 * 128], BF16)
    eb = sb("l1a_eb", [128, 2], BF16)
    eps_t = sb("l1a_eps", [128, 1], F32)
    ln8_t = sb("l1a_ln8", [128, 1], F32)
    xin_t = sb("l1a_xin", [128, NCT, CH + 2], F32)
    xcb = sb("l1a_xcb", [128, NCT, CH], BF16)
    xs = sb("l1a_xs", [128, NCT, CH], BF16)
    outwT = [xcb[:, :, :].rearrange("p a b -> p (a b)").rearrange("p (a b) -> p a b", a=4, b=1024),
             xs[:, :, :].rearrange("p a b -> p (a b)").rearrange("p (a b) -> p a b", a=4, b=1024)]
    qf = sb("l1a_qf", [128, NCT, CH], BF16)
    kf = sb("l1a_kf", [128, NCT, CH], BF16)
    qsh = sb("l1a_qsh", [128, NCT, CH], BF16)
    ksh = sb("l1a_ksh", [128, NCT, CH], BF16)
    scr8 = sb("l1a_scr8", [128, NCT, CH], BF16)
    qv = scr8[:, :, :].rearrange("p a b -> p (a b)")[:, 0:2048].rearrange(
        "p (a b) -> p a b", a=16, b=128)
    kv = scr8[:, :, :].rearrange("p a b -> p (a b)")[:, 2048:4096].rearrange(
        "p (a b) -> p a b", a=16, b=128)
    vtok = sb("l1a_vtok", [128, 4, H * 65], BF16)
    o_buf = sb("l1a_obuf", [128, NCT, T], BF16)
    probs = [sb(f"l1a_pr{i}", [128, CH], BF16) for i in range(4)]
    u_t = sb("l1a_u", [128, 4 * CH], F32)
    mb, m2 = u_t[:, 0:CH], u_t[:, CH:2 * CH]
    vb, rstd = u_t[:, 2 * CH:3 * CH], u_t[:, 3 * CH:4 * CH]
    den = u_t[:, :]
    pnacc = u_t[0:2, 0:CH]
    dend = u_t[0:4, :]
    dendb = sb("l1a_dendb", [4, 4 * CH], BF16)
    ssum = dend[0:1, 0:CH]
    ssq = dend[0:1, CH:2 * CH]
    rb = sb("l1a_rb", [128, CH], BF16)
    r2b = sb("l1a_r2b", [128, CH], BF16)
    tmp0 = sb("l1a_tmp0", [128, CH], F32)
    tinyB = sb("l1a_tinyB", [2, 1], F32)
    tinyC = sb("l1a_tinyC", [1, 1], F32)
    tinyC2 = sb("l1a_tinyC2", [1, 1], F32)
    negB = sb("l1a_negB", [128, 1], F32)

    psA = [E(nc.psum_tensor(f"l1a_psA{i}", [128, CH], F32)) for i in range(2)]
    psS = [E(nc.psum_tensor(f"l1a_psS{i}", [128, CH], F32)) for i in range(2)]
    psO = [E(nc.psum_tensor(f"l1a_psO{i}", [65, CH], F32)) for i in range(2)]
    psE1 = E(nc.psum_tensor("l1a_psE1", [128, CH], F32))
    psE2 = E(nc.psum_tensor("l1a_psE2", [128, CH], F32))

    t_id = P.dma("sync", ident[:, :], io["ident_f32"])
    t_ob = P.dma("sync", ones_b[:, :], io["ones_bf16"])
    t_of = P.dma("sync", ones_f[:, :], io["ones_f32"])
    t_ms = P.dma("sync", m_strict[:, :], io["mask_strict"])
    t_mi = P.dma("sync", m_incl[:, :], io["mask_incl"])
    t_s4 = P.dma("sync", sel4[:, :], io["sel4"])
    t_eb = P.dma("sync", eb[:, :], io["eb_bf16"])
    t_cw = P.dma("sync", cw[:, :, :], io["conv3"].transpose([1, 0, 2]))
    t_qk = P.dma("gpsimd", qkvT[:, :, :], io["qkv_t"].transpose([1, 0, 2]))
    t_pe = P.dma("gpsimd", peT[:, :, :], io["pe_t"].transpose([1, 0, 2]))
    t_eps0 = P.op("vector", lambda v: v.memset(eps_t[:, :], 1e-5), inc=True)
    t_ln8 = P.op("vector", lambda v: v.memset(ln8_t[:, :], float(np.log(8.0))), inc=True)
    vones_ap = vtok[:, :, :].rearrange("p j (h c) -> p j h c", h=H, c=65)[:, :, :, 64:65]
    t_vo = P.op("vector", lambda v: v.memset(vones_ap, 1.0), inc=True)

    fr_xin = None
    fr_psA = [None, None]
    fr_psS = [None, None]
    fr_psO = [None, None]
    fr_psE1 = None
    fr_psE2 = None
    fr_probs = [None, None, None, None]
    fr_den = None
    fr_dend = None
    fr_dendb = None
    fr_obuf = None
    fr_xcb = None
    fr_xs = None
    fr_qf = None
    fr_scr8 = None
    fr_vtok = None
    fr_outw = None
    fr_rb = None
    fr_r2b = None
    fr_u_ph2 = None
    for s in range(NSEQ):
        den_dmas = []
        scale_toks_all = []
        for c in range(NCHUNK):
            cb = c * CH
            lo = max(0, cb - 1)
            hi = min(T, cb + CH + 1)
            dlo = 1 - (cb - lo)
            t_xi = P.dma("sync", xin_t[:, :, dlo:dlo + (hi - lo)],
                         io["xin"][s, :, :, lo:hi].transpose([1, 0, 2]),
                         after=(fr_xin,))
            pads = []
            if c == 0:
                pads.append(P.op("vector", lambda v: v.memset(xin_t[:, :, 0:1], 0.0),
                                 after=(fr_xin,), inc=True))
            if c == NCHUNK - 1:
                pads.append(P.op("vector", lambda v: v.memset(
                    xin_t[:, :, CH + 1:CH + 2], 0.0), after=(fr_xin,), inc=True))

            cv_toks = []
            for ct in range(NCT):
                c1 = P.op("vector", lambda v, ct=ct: v.tensor_scalar(
                    tmp0[:, :], xin_t[:, ct, 0:CH], cw[:, ct, 0:1], None, ALU.mult),
                    after=(t_xi, t_cw, fr_outw,
                           cv_toks[-1] if cv_toks else fr_xcb) + tuple(pads),
                    inc=True)
                c2 = P.op("vector", lambda v, ct=ct: v.scalar_tensor_tensor(
                    tmp0[:, :], xin_t[:, ct, 1:CH + 1], cw[:, ct, 1:2], tmp0[:, :],
                    ALU.mult, ALU.add), after=(c1,), inc=True)
                cv_toks.append(P.op("vector", lambda v, ct=ct: v.scalar_tensor_tensor(
                    xcb[:, ct, :], xin_t[:, ct, 2:CH + 2], cw[:, ct, 2:3], tmp0[:, :],
                    ALU.mult, ALU.add), after=(c2,), inc=True))

            xs_toks = []
            for mt in range(NCT):
                slot = mt % 2
                dep = xs_toks[-2] if len(xs_toks) >= 2 else fr_psA[slot]
                for ci in range(NCT):
                    tk = P.op("tensor", lambda t, mt=mt, ci=ci, slot=slot: t.matmul(
                        psA[slot][:, :], peT[:, ci, mt * 128:(mt + 1) * 128],
                        xcb[:, ci, :], start=(ci == 0), stop=(ci == NCT - 1)),
                        after=(cv_toks[-1], t_pe, dep, fr_xs) if ci == 0 else (),
                        inc=(ci == NCT - 1))
                xs_toks.append(P.op("vector", lambda v, mt=mt, slot=slot: v.tensor_add(
                    xs[:, mt, :], xcb[:, mt, :], psA[slot][:, :]),
                    after=(tk,), inc=True))

            ev_q = []
            ev_k = []
            for mt in range(NCT):
                slot = mt % 2
                dep = ev_q[-2] if len(ev_q) >= 2 else xs_toks[-2 + slot]
                for ci in range(NCT):
                    tk = P.op("tensor", lambda t, mt=mt, ci=ci, slot=slot: t.matmul(
                        psA[slot][:, :], qkvT[:, ci, mt * 128:(mt + 1) * 128],
                        xs[:, ci, :], start=(ci == 0), stop=(ci == NCT - 1)),
                        after=(xs_toks[-1], t_qk, dep, fr_qf) if ci == 0 else (),
                        inc=(ci == NCT - 1))
                ev_q.append(P.op("scalar", lambda v, mt=mt, slot=slot: v.copy(
                    qf[:, mt, :], psA[slot][:, :]), after=(tk,), inc=True))
            for mt in range(NCT):
                slot = mt % 2
                dep = ev_k[-2] if len(ev_k) >= 2 else ev_q[-2 + slot]
                for ci in range(NCT):
                    tk = P.op("tensor", lambda t, mt=mt, ci=ci, slot=slot: t.matmul(
                        psA[slot][:, :], qkvT[:, ci, 1024 + mt * 128:1024 + (mt + 1) * 128],
                        xs[:, ci, :], start=(ci == 0), stop=(ci == NCT - 1)),
                        after=(dep,) if ci == 0 else (), inc=(ci == NCT - 1))
                ev_k.append(P.op("scalar", lambda v, mt=mt, slot=slot: v.copy(
                    kf[:, mt, :], psA[slot][:, :]), after=(tk,), inc=True))
            ev_v = []
            for s4 in range(4):
                for nch in range(2):
                    i = s4 * 2 + nch
                    slot = i % 2
                    dep = ev_v[-2] if len(ev_v) >= 2 else ev_k[-2 + slot]
                    for ci in range(NCT):
                        tk = P.op("tensor", lambda t, s4=s4, nch=nch, ci=ci, slot=slot: t.matmul(
                            psA[slot][:, :], xs[:, ci, s4 * 128:(s4 + 1) * 128],
                            qkvT[:, ci, 2048 + nch * 512:2048 + (nch + 1) * 512],
                            start=(ci == 0), stop=(ci == NCT - 1)),
                            after=(dep, fr_vtok if i == 0 else None) if ci == 0 else (),
                            inc=(ci == NCT - 1))
                    dst = vtok[:, s4, :].rearrange("p (h c2) -> p h c2", h=H, c2=65)[
                        :, nch * 8:(nch + 1) * 8, 0:64]
                    srcp = psA[slot][:, :].rearrange("p (h k) -> p h k", h=8, k=64)
                    ev_v.append(P.op("scalar", lambda v, dst=dst, srcp=srcp: v.copy(dst, srcp),
                                     after=(tk, t_vo), inc=True))
            fr_xs = tk
            fr_psA[0] = ev_v[-2]
            fr_psA[1] = ev_v[-1]

            t_qs0 = P.dma("sync", qsh[0:64, :, :], qf[64:128, :, :], after=ev_q + [fr_scr8])
            t_qs1 = P.dma("sync", qsh[64:128, :, :], qf[0:64, :, :], after=ev_q)
            t_ks0 = P.dma("sync", ksh[0:64, :, :], kf[64:128, :, :], after=ev_k)
            t_ks = P.dma("sync", ksh[64:128, :, :], kf[0:64, :, :], after=ev_k)
            t_qs = t_qs1

            sq1 = P.op("scalar", lambda s_: s_.activation(
                scr8[:, :, :], qf[:, :, :], AF.Square), after=ev_q + [fr_scr8], inc=True)
            nb1 = []
            for blk in range(NCT):
                psE = psE1 if blk % 2 == 0 else psE2
                mmn = P.op("tensor", lambda t, blk=blk, psE=psE: t.matmul(
                    psE[0:2, :], eb[:, :], scr8[:, blk, :], start=True, stop=True),
                    after=(sq1, t_eb, fr_psE1 if blk == 0 else None,
                           fr_psE2 if blk == 1 else None,
                           nb1[-2] if len(nb1) >= 2 else None), inc=True)
                if blk == 0:
                    nb1.append(P.op("vector", lambda v, psE=psE: v.tensor_copy(
                        pnacc, psE[0:2, :]), after=(mmn, fr_u_ph2), inc=True))
                else:
                    nb1.append(P.op("vector", lambda v, psE=psE: v.tensor_tensor(
                        pnacc, pnacc, psE[0:2, :], ALU.max), after=(mmn,), inc=True))
            q1 = P.op("vector", lambda v: v.tensor_reduce(
                tinyB[:, :], pnacc, AX.X, ALU.max), after=(nb1[-1],), inc=True)
            g1 = P.op("gpsimd", lambda g: g.tensor_reduce(
                tinyC[:, :], tinyB[:, :], AX.C, ALU.max), after=(q1,), inc=True)
            sq2 = P.op("scalar", lambda s_: s_.activation(
                scr8[:, :, :], kf[:, :, :], AF.Square), after=ev_k + [nb1[-1], t_qs], inc=True)
            nb2 = []
            for blk in range(NCT):
                psE = psE1 if blk % 2 == 0 else psE2
                mmn = P.op("tensor", lambda t, blk=blk, psE=psE: t.matmul(
                    psE[0:2, :], eb[:, :], scr8[:, blk, :], start=True, stop=True),
                    after=(sq2, nb1[-2 + (blk % 2)] if blk < 2 else None,
                           nb2[-2] if len(nb2) >= 2 else None), inc=True)
                if blk == 0:
                    nb2.append(P.op("vector", lambda v, psE=psE: v.tensor_copy(
                        pnacc, psE[0:2, :]), after=(mmn, q1), inc=True))
                else:
                    nb2.append(P.op("vector", lambda v, psE=psE: v.tensor_tensor(
                        pnacc, pnacc, psE[0:2, :], ALU.max), after=(mmn,), inc=True))
            fr_psE1 = nb2[-2]
            fr_psE2 = nb2[-1]
            k1 = P.op("vector", lambda v: v.tensor_reduce(
                tinyB[:, :], pnacc, AX.X, ALU.max), after=(nb2[-1],), inc=True)
            g2 = P.op("gpsimd", lambda g: g.tensor_reduce(
                tinyC2[:, :], tinyB[:, :], AX.C, ALU.max), after=(k1, g1), inc=True)
            l1_ = P.op("scalar", lambda s_: s_.activation(tinyC[:, :], tinyC[:, :], AF.Ln),
                       after=(g1, sq2), inc=True)
            l2_ = P.op("scalar", lambda s_: s_.activation(tinyC2[:, :], tinyC2[:, :], AF.Ln),
                       after=(g2,), inc=True)
            l3_ = P.op("vector", lambda v: v.tensor_add(tinyC[:, :], tinyC[:, :], tinyC2[:, :]),
                       after=(l1_, l2_), inc=True)
            l4_ = P.op("vector", lambda v: v.tensor_scalar(
                tinyC[:, :], tinyC[:, :], 0.5, -float(np.log(8.0)), ALU.mult, ALU.add),
                after=(l3_,), inc=True)
            l5_ = P.op("scalar", lambda s_: s_.activation(tinyC[:, :], tinyC[:, :], AF.Exp),
                       after=(l4_,), inc=True)
            nb6 = P.op("tensor", lambda t: t.matmul(
                psE2[:, 0:1], ones_f[0:1, :], tinyC[:, :], start=True, stop=True),
                after=(l5_, fr_psE2, t_of), inc=True)
            t_negB = P.op("vector", lambda v: v.tensor_scalar(
                negB[:, :], psE2[:, 0:1], -1.0, None, ALU.mult), after=(nb6,), inc=True)
            fr_psE2 = t_negB

            psSS = [psS[0], psS[1], psA[0], psA[1]]
            fr_psSA = [fr_psS[0], fr_psS[1], fr_psA[0], fr_psA[1]]
            itc = 0
            dn_tok = []
            ev_o = []
            den_local = []
            fr_den_gate = fr_den
            for s4 in range(4):
                h = 4 * c + s4
                base = (h % 2) * 64
                rp = []
                base_dep = (t_qs0, t_qs1, t_ks0, t_ks, sq2, l1_, ev_q[-1],
                            fr_scr8 if s4 == 0 else None,
                            dn_tok[-1] if dn_tok else None)
                for i3, (dv, srcbuf, lo, par) in enumerate((
                        (qv, qf, True, 0), (qv, qsh, True, 1),
                        (qv, qsh, False, 0), (qv, qf, False, 1),
                        (kv, kf, True, 0), (kv, ksh, True, 1),
                        (kv, ksh, False, 0), (kv, kf, False, 1))):
                    prange = slice(0, 64) if lo else slice(64, 128)
                    dst = dv[prange, par::2, :]
                    srca = srcbuf[prange, :, s4 * 128:(s4 + 1) * 128]
                    if i3 % 2 == 0:
                        rp.append(P.op("scalar", lambda v, dst=dst, srca=srca: v.copy(dst, srca),
                                       after=base_dep, inc=True))
                    else:
                        rp.append(P.op("vector", lambda v, dst=dst, srca=srca: v.tensor_copy(dst, srca),
                                       after=base_dep, inc=True))
                for qc in range(4):
                    oslot = (s4 * 4 + qc) % 2
                    if len(ev_o) >= 2:
                        o_dep, o_dep2 = ev_o[-2], dn_tok[-2]
                    else:
                        o_dep, o_dep2 = fr_psO[oslot] or (None, None)
                    av = None
                    pending = []

                    def emit_av(args, s4=s4, oslot=oslot, o_dep=o_dep, o_dep2=o_dep2):
                        nonlocal av
                        ck, pslot, mk = args
                        av = P.op("tensor", lambda t, s4=s4, ck=ck, pslot=pslot, oslot=oslot: t.matmul(
                            psO[oslot][:, :],
                            vtok[:, s4, ck * 65:(ck + 1) * 65],
                            probs[pslot][:, :],
                            start=(ck == 0), stop=(ck == 15)),
                            after=((mk,) + ((ev_v[-1], o_dep, o_dep2) if ck == 0 else ())),
                            inc=(ck == 15))
                        fr_probs[pslot] = av

                    for ckp in range(8):
                        scp = []
                        for ck in (2 * ckp, 2 * ckp + 1):
                            sslot = itc % 4
                            itc += 1
                            pr2 = slice(0, 64) if ck % 2 == 0 else slice(64, 128)
                            tk_sc = P.op("tensor", lambda t, ck=ck, qc=qc, sslot=sslot, pr2=pr2: t.matmul(
                                psSS[sslot][:, :], kv[pr2, ck, :],
                                qv[pr2, 4 * qc:4 * qc + 4, :], start=True, stop=True),
                                after=(fr_psSA[sslot], rp[-1], rp[-2]), inc=True)
                            scp.append((ck, sslot, tk_sc))
                        for ck, sslot, tk_sc in scp:
                            pslot = sslot
                            tk_ex = P.op("scalar", lambda s_, sslot=sslot, pslot=pslot: s_.activation(
                                probs[pslot][:, :], psSS[sslot][:, :],
                                AF.Exp, bias=negB[:, :], scale=0.125),
                                after=(tk_sc, t_negB, fr_probs[pslot]), inc=True)
                            fr_psSA[sslot] = tk_ex
                            dd_ = min(max(ck - 4 * qc, 0), 4)
                            mk = tk_ex
                            if dd_ > 0:
                                pv = probs[pslot][:, 0:dd_ * 128].rearrange(
                                    "p (a b) -> p a b", a=dd_, b=128)
                                mk = P.op("vector", lambda v, pv=pv, dd_=dd_: v.tensor_mul(
                                    pv, pv, m_strict[:, :].unsqueeze(1).broadcast_to([128, dd_, 128])),
                                    after=(mk, t_ms), inc=True)
                            if dd_ < 4:
                                nsl = 4 - dd_
                                pv = probs[pslot][:, dd_ * 128:CH].rearrange(
                                    "p (a b) -> p a b", a=nsl, b=128)
                                mk = P.op("vector", lambda v, pv=pv, nsl=nsl: v.tensor_mul(
                                    pv, pv, m_incl[:, :].unsqueeze(1).broadcast_to([128, nsl, 128])),
                                    after=(mk, t_mi), inc=True)
                            pending.append((ck, pslot, mk))
                        while len(pending) > 2:
                            emit_av(pending.pop(0))
                    while pending:
                        emit_av(pending.pop(0))
                    o_dst = o_buf[base:base + 64, h // 2, :].rearrange(
                        "p (j cc) -> p j cc", j=128, cc=16)[
                        :, :, 4 * qc:4 * qc + 4].transpose([0, 2, 1])
                    ev_o.append(P.op("vector", lambda v, o_dst=o_dst, oslot=oslot: v.tensor_copy(
                        o_dst, psO[oslot][0:64, :].rearrange("p (s j) -> p s j", s=4, j=128)),
                        after=(av, fr_obuf), inc=True))
                    dn_dst = den[64:65].rearrange(
                        "o (j cc) -> o j cc", j=128, cc=16)[
                        :, :, 4 * qc:4 * qc + 4].transpose([0, 2, 1])
                    dn_tok.append(P.op("scalar", lambda s_, dn_dst=dn_dst, oslot=oslot: s_.copy(
                        dn_dst, psO[oslot][64:65, :].rearrange("o (s j) -> o s j", s=4, j=128)),
                        after=(av, fr_den_gate), inc=True))
                half_dma = P.dma("sync",
                                 dend[s4:s4 + 1, :],
                                 den[64:65][:, 0:4 * CH],
                                 after=dn_tok[-4:] + [fr_dend])
                den_local.append(half_dma)
                fr_den_gate = half_dma
            fr_psO[0] = (ev_o[-2], dn_tok[-2])
            fr_psO[1] = (ev_o[-1], dn_tok[-1])
            den_dmas += den_local
            fr_den = den_local[-1]
            fr_scr8 = av
            fr_psS[0] = fr_psSA[0]
            fr_psS[1] = fr_psSA[1]
            fr_psA[0] = fr_psSA[2]
            fr_psA[1] = fr_psSA[3]

            t_rec = P.op("vector", lambda v: v.reciprocal(dend[:, :], dend[:, :]),
                         after=den_local, inc=True)
            t_recb = P.op("vector", lambda v: v.tensor_copy(dendb[:, :], dend[:, :]),
                          after=(t_rec, fr_dendb), inc=True)
            fr_dend = t_recb
            scl_prev = [fr_psE1, fr_psE2]
            psE = [psE1, psE2]
            mul_toks = []
            for i2 in range(2):
                ct = 2 * c + i2
                for qc in range(4):
                    eslot = (i2 * 4 + qc) % 2
                    t_sc = P.op("tensor", lambda t, i2=i2, qc=qc, eslot=eslot: t.matmul(
                        psE[eslot][:, :], sel4[:, i2 * 128:(i2 + 1) * 128],
                        dendb[0:4, qc * CH:(qc + 1) * CH], start=True, stop=True),
                        after=(t_recb, t_s4, scl_prev[eslot],
                               mul_toks[-2] if len(mul_toks) >= 2 else None), inc=True)
                    scl_prev[eslot] = None
                    mul_toks.append(P.op("vector", lambda v, ct=ct, qc=qc, eslot=eslot: v.tensor_tensor(
                        o_buf[:, ct, qc * CH:(qc + 1) * CH],
                        o_buf[:, ct, qc * CH:(qc + 1) * CH],
                        psE[eslot][:, :], ALU.mult),
                        after=(t_sc, ev_o[-1]), inc=True))
            fr_dendb = t_sc
            fr_psE1 = mul_toks[-2]
            fr_psE2 = mul_toks[-1]
            scale_toks_all += mul_toks
            fr_xin = cv_toks[-1]

        t_ow = [P.dma("gpsimd", outwT[i2],
                      io["outw_t"][i2 * 4:(i2 + 1) * 4].transpose([1, 0, 2]),
                      after=(cv_toks[-1] if i2 == 0 else None, fr_xs, fr_outw))
                for i2 in range(2)]
        out_last = None
        for c in range(NCHUNK):
            cb = c * CH
            t_xf = P.dma("sync", xin_t[:, :, 1:CH + 1],
                         io["xin"][s, :, :, cb:cb + CH].transpose([1, 0, 2]),
                         after=(fr_xin, out_last))
            res_toks = []
            for mt in range(NCT):
                slot = mt % 2
                dep = res_toks[-2] if len(res_toks) >= 2 else fr_psA[slot]
                for ci in range(NCT):
                    tk = P.op("tensor", lambda t, mt=mt, ci=ci, slot=slot, cb=cb: t.matmul(
                        psA[slot][:, :],
                        outwT[ci // 4][:, ci % 4, mt * 128:(mt + 1) * 128],
                        o_buf[:, ci, cb:cb + CH], start=(ci == 0), stop=(ci == NCT - 1)),
                        after=(scale_toks_all[-1], t_ow[0], t_ow[1], dep) if ci == 0 else (),
                        inc=(ci == NCT - 1))
                res_toks.append(P.op("vector", lambda v, mt=mt, slot=slot: v.tensor_add(
                    xin_t[:, mt, 1:CH + 1], xin_t[:, mt, 1:CH + 1], psA[slot][:, :]),
                    after=(tk, t_xf), inc=True))
            fr_psA[0] = res_toks[-2]
            fr_psA[1] = res_toks[-1]
            st2 = []
            sq2s = []
            for ct in range(NCT):
                t_rb = P.op("vector", lambda v, ct=ct: v.tensor_copy(
                    rb[:, :], xin_t[:, ct, 1:CH + 1]),
                    after=(res_toks[ct], st2[-1] if st2 else None, fr_rb), inc=True)
                t_r2 = P.op("scalar", lambda s_, ct=ct: s_.activation(
                    r2b[:, :], xin_t[:, ct, 1:CH + 1], AF.Square),
                    after=(res_toks[ct], sq2s[-1] if sq2s else None, fr_r2b), inc=True)
                st2.append(P.op("tensor", lambda t, ct=ct: t.matmul(
                    psE1[0:1, :], ones_b[:, 0:1], rb[:, :],
                    start=(ct == 0), stop=(ct == NCT - 1)),
                    after=(t_rb, t_ob) + ((fr_psE1,) if ct == 0 else ()), inc=True))
                sq2s.append(P.op("tensor", lambda t, ct=ct: t.matmul(
                    psE2[0:1, :], ones_b[:, 0:1], r2b[:, :],
                    start=(ct == 0), stop=(ct == NCT - 1)),
                    after=(t_r2,) + ((fr_psE2,) if ct == 0 else ()), inc=True))
                fr_rb = st2[-1]
                fr_r2b = sq2s[-1]
            f1 = P.op("vector", lambda v: v.tensor_copy(ssum, psE1[0:1, :]),
                      after=(st2[-1], fr_dend if c == 0 else None), inc=True)
            f2 = P.op("vector", lambda v: v.tensor_copy(ssq, psE2[0:1, :]),
                      after=(sq2s[-1],), inc=True)
            f3 = P.op("tensor", lambda t: t.matmul(
                psE1[:, :], ones_f[0:1, :], ssum, start=True, stop=True),
                after=(f1,), inc=True)
            f4 = P.op("tensor", lambda t: t.matmul(
                psE2[:, :], ones_f[0:1, :], ssq, start=True, stop=True),
                after=(f2,), inc=True)
            f5 = P.op("vector", lambda v: v.tensor_scalar(
                mb, psE1[:, :], 1.0 / D, None, ALU.mult),
                after=(f3, scale_toks_all[-1] if c == 0 else None), inc=True)
            fr_psE1 = f5
            f6 = P.op("scalar", lambda s_: s_.activation(m2, mb, AF.Square),
                      after=(f5,), inc=True)
            f7 = P.op("vector", lambda v: v.scalar_tensor_tensor(
                vb, psE2[:, :], 1.0 / D, m2, ALU.mult, ALU.subtract),
                after=(f4, f6), inc=True)
            fr_psE2 = f7
            f8 = P.op("scalar", lambda s_: s_.activation(vb, vb, AF.Ln, bias=eps_t[:, :]),
                      after=(f7, t_eps0), inc=True)
            f9 = P.op("scalar", lambda s_: s_.activation(rstd, vb, AF.Exp, scale=-0.5),
                      after=(f8,), inc=True)
            out_toks = []
            for ct in range(NCT):
                u1 = P.op("vector", lambda v, ct=ct: v.tensor_sub(
                    xin_t[:, ct, 1:CH + 1], xin_t[:, ct, 1:CH + 1], mb),
                    after=(f5, res_toks[ct]), inc=True)
                out_toks.append(P.op("vector", lambda v, ct=ct: v.tensor_mul(
                    xin_t[:, ct, 1:CH + 1], xin_t[:, ct, 1:CH + 1], rstd),
                    after=(u1, f9), inc=True))
            t_out = P.dma("sync", io["xout"][s, :, :, cb:cb + CH].transpose([1, 0, 2]),
                          xin_t[:, :, 1:CH + 1], after=out_toks)
            out_last = t_out
        fr_xin = out_last
        fr_obuf = tk
        fr_outw = tk
        fr_xcb = tk
        fr_den = out_last
        fr_u_ph2 = out_last
    final_tok = ("d", "sync", P.dn["sync"])
    P.op("sync", lambda e: e.nop(), after=(final_tok,))
    return final_tok


def build_l1a_program():
    nc = bass.Bass("TRN2", target_bir_lowering=False, debug=False)
    io = {}
    io["xin"] = nc.dram_tensor("xin", [NSEQ, NCT, 128, T], F32, kind="ExternalInput").ap()
    io["xout"] = nc.dram_tensor("xout", [NSEQ, NCT, 128, T], F32, kind="ExternalOutput").ap()
    io["conv3"] = nc.dram_tensor("conv3", [NCT, 128, 3], F32, kind="ExternalInput").ap()
    io["pe_t"] = nc.dram_tensor("pe_t", [NCT, 128, D], F32, kind="ExternalInput").ap()
    io["qkv_t"] = nc.dram_tensor("qkv_t", [NCT, 128, 3072], F32, kind="ExternalInput").ap()
    io["outw_t"] = nc.dram_tensor("outw_t", [NCT, 128, D], F32, kind="ExternalInput").ap()
    io["ident_f32"] = nc.dram_tensor("ident_f32", [128, 128], F32, kind="ExternalInput").ap()
    io["ones_bf16"] = nc.dram_tensor("ones_bf16", [128, 128], BF16, kind="ExternalInput").ap()
    io["ones_f32"] = nc.dram_tensor("ones_f32", [128, 128], F32, kind="ExternalInput").ap()
    io["mask_strict"] = nc.dram_tensor("mask_strict", [128, 128], BF16, kind="ExternalInput").ap()
    io["mask_incl"] = nc.dram_tensor("mask_incl", [128, 128], BF16, kind="ExternalInput").ap()
    io["sel4"] = nc.dram_tensor("sel4", [4, 2 * 128], BF16, kind="ExternalInput").ap()
    io["eb_bf16"] = nc.dram_tensor("eb_bf16", [128, 2], BF16, kind="ExternalInput").ap()
    stack = ExitStack()
    P = Prog(nc)
    P.make_sems(stack)
    build_l1a(nc, P, stack, io)
    block = stack.enter_context(nc.Block())
    P.finalize(block)
    stack.close()
    return nc


def l1a_in_maps(inputs, x2_core):
    ident, ones_b, ones_f, maskT, maskLK, sel, eb = _host_consts()
    cw = np.asarray(inputs["L1_attn_conv_w"])[:, 0, :, 1]
    j = np.arange(128)
    strict = (j[:, None] < j[None, :]).astype(ml_dtypes.bfloat16)
    incl = (j[:, None] <= j[None, :]).astype(ml_dtypes.bfloat16)
    m_strict = strict
    m_incl = incl
    sel4 = np.zeros((4, 2 * 128), dtype=ml_dtypes.bfloat16)
    for i2 in range(2):
        for pp in range(128):
            sel4[2 * i2 + (pp >= 64), i2 * 128 + pp] = 1.0
    return {
        "xin": x2_core,
        "conv3": np.ascontiguousarray(cw.reshape(NCT, 128, 3)),
        "pe_t": _wT(np.asarray(inputs["L1_attn_pe_w"])),
        "qkv_t": _wT(np.asarray(inputs["L1_attn_qkv_w"])),
        "outw_t": _wT(np.asarray(inputs["L1_attn_out_w"])),
        "ident_f32": ident, "ones_bf16": ones_b, "ones_f32": ones_f,
        "mask_strict": np.ascontiguousarray(m_strict),
        "mask_incl": np.ascontiguousarray(m_incl),
        "sel4": sel4, "eb_bf16": eb,
    }



def build_l1f(nc, P, stack, io):
    E = stack.enter_context
    sb = lambda name, shape, dt=F32: E(nc.sbuf_tensor(name, shape, dt))
    NE = 8

    xb = sb("l1f_xb", [128, NCT, T], BF16)
    yacc = sb("l1f_yacc", [128, NCT, T], F32)
    w1s = [sb(f"l1f_w1_{i}", [128, NCT, 1024], BF16) for i in range(2)]
    w3s = [sb(f"l1f_w3_{i}", [128, NCT, 1024], BF16) for i in range(2)]
    w2s = sb("l1f_w2", [128, NCT, 1024], BF16)
    xf = w2s[:, :, :].rearrange("p a b -> p (a b)").bitcast(F32).rearrange(
        "p (a b) -> p a b", a=NCT, b=CH)
    G = sb("l1f_G", [128, NCT, CH], BF16)
    sil0 = sb("l1f_sil0", [128, CH], BF16)
    rout = sb("l1f_rout", [128, 8, 128], F32)
    logits = rout[:, 0, :].rearrange("p (a b) -> p a b", a=16, b=8)
    scores = rout[:, 1, :].rearrange("p (a b) -> p a b", a=16, b=8)
    sbpb = rout[:, 2, :].rearrange("p (a b) -> p a b", a=16, b=8)
    oh1 = rout[:, 3, :].rearrange("p (a b) -> p a b", a=16, b=8)
    oh2 = rout[:, 4, :].rearrange("p (a b) -> p a b", a=16, b=8)
    cumt = rout[:, 5, :].rearrange("p (a b) -> p a b", a=16, b=8)
    cumt2 = rout[:, 6, :].rearrange("p (a b) -> p a b", a=16, b=8)
    gts = rout[:, 7, :].rearrange("p (a b) -> p a b", a=16, b=8)
    red16 = sb("l1f_red16", [128, 16], F32)
    m8 = sb("l1f_m8", [128, 16, 8], F32)
    gT = yacc[0:8, 0, :]
    gTb = sb("l1f_gTb", [8, T], BF16)
    e8sel = sb("l1f_e8sel", [8, NE * 128], BF16)
    rw = sb("l1f_rw", [128, NCT, 8], F32)
    ebias = sb("l1f_ebias", [128, 8], F32)
    ones_b = sb("l1f_ones_b", [128, 128], BF16)
    ones_f = sb("l1f_ones_f", [128, 128], F32)
    ident = sb("l1f_ident", [128, 128], F32)
    eps_t = sb("l1f_eps", [128, 1], F32)
    u1_t = sb("l1f_u1", [128, 2 * CH], F32)
    u2_t = sb("l1f_u2", [128, 2 * CH], F32)
    mb, m2 = u1_t[:, 0:CH], u1_t[:, CH:2 * CH]
    vb, rstd = u2_t[:, 0:CH], u2_t[:, CH:2 * CH]
    _routf = rout[:, :, :].rearrange("p a b -> p (a b)")
    ssum = _routf[0:1, 0:CH]
    ssq = _routf[0:1, CH:2 * CH]
    rb_t = sb("l1f_rb0", [128, CH], BF16)
    r2b_t = sb("l1f_r2b0", [128, CH], BF16)
    rb = [rb_t, rb_t]
    r2b = [r2b_t, r2b_t]
    sil = [sil0, rb_t[:, :]]

    psA = [E(nc.psum_tensor(f"l1f_psA{i}", [128, CH], F32)) for i in range(4)]
    psY = [E(nc.psum_tensor(f"l1f_psY{i}", [128, CH], F32)) for i in range(2)]
    psE1 = E(nc.psum_tensor("l1f_psE1", [128, CH], F32))
    psE2 = E(nc.psum_tensor("l1f_psE2", [128, CH], F32))

    t_ob = P.dma("sync", ones_b[:, :], io["ones_bf16"])
    t_of = P.dma("sync", ones_f[:, :], io["ones_f32"])
    t_id = P.dma("sync", ident[:, :], io["ident_f32"])
    t_es = P.dma("sync", e8sel[:, :], io["e8sel"])
    t_rw = P.dma("sync", rw[:, :, :], io["router_t"].transpose([1, 0, 2]))
    t_eb = P.dma("sync", ebias[:, :], io["ebias_b"])
    t_eps0 = P.op("vector", lambda v: v.memset(eps_t[:, :], 1e-5), inc=True)

    def wload(dst, e, which, after=()):
        if e == 8:
            src = io[("sw1", "sw3", "sw2")[which]].transpose([1, 0, 2])
        else:
            src = io[("w1", "w3", "w2")[which]][e].transpose([1, 0, 2])
        return P.dma("gpsimd", dst[:, :, :], src, after=after)

    fr_xb = None
    fr_w1 = [None, None]
    fr_w3 = [None, None]
    fr_w2 = None
    fr_G = None
    fr_sil = [None, None]
    fr_psA = [None, None, None, None]
    fr_psY = [None, None]
    fr_psE1 = None
    fr_psE2 = None
    fr_yacc = None
    fr_gT = None
    fr_rout = None
    fr_rb = [None, None]
    fr_r2b = [None, None]
    t_w = {}
    for s in range(NSEQ):
        t_xb = P.dma("gpsimd", xb[:, :, :],
                     io["xin"][s, :, :, :].transpose([1, 0, 2]), after=(fr_xb,))

        lg_toks = []
        t_xr_last = None
        for rc in range(4):
            t_xr = P.dma("sync", xf[:, :, :],
                         io["xin"][s, :, :, rc * CH:(rc + 1) * CH].transpose([1, 0, 2]),
                         after=(fr_w2 if rc == 0 else None, t_xr_last))
            lg_local = []
            for tl in range(4):
                tt = rc * 4 + tl
                slot = tt % 2
                dep = lg_toks[-2] if len(lg_toks) >= 2 else fr_psA[slot]
                for ci in range(NCT):
                    tk = P.op("tensor", lambda t, tt=tt, tl=tl, ci=ci, slot=slot: t.matmul(
                        psA[slot][:, 0:8], xf[:, ci, tl * 128:(tl + 1) * 128],
                        rw[:, ci, :], start=(ci == 0), stop=(ci == NCT - 1)),
                        after=(t_xr, t_rw, dep, fr_rout) if ci == 0 else (),
                        inc=(ci == NCT - 1))
                lg_toks.append(P.op("vector", lambda v, tt=tt, slot=slot: v.tensor_copy(
                    logits[:, tt, :], psA[slot][:, 0:8]), after=(tk,), inc=True))
            t_xr_last = lg_toks[-1]
        fr_w2 = lg_toks[-1]
        r1 = P.op("vector", lambda v: v.tensor_reduce(
            red16[:, :], logits[:, :, :], AX.X, ALU.max), after=lg_toks, inc=True)
        r2 = P.op("vector", lambda v: v.tensor_sub(
            scores[:, :, :], logits[:, :, :],
            red16[:, :].unsqueeze(2).broadcast_to([128, 16, 8])), after=(r1,), inc=True)
        r3 = P.op("scalar", lambda s_: s_.activation(scores[:, :, :], scores[:, :, :], AF.Exp),
                  after=(r2,), inc=True)
        r4 = P.op("vector", lambda v: v.tensor_reduce(
            red16[:, :], scores[:, :, :], AX.X, ALU.add), after=(r3,), inc=True)
        r5 = P.op("vector", lambda v: v.reciprocal(red16[:, :], red16[:, :]),
                  after=(r4,), inc=True)
        r6 = P.op("vector", lambda v: v.tensor_mul(
            scores[:, :, :], scores[:, :, :],
            red16[:, :].unsqueeze(2).broadcast_to([128, 16, 8])), after=(r5,), inc=True)
        r7 = P.op("vector", lambda v: v.tensor_add(
            sbpb[:, :, :], scores[:, :, :],
            ebias[:, :].unsqueeze(1).broadcast_to([128, 16, 8])), after=(r6, t_eb), inc=True)
        m8t = []
        for tt in range(16):
            m8t.append(P.op("vector", lambda v, tt=tt: v.max(
                m8[:, tt, :], sbpb[:, tt, :]), after=(r7,), inc=True))

        def onehot(dst, val_col, excl, after):
            t1 = P.op("vector", lambda v: v.tensor_tensor(
                dst, sbpb[:, :, :],
                m8[:, :, val_col:val_col + 1].broadcast_to([128, 16, 8]),
                ALU.is_equal), after=after, inc=True)
            if excl is not None:
                t1a = P.op("vector", lambda v: v.tensor_mul(
                    cumt[:, :, :], dst, excl), after=(t1,), inc=True)
                t1 = P.op("vector", lambda v: v.tensor_sub(
                    dst, dst, cumt[:, :, :]), after=(t1a,), inc=True)
            t2 = P.op("vector", lambda v: v.tensor_copy(cumt[:, :, :], dst), after=(t1,), inc=True)
            for sh, (a, b) in enumerate(((cumt, cumt2), (cumt2, cumt))):
                shift = 1 << sh
                t3 = P.op("vector", lambda v, a=a, b=b, shift=shift: v.tensor_copy(
                    b[:, :, 0:shift], a[:, :, 0:shift]), after=(t2,), inc=True)
                t4 = P.op("vector", lambda v, a=a, b=b, shift=shift: v.tensor_add(
                    b[:, :, shift:8], a[:, :, shift:8], a[:, :, 0:8 - shift]),
                    after=(t3,), inc=True)
                t2 = t4
            t5 = P.op("vector", lambda v: v.tensor_copy(
                cumt2[:, :, 0:4], cumt[:, :, 0:4]), after=(t2,), inc=True)
            t6 = P.op("vector", lambda v: v.tensor_add(
                cumt2[:, :, 4:8], cumt[:, :, 4:8], cumt[:, :, 0:4]), after=(t5,), inc=True)
            t7 = P.op("vector", lambda v: v.tensor_scalar(
                cumt2[:, :, :], cumt2[:, :, :], 1.0, None, ALU.is_equal),
                after=(t6,), inc=True)
            t8 = P.op("vector", lambda v: v.tensor_mul(dst, dst, cumt2[:, :, :]),
                      after=(t7,), inc=True)
            return t8

        o1t = onehot(oh1[:, :, :], 0, None, tuple(m8t) + (r7, fr_rout))
        o2t = onehot(oh2[:, :, :], 1, oh1[:, :, :], (o1t,))
        g1 = P.op("vector", lambda v: v.tensor_mul(gts[:, :, :], scores[:, :, :], oh1[:, :, :]),
                  after=(o1t, r6), inc=True)
        g2 = P.op("vector", lambda v: v.scalar_tensor_tensor(
            cumt[:, :, :], scores[:, :, :], 1.0, oh2[:, :, :], ALU.mult, ALU.mult),
            after=(o2t, g1), inc=True)
        g3 = P.op("vector", lambda v: v.tensor_add(
            gts[:, :, :], gts[:, :, :], cumt[:, :, :]), after=(g2,), inc=True)
        tr_toks = []
        for tt in range(16):
            tslot = tt % 2
            psE = psE1 if tslot == 0 else psE2
            tk = P.op("tensor", lambda t, tt=tt, psE=psE: t.transpose(
                psE[0:8, 0:128], gts[:, tt, :], ident[:, :]),
                after=(g3, t_id, fr_psE1 if tt == 0 else None,
                       fr_psE2 if tt == 1 else None,
                       tr_toks[-2] if len(tr_toks) >= 2 else None), inc=True)
            tr_toks.append(P.op("vector", lambda v, tt=tt, psE=psE: v.tensor_copy(
                gT[:, tt * 128:(tt + 1) * 128], psE[0:8, 0:128]),
                after=(tk, fr_gT), inc=True))
        t_gTb = P.op("vector", lambda v: v.tensor_copy(gTb[:, :], gT[:, :]),
                     after=(tr_toks[-1],), inc=True)
        fr_psE1 = tr_toks[-2]
        fr_psE2 = tr_toks[-1]
        fr_rout = t_gTb
        if DEBUG and s == 0:
            P.dma("sync", io["dbg_gTb"], gTb[:, :], after=(t_gTb,))
            P.dma("sync", io["dbg_scores"],
                  rout[:, 1, :], after=(t_gTb,))

        if s == 0:
            t_w[(0, 0)] = wload(w1s[0], 0, 0, after=(fr_w1[0],))
            t_w[(0, 1)] = wload(w3s[0], 0, 1, after=(fr_w3[0],))
        yadd_toks = []
        for e in range(9):
            eslot = e % 2
            if e < 8:
                t_w[(e + 1, 0)] = wload(w1s[1 - eslot], e + 1, 0, after=(fr_w1[1 - eslot],))
                t_w[(e + 1, 1)] = wload(w3s[1 - eslot], e + 1, 1, after=(fr_w3[1 - eslot],))
            t_w[(e, 2)] = wload(w2s, e, 2, after=(fr_w2,))
            y_this = []
            if DEBUG and s == 0 and e == 1:
                P.dma("sync", io["dbg_w11"], w1s[eslot][:, :, :], after=(t_w[(1, 0)],))
            for c in range(NCHUNK):
                cb = c * CH
                if e < 8:
                    bcm = P.op("tensor", lambda t, e=e, cb=cb: t.matmul(
                        psE1[:, :], e8sel[:, e * 128:(e + 1) * 128], gTb[:, cb:cb + CH],
                        start=True, stop=True), after=(t_gTb, t_es, fr_psE1), inc=True)
                g_toks = []
                for ht in range(NCT):
                    s1_ = (ht % 2) * 2
                    s3_ = s1_ + 1
                    dep1 = g_toks[-2] if len(g_toks) >= 2 else fr_psA[s1_]
                    for ci in range(NCT):
                        tk1 = P.op("tensor", lambda t, ht=ht, ci=ci, s1_=s1_, eslot=eslot, cb=cb: t.matmul(
                            psA[s1_][:, :], w1s[eslot][:, ci, ht * 128:(ht + 1) * 128],
                            xb[:, ci, cb:cb + CH], start=(ci == 0), stop=(ci == NCT - 1)),
                            after=(t_w[(e, 0)], t_xb, dep1) if ci == 0 else (),
                            inc=(ci == NCT - 1))
                    dep3 = g_toks[-2] if len(g_toks) >= 2 else fr_psA[s3_]
                    for ci in range(NCT):
                        tk3 = P.op("tensor", lambda t, ht=ht, ci=ci, s3_=s3_, eslot=eslot, cb=cb: t.matmul(
                            psA[s3_][:, :], w3s[eslot][:, ci, ht * 128:(ht + 1) * 128],
                            xb[:, ci, cb:cb + CH], start=(ci == 0), stop=(ci == NCT - 1)),
                            after=(t_w[(e, 1)], dep3) if ci == 0 else (),
                            inc=(ci == NCT - 1))
                    ss = ht % 2
                    dbg_h = None
                    if DEBUG and s == 0 and e == 1 and c == 0 and ht == 0:
                        cpy1 = P.op("vector", lambda v, s1_=s1_: v.tensor_copy(
                            u1_t[:, 0:CH], psA[s1_][:, :]), after=(tk1,), inc=True)
                        d1 = P.dma("sync", io["dbg_H1"], u1_t[:, 0:CH], after=(cpy1,))
                        cpy2 = P.op("vector", lambda v, s3_=s3_: v.tensor_copy(
                            u1_t[:, CH:2 * CH], psA[s3_][:, :]), after=(tk3, d1), inc=True)
                        d2 = P.dma("sync", io["dbg_H3"], u1_t[:, CH:2 * CH], after=(cpy2,))
                        dbg_h = d2
                    tsl = P.op("scalar", lambda s_, ht=ht, s1_=s1_, ss=ss: s_.activation(
                        sil[ss][:, :], psA[s1_][:, :], AF.Silu),
                        after=(tk1, fr_sil[ss], dbg_h,
                               fr_rb[0] if ss == 1 and ht == 1 and c == 0 and e == 0 else None),
                        inc=True)
                    if DEBUG and s == 0 and e == 1 and c == 0 and ht == 0:
                        d3 = P.dma("sync", io["dbg_sil"], sil[ss][:, :], after=(tsl,))
                        tsl = P.op("vector", lambda v: v.nop(), after=(d3, tsl), inc=True)
                    if e < 8:
                        gm = P.op("vector", lambda v, ht=ht, s3_=s3_, ss=ss: v.tensor_tensor(
                            G[:, ht, :], sil[ss][:, :], psA[s3_][:, :], ALU.mult),
                            after=(tsl, tk3, fr_G), inc=True)
                        gm = P.op("vector", lambda v, ht=ht: v.tensor_tensor(
                            G[:, ht, :], G[:, ht, :], psE1[:, :], ALU.mult),
                            after=(gm, bcm), inc=True)
                    else:
                        gm = P.op("vector", lambda v, ht=ht, s3_=s3_, ss=ss: v.tensor_tensor(
                            G[:, ht, :], sil[ss][:, :], psA[s3_][:, :], ALU.mult),
                            after=(tsl, tk3, fr_G), inc=True)
                    fr_sil[ss] = gm
                    g_toks.append(gm)
                if e < 8:
                    fr_psE1 = g_toks[-1]
                dbg_pending = None
                if DEBUG and s == 0 and e == 1 and c == 0:
                    dbg_pending = P.dma("sync", io["dbg_G1"], G[:, :, :], after=g_toks)
                for dt in range(NCT):
                    yslot = dt % 2
                    dep = yadd_toks[-2] if len(yadd_toks) >= 2 else fr_psY[yslot]
                    for ht in range(NCT):
                        tk = P.op("tensor", lambda t, dt=dt, ht=ht, yslot=yslot: t.matmul(
                            psY[yslot][:, :], w2s[:, ht, dt * 128:(dt + 1) * 128],
                            G[:, ht, :], start=(ht == 0), stop=(ht == NCT - 1)),
                            after=(g_toks[ht], t_w[(e, 2)], dep) if True else (),
                            inc=(ht == NCT - 1))
                    if e == 0:
                        yadd_toks.append(P.op("vector", lambda v, dt=dt, cb=cb, yslot=yslot: v.tensor_copy(
                            yacc[:, dt, cb:cb + CH], psY[yslot][:, :]),
                            after=(tk, fr_yacc, t_gTb), inc=True))
                    else:
                        yadd_toks.append(P.op("vector", lambda v, dt=dt, cb=cb, yslot=yslot: v.tensor_add(
                            yacc[:, dt, cb:cb + CH], yacc[:, dt, cb:cb + CH], psY[yslot][:, :]),
                            after=(tk,), inc=True))
                y_this.append(yadd_toks[-1])
                if dbg_pending is not None:
                    fr_G = P.op("vector", lambda v: v.nop(), after=(dbg_pending, tk), inc=True)
                else:
                    fr_G = tk
                fr_psA = [g_toks[-2], g_toks[-2], g_toks[-1], g_toks[-1]]
                fr_psY[0] = yadd_toks[-2]
                fr_psY[1] = yadd_toks[-1]
            fr_w1[eslot] = tk1
            fr_w3[eslot] = tk3
            fr_w2 = tk
        fr_xb = tk1

        if DEBUG and s == 0:
            dbg_y = P.dma("sync", io["dbg_yacc"], yacc[:, :, :], after=(yadd_toks[-1],))
            yadd_toks = yadd_toks + [dbg_y]
        out_last = None
        for c in range(NCHUNK):
            cb = c * CH
            t_xf = P.dma("sync", xf[:, :, :],
                         io["xin"][s, :, :, cb:cb + CH].transpose([1, 0, 2]),
                         after=(fr_w2 if c == 0 else None, out_last))
            res_toks = []
            for ct in range(NCT):
                res_toks.append(P.op("vector", lambda v, ct=ct, cb=cb: v.tensor_add(
                    xf[:, ct, :], xf[:, ct, :], yacc[:, ct, cb:cb + CH]),
                    after=(t_xf, yadd_toks[-1]), inc=True))
            st2 = []
            sq2s = []
            for ct in range(NCT):
                rs = ct % 2
                t_rb = P.op("vector", lambda v, ct=ct: v.tensor_copy(
                    rb[0][:, :], xf[:, ct, :]),
                    after=(res_toks[ct], st2[-1] if st2 else None,
                           fr_rb[0], fr_sil[1] if ct == 0 and c == 0 else None), inc=True)
                t_r2 = P.op("scalar", lambda s_, ct=ct: s_.activation(
                    r2b[0][:, :], xf[:, ct, :], AF.Square),
                    after=(res_toks[ct], sq2s[-1] if sq2s else None,
                           fr_r2b[0]), inc=True)
                st2.append(P.op("tensor", lambda t, ct=ct: t.matmul(
                    psE1[0:1, :], ones_b[:, 0:1], rb[0][:, :],
                    start=(ct == 0), stop=(ct == NCT - 1)),
                    after=(t_rb, t_ob) + ((fr_psE1,) if ct == 0 else ()), inc=True))
                sq2s.append(P.op("tensor", lambda t, ct=ct: t.matmul(
                    psE2[0:1, :], ones_b[:, 0:1], r2b[0][:, :],
                    start=(ct == 0), stop=(ct == NCT - 1)),
                    after=(t_r2,) + ((fr_psE2,) if ct == 0 else ()), inc=True))
                fr_rb[0] = st2[-1]
                fr_r2b[0] = sq2s[-1]
            f1 = P.op("vector", lambda v: v.tensor_copy(ssum[:, :], psE1[0:1, :]),
                      after=(st2[-1],), inc=True)
            f2 = P.op("vector", lambda v: v.tensor_copy(ssq[:, :], psE2[0:1, :]),
                      after=(sq2s[-1],), inc=True)
            f3 = P.op("tensor", lambda t: t.matmul(
                psE1[:, :], ones_f[0:1, :], ssum[:, :], start=True, stop=True),
                after=(f1, t_of), inc=True)
            f4 = P.op("tensor", lambda t: t.matmul(
                psE2[:, :], ones_f[0:1, :], ssq[:, :], start=True, stop=True),
                after=(f2,), inc=True)
            f5 = P.op("vector", lambda v: v.tensor_scalar(
                mb, psE1[:, :], 1.0 / D, None, ALU.mult), after=(f3,), inc=True)
            fr_psE1 = f5
            f6 = P.op("scalar", lambda s_: s_.activation(m2, mb, AF.Square),
                      after=(f5,), inc=True)
            f7 = P.op("vector", lambda v: v.scalar_tensor_tensor(
                vb, psE2[:, :], 1.0 / D, m2, ALU.mult, ALU.subtract),
                after=(f4, f6), inc=True)
            fr_psE2 = f7
            f8 = P.op("scalar", lambda s_: s_.activation(vb, vb, AF.Ln, bias=eps_t[:, :]),
                      after=(f7, t_eps0), inc=True)
            f9 = P.op("scalar", lambda s_: s_.activation(rstd, vb, AF.Exp, scale=-0.5),
                      after=(f8,), inc=True)
            out_toks = []
            for ct in range(NCT):
                u1 = P.op("vector", lambda v, ct=ct: v.tensor_sub(
                    xf[:, ct, :], xf[:, ct, :], mb), after=(f5, res_toks[ct]), inc=True)
                out_toks.append(P.op("vector", lambda v, ct=ct: v.tensor_mul(
                    xf[:, ct, :], xf[:, ct, :], rstd), after=(u1, f9), inc=True))
            t_out = P.dma("sync", io["xout"][s, :, :, cb:cb + CH].transpose([1, 0, 2]),
                          xf[:, :, :], after=out_toks)
            out_last = t_out
        fr_w2 = out_last
        fr_rout = out_last
        fr_yacc = res_toks[-1]
        fr_gT = res_toks[-1]
    final_tok = ("d", "sync", P.dn["sync"])
    P.op("sync", lambda e: e.nop(), after=(final_tok,))
    return final_tok


def build_l1f_program():
    nc = bass.Bass("TRN2", target_bir_lowering=False, debug=False)
    io = {}
    io["xin"] = nc.dram_tensor("xin", [NSEQ, NCT, 128, T], F32, kind="ExternalInput").ap()
    io["xout"] = nc.dram_tensor("xout", [NSEQ, NCT, 128, T], F32, kind="ExternalOutput").ap()
    io["router_t"] = nc.dram_tensor("router_t", [NCT, 128, 8], F32, kind="ExternalInput").ap()
    io["ebias_b"] = nc.dram_tensor("ebias_b", [128, 8], F32, kind="ExternalInput").ap()
    io["w1"] = nc.dram_tensor("w1", [8, NCT, 128, 1024], F32, kind="ExternalInput").ap()
    io["w3"] = nc.dram_tensor("w3", [8, NCT, 128, 1024], F32, kind="ExternalInput").ap()
    io["w2"] = nc.dram_tensor("w2", [8, NCT, 128, 1024], F32, kind="ExternalInput").ap()
    io["sw1"] = nc.dram_tensor("sw1", [NCT, 128, 1024], F32, kind="ExternalInput").ap()
    io["sw3"] = nc.dram_tensor("sw3", [NCT, 128, 1024], F32, kind="ExternalInput").ap()
    io["sw2"] = nc.dram_tensor("sw2", [NCT, 128, 1024], F32, kind="ExternalInput").ap()
    io["e8sel"] = nc.dram_tensor("e8sel", [8, 8 * 128], BF16, kind="ExternalInput").ap()
    io["ident_f32"] = nc.dram_tensor("ident_f32", [128, 128], F32, kind="ExternalInput").ap()
    io["ones_bf16"] = nc.dram_tensor("ones_bf16", [128, 128], BF16, kind="ExternalInput").ap()
    io["ones_f32"] = nc.dram_tensor("ones_f32", [128, 128], F32, kind="ExternalInput").ap()
    if DEBUG:
        io["dbg_gTb"] = nc.dram_tensor("dbg_gTb", [8, T], BF16, kind="ExternalOutput").ap()
        io["dbg_yacc"] = nc.dram_tensor("dbg_yacc", [128, NCT, T], F32, kind="ExternalOutput").ap()
        io["dbg_scores"] = nc.dram_tensor("dbg_scores", [128, 128], F32, kind="ExternalOutput").ap()
        io["dbg_w11"] = nc.dram_tensor("dbg_w11", [128, NCT, 1024], BF16, kind="ExternalOutput").ap()
        io["dbg_G1"] = nc.dram_tensor("dbg_G1", [128, NCT, CH], BF16, kind="ExternalOutput").ap()
        io["dbg_H1"] = nc.dram_tensor("dbg_H1", [128, CH], F32, kind="ExternalOutput").ap()
        io["dbg_H3"] = nc.dram_tensor("dbg_H3", [128, CH], F32, kind="ExternalOutput").ap()
        io["dbg_sil"] = nc.dram_tensor("dbg_sil", [128, CH], BF16, kind="ExternalOutput").ap()
    stack = ExitStack()
    P = Prog(nc)
    P.make_sems(stack)
    build_l1f(nc, P, stack, io)
    block = stack.enter_context(nc.Block())
    P.finalize(block)
    stack.close()
    return nc


def l1f_in_maps(inputs, x3_core):
    ident, ones_b, ones_f, maskT, maskLK, sel, eb = _host_consts()
    e8sel = np.zeros((8, 8 * 128), dtype=ml_dtypes.bfloat16)
    for e in range(8):
        e8sel[e, e * 128:(e + 1) * 128] = 1.0
    rt = np.asarray(inputs["L1_ffn_router_w"])
    return {
        "xin": x3_core,
        "router_t": np.ascontiguousarray(rt.reshape(NCT, 128, 8)),
        "ebias_b": np.ascontiguousarray(np.broadcast_to(
            np.asarray(inputs["L1_ffn_expert_bias"], np.float32)[None, :], (128, 8))),
        "w1": np.ascontiguousarray(np.asarray(inputs["L1_ffn_w1"]).reshape(8, NCT, 128, 1024)),
        "w3": np.ascontiguousarray(np.asarray(inputs["L1_ffn_w3"]).reshape(8, NCT, 128, 1024)),
        "w2": np.ascontiguousarray(np.asarray(inputs["L1_ffn_w2"]).reshape(8, NCT, 128, 1024)),
        "sw1": np.ascontiguousarray(np.asarray(inputs["L1_ffn_sw1"]).reshape(NCT, 128, 1024)),
        "sw3": np.ascontiguousarray(np.asarray(inputs["L1_ffn_sw3"]).reshape(NCT, 128, 1024)),
        "sw2": np.ascontiguousarray(np.asarray(inputs["L1_ffn_sw2"]).reshape(NCT, 128, 1024)),
        "e8sel": e8sel,
        "ident_f32": ident, "ones_bf16": ones_b, "ones_f32": ones_f,
    }


# version 08 (final)
# speedup vs baseline: 1.0276x; 1.0276x over previous
import numpy as np
import ml_dtypes
from contextlib import ExitStack

import concourse.bass as bass
import concourse.mybir as mybir
from concourse.bass_utils import run_bass_kernel_spmd

F32 = mybir.dt.float32
BF16 = mybir.dt.bfloat16
AF = mybir.ActivationFunctionType
ALU = mybir.AluOpType
AX = mybir.AxisListType

NCORES = 8
NSEQ = 4
T = 2048
D = 1024
NCT = 8
H = 16
HD = 64
CH = 512
NCHUNK = T // CH
NTILE = T // 128

ENGS = ("tensor", "vector", "scalar", "gpsimd", "sync")
DEBUG = False


class Prog:

    def __init__(self, nc):
        self.nc = nc
        self.q = {e: [] for e in ENGS}
        self.cn = {e: 0 for e in ENGS}
        self.dn = {e: 0 for e in ENGS}
        self.hw = {}
        self.sems = {}

    def make_sems(self, stack):
        for e in ENGS:
            self.sems[("c", e)] = stack.enter_context(self.nc.semaphore(f"c_{e}"))
            self.sems[("d", e)] = stack.enter_context(self.nc.semaphore(f"d_{e}"))

    def _wait(self, eng_name, tok):
        kind, peng, n = tok
        val = n if kind == "c" else 16 * n
        key = (eng_name, (kind, peng))
        if self.hw.get(key, -1) >= val:
            return
        self.hw[key] = val
        sem = self.sems[(kind, peng)]
        self.q[eng_name].append(lambda eng, sem=sem, val=val: eng.wait_ge(sem, val))

    def op(self, eng_name, fn, after=(), inc=False):
        for tok in after:
            if tok is not None:
                self._wait(eng_name, tok)
        if inc:
            self.cn[eng_name] += 1
            n = self.cn[eng_name]
            key = ("c", eng_name)

            def wrapped(eng, fn=fn, key=key):
                fn(eng).then_inc(self.sems[key], 1)
            self.q[eng_name].append(wrapped)
            return ("c", eng_name, n)
        self.q[eng_name].append(fn)
        return None

    def dma(self, eng_name, out_ap, in_ap, after=()):
        for tok in after:
            if tok is not None:
                self._wait(eng_name, tok)
        self.dn[eng_name] += 1
        n = self.dn[eng_name]
        key = ("d", eng_name)

        def wrapped(eng, out_ap=out_ap, in_ap=in_ap, key=key):
            eng.dma_start(out_ap, in_ap).then_inc(self.sems[key], 16)
        self.q[eng_name].append(wrapped)
        return ("d", eng_name, n)

    def barrier(self):
        toks = [("c", e, self.cn[e]) for e in ENGS if self.cn[e] > 0]
        dtoks = [("d", e, self.dn[e]) for e in ENGS if self.dn[e] > 0]
        for e in ENGS:
            for t in toks:
                if t[1] != e:
                    self._wait(e, t)
            for t in dtoks:
                self._wait(e, t)

    def finalize(self, block):
        for e in ENGS:
            ops = self.q[e]

            def runner(eng, ops=ops):
                for f in ops:
                    f(eng)
            getattr(block, e)(runner)



def build_l0a(nc, P, stack, io):
    E = stack.enter_context
    sb = lambda name, shape, dt=F32: E(nc.sbuf_tensor(name, shape, dt))

    wp = sb("l0a_wp", [128, NCT, D], BF16)
    wm = sb("l0a_wm", [128, NCT, D], BF16)
    metric = sb("l0a_metric", [128, 8, 64], BF16)
    ident = sb("sb_ident_f32", [128, 128], F32)
    ones_b = sb("sb_ones_bf16", [128, 128], BF16)
    ones_f = sb("sb_ones_f32", [128, 128], F32)
    maskT = sb("sb_maskT_f32", [128, 128], F32)
    masklk = sb("sb_masklk_f32", [64, 64], F32)
    sel = sb("sb_sel_bf16", [16, NCT * 128], BF16)
    eps_t = sb("sb_eps_t", [128, 1], F32)
    ln8_t = sb("sb_ln8_t", [128, 1], F32)
    xb = [sb(f"l0a_xb{i}", [128, NCT, CH], BF16) for i in range(2)]
    xf = [sb(f"l0a_xf{i}", [128, NCT, CH], F32) for i in range(2)]
    pfm = sb("l0a_pfm", [128, NCT, T], BF16)
    ptok = sb("l0a_ptok", [128, NTILE, H * 65], BF16)
    qm = sb("l0a_qm", [128, 8, CH], BF16)
    probs = [sb(f"l0a_pr{i}", [128, CH], BF16) for i in range(4)]
    oraw = sb("l0a_oraw", [128, NCT, CH], BF16)
    dend = sb("l0a_dend", [16, CH], F32)
    dendb = sb("l0a_dendb", [16, CH], BF16)
    negB = sb("l0a_negB", [128, 1], F32)
    tinyB = sb("l0a_tinyB", [2, 1], F32)
    tinyC = sb("l0a_tinyC", [1, 1], F32)
    pn2run = sb("l0a_pn2run", [2, 1], F32)
    eb = sb("sb_eb_bf16", [128, 2], BF16)
    msum = sb("l0a_msum", [128, 8], F32)
    ebf = sb("sb_eb_f32", [128, 2], F32)
    lnc = sb("l0a_lnc", [1, 1], F32)
    pmstat = sb("l0a_pmstat", [64, H], F32)
    negmax = sb("l0a_negmax", [64, H], F32)
    rb = [sb(f"l0a_rb{i}", [128, CH], BF16) for i in range(2)]
    r2b = [sb(f"l0a_r2b{i}", [128, CH], BF16) for i in range(2)]
    ssum = sb("l0a_ssum", [1, CH], F32)
    ssq = sb("l0a_ssq", [1, CH], F32)
    u1_t = sb("l0a_u1", [128, 2 * CH], F32)
    u2_t = sb("l0a_u2", [128, 2 * CH], F32)
    mb = u1_t[:, 0:CH]
    m2 = u1_t[:, CH:2 * CH]
    vb = u2_t[:, 0:CH]
    rstd = u2_t[:, CH:2 * CH]
    pnacc = u1_t[0:2, 0:CH]
    pm = u1_t[0:64, :].rearrange("p (h l) -> p h l", h=H, l=64)
    pmT = u2_t[0:64, :].rearrange("p (h l) -> p h l", h=H, l=64)
    xo_t = sb("l0a_xo", [128, NCT, CH], F32)
    xo = xo_t[:, :, :]
    den = xo_t[:, :, :].rearrange("p a b -> p (a b)")

    psA = [E(nc.psum_tensor(f"l0a_psA{i}", [128, CH], F32)) for i in range(2)]
    psS = [E(nc.psum_tensor(f"l0a_psS{i}", [128, CH], F32)) for i in range(2)]
    psO = [E(nc.psum_tensor(f"l0a_psO{i}", [65, CH], F32)) for i in range(2)]
    psE1 = E(nc.psum_tensor("l0a_psE1", [128, CH], F32))
    psE2 = E(nc.psum_tensor("l0a_psE2", [128, CH], F32))

    t_id = P.dma("sync", ident[:, :], io["ident_f32"])
    t_ob = P.dma("sync", ones_b[:, :], io["ones_bf16"])
    t_of = P.dma("sync", ones_f[:, :], io["ones_f32"])
    t_mk = P.dma("sync", maskT[:, :], io["maskT_f32"])
    t_ml = P.dma("sync", masklk[:, :], io["maskLK_f32"])
    t_se = P.dma("sync", sel[:, :], io["sel_bf16"])
    t_eb = P.dma("sync", eb[:, :], io["eb_bf16"])
    t_ebf = P.dma("sync", ebf[:, :], io["eb_f32"])
    t_pm = P.dma("sync", pm[:, :, :], io["pre_metric"])
    t_wp = P.dma("gpsimd", wp[:, :, :], io["wproj_t"].transpose([1, 0, 2]))
    t_wm = P.dma("gpsimd", wm[:, :, :], io["wmix_t"].transpose([1, 0, 2]))

    t_eps0 = P.op("vector", lambda v: v.memset(eps_t[:, :], 1e-5), inc=True)
    t_eps = P.op("vector", lambda v: v.memset(ln8_t[:, :], float(np.log(8.0))), inc=True)
    ones_ap = ptok[:, :, :].rearrange("p j (h c) -> p j h c", h=H, c=65)[:, :, :, 64:65]
    t_po = P.op("vector", lambda v: v.memset(ones_ap, 1.0), inc=True)

    t_add = []
    for h in range(H):
        t_add.append(P.op("vector", lambda v, h=h: v.tensor_add(
            pm[:, h, :], pm[:, h, :], masklk[:, :]), after=(t_pm, t_ml), inc=True))
    t1 = P.op("vector", lambda v: v.tensor_reduce(pmstat[:, :], pm[:, :, :], AX.X, ALU.max),
              after=t_add, inc=True)
    t2 = P.op("vector", lambda v: v.tensor_scalar(negmax[:, :], pmstat[:, :], -0.125, None, ALU.mult),
              after=(t1,), inc=True)
    t_exp = []
    for h in range(H):
        t_exp.append(P.op("scalar", lambda s, h=h: s.activation(
            pm[:, h, :], pm[:, h, :], AF.Exp, bias=negmax[:, h:h + 1], scale=0.125),
            after=(t2,), inc=True))
    t3 = P.op("vector", lambda v: v.tensor_reduce(pmstat[:, :], pm[:, :, :], AX.X, ALU.add),
              after=t_exp, inc=True)
    t4 = P.op("vector", lambda v: v.reciprocal(pmstat[:, :], pmstat[:, :]), after=(t3,), inc=True)
    t_nrm = []
    for h in range(H):
        t_nrm.append(P.op("vector", lambda v, h=h: v.tensor_scalar(
            pm[:, h, :], pm[:, h, :], pmstat[:, h:h + 1], None, ALU.mult),
            after=(t4,), inc=True))
    t_tp = []
    for h in range(H):
        tk = P.op("tensor", lambda t, h=h: t.transpose(
            psA[0][0:64, 0:64], pm[:, h, :], ident[0:64, 0:64]),
            after=(t_nrm[h], t_id, t_tp[h - 1] if h else None), inc=True)
        t_tp.append(P.op("vector", lambda v, h=h: v.tensor_copy(
            pmT[:, h, :], psA[0][0:64, 0:64]), after=(tk,), inc=True))
    t_met = []
    for h in range(H):
        base = (h % 2) * 64
        tk = P.op("tensor", lambda t, h=h: t.matmul(
            psA[1][0:64, 0:64], pmT[:, h, :], pmT[:, h, :], start=True, stop=True),
            after=(t_tp[h], t_met[h - 1] if h else None), inc=True)
        t_met.append(P.op("vector", lambda v, h=h, base=base: v.tensor_copy(
            metric[base:base + 64, h // 2, :], psA[1][0:64, 0:64]), after=(tk,), inc=True))

    mf1 = P.op("scalar", lambda s_: s_.activation(
        probs[0][:, :].rearrange("p (a b) -> p a b", a=8, b=64), metric[:, :, :], AF.Square),
        after=t_met, inc=True)
    mf2 = P.op("vector", lambda v: v.tensor_reduce(
        msum[:, :], probs[0][:, :].rearrange("p (a b) -> p a b", a=8, b=64),
        AX.X, ALU.add), after=(mf1,), inc=True)
    mf3 = P.op("tensor", lambda t: t.matmul(
        psE1[0:2, 0:8], ebf[:, :], msum[:, :], start=True, stop=True),
        after=(mf2, t_ebf), inc=True)
    mf4 = P.op("vector", lambda v: v.tensor_reduce(
        tinyB[:, :], psE1[0:2, 0:8], AX.X, ALU.max), after=(mf3,), inc=True)
    mf5 = P.op("gpsimd", lambda g: g.tensor_reduce(
        tinyC[:, :], tinyB[:, :], AX.C, ALU.max), after=(mf4,), inc=True)
    mf6 = P.op("scalar", lambda s_: s_.activation(tinyC[:, :], tinyC[:, :], AF.Ln),
               after=(mf5,), inc=True)
    mf7 = P.op("vector", lambda v: v.tensor_scalar(
        lnc[:, :], tinyC[:, :], 0.5, -float(np.log(8.0)), ALU.mult, ALU.add),
        after=(mf6,), inc=True)
    t_lnc = mf7
    fr_probs0_init = mf3

    fr_xb = [None, None]
    fr_xf = [None, None]
    fr_psA = [t_tp[-1], t_met[-1]]
    fr_psS = [None, None]
    fr_psSA = [None, None, None, None]
    fr_psO = [None, None]
    fr_psE1 = None
    fr_psE2 = None
    fr_probs = [fr_probs0_init, None, None, None]
    fr_den = None
    fr_dend = None
    fr_dendb = None
    fr_qm = None
    fr_oraw = None
    fr_xo = None
    fr_rb = [None, None]
    fr_r2b = [None, None]

    pend = None
    for s in range(NSEQ):
        for c in range(NCHUNK):
            buf = c % 2
            cb = c * CH
            t_xb = P.dma("gpsimd", xb[buf][:, :, :],
                         io["xin"][s, :, :, cb:cb + CH].transpose([1, 0, 2]),
                         after=(fr_xb[buf],))
            t_xf = P.dma("sync", xf[buf][:, :, :],
                         io["xin"][s, :, :, cb:cb + CH].transpose([1, 0, 2]),
                         after=(fr_xf[buf],))

            ev_fm = []
            for mt in range(NCT):
                slot = mt % 2
                dep = ev_fm[-2] if len(ev_fm) >= 2 else fr_psA[slot]
                for ci in range(NCT):
                    tk = P.op("tensor", lambda t, mt=mt, ci=ci, slot=slot, buf=buf: t.matmul(
                        psA[slot][:, :], wp[:, ci, mt * 128:(mt + 1) * 128],
                        xb[buf][:, ci, :], start=(ci == 0), stop=(ci == NCT - 1)),
                        after=(t_xb, t_wp, dep) if ci == 0 else (), inc=(ci == NCT - 1))
                ev_fm.append(P.op("scalar", lambda v, mt=mt, slot=slot, cb=cb: v.copy(
                    pfm[:, mt, cb:cb + CH], psA[slot][:, :]), after=(tk,), inc=True))

            ev_tok = []
            last_tokmm = None
            for tt4 in range(4):
                jt = c * 4 + tt4
                for nch in range(2):
                    i = tt4 * 2 + nch
                    slot = i % 2
                    dep = ev_tok[-2] if len(ev_tok) >= 2 else ev_fm[-2 + slot]
                    for ci in range(NCT):
                        tk = P.op("tensor", lambda t, tt4=tt4, nch=nch, ci=ci, slot=slot, buf=buf: t.matmul(
                            psA[slot][:, :], xb[buf][:, ci, tt4 * 128:(tt4 + 1) * 128],
                            wp[:, ci, nch * 512:(nch + 1) * 512],
                            start=(ci == 0), stop=(ci == NCT - 1)),
                            after=(dep,) if ci == 0 else (), inc=(ci == NCT - 1))
                    last_tokmm = tk
                    dst = ptok[:, jt, :].rearrange("p (h c) -> p h c", h=H, c=65)[
                        :, nch * 8:(nch + 1) * 8, 0:64]
                    src = psA[slot][:, :].rearrange("p (h k) -> p h k", h=8, k=64)
                    ev_tok.append(P.op("scalar", lambda v, dst=dst, src=src: v.copy(dst, src),
                                       after=(tk, t_po), inc=True))
            fr_xb[buf] = last_tokmm

            ev_qm = []
            for h in range(H):
                base = (h % 2) * 64
                slot = h % 2
                dep = ev_qm[-2] if len(ev_qm) >= 2 else ev_tok[-2 + slot]
                tk = P.op("tensor", lambda t, h=h, base=base, slot=slot, cb=cb: t.matmul(
                    psA[slot][base:base + 64, :], metric[base:base + 64, h // 2, :],
                    pfm[base:base + 64, h // 2, cb:cb + CH], start=True, stop=True),
                    after=(t_met[h], ev_fm[h // 2], dep, fr_qm), inc=True)
                ev_qm.append(P.op("scalar", lambda v, h=h, base=base, slot=slot: v.copy(
                    qm[base:base + 64, h // 2, :], psA[slot][base:base + 64, :]),
                    after=(tk,), inc=True))
            fr_psA[0] = ev_qm[-2]
            fr_psA[1] = ev_qm[-1]

            if pend is not None:
                pend()
                pend = None
            sq1 = P.op("scalar", lambda s_, cb=cb: s_.activation(
                oraw[:, :, :], pfm[:, :, cb:cb + CH], AF.Square),
                after=ev_fm + [fr_oraw], inc=True)
            nb_toks = []
            for blk in range(NCT):
                eslot = blk % 2
                psE = psE1 if eslot == 0 else psE2
                mmn = P.op("tensor", lambda t, blk=blk, psE=psE: t.matmul(
                    psE[0:2, :], eb[:, :], oraw[:, blk, :], start=True, stop=True),
                    after=(sq1, fr_psE1 if blk == 0 else None,
                           fr_psE2 if blk == 1 else None,
                           nb_toks[-2] if len(nb_toks) >= 2 else None), inc=True)
                if blk == 0:
                    nb_toks.append(P.op("vector", lambda v, psE=psE: v.tensor_copy(
                        pnacc[:, :], psE[0:2, :]), after=(mmn,), inc=True))
                else:
                    nb_toks.append(P.op("vector", lambda v, psE=psE: v.tensor_tensor(
                        pnacc[:, :], pnacc[:, :], psE[0:2, :], ALU.max),
                        after=(mmn,), inc=True))
            fr_psE1 = nb_toks[-2]
            fr_psE2 = nb_toks[-1]
            nb1 = P.op("vector", lambda v: v.tensor_reduce(
                tinyB[:, :], pnacc[:, :], AX.X, ALU.max), after=(nb_toks[-1],), inc=True)
            if c == 0:
                nb2 = P.op("vector", lambda v: v.tensor_copy(pn2run[:, :], tinyB[:, :]),
                           after=(nb1,), inc=True)
            else:
                nb2 = P.op("vector", lambda v: v.tensor_tensor(
                    pn2run[:, :], pn2run[:, :], tinyB[:, :], ALU.max),
                    after=(nb1,), inc=True)
            nb3 = P.op("gpsimd", lambda g: g.tensor_reduce(
                tinyC[:, :], pn2run[:, :], AX.C, ALU.max), after=(nb2,), inc=True)
            nb4 = P.op("scalar", lambda s_: s_.activation(
                tinyC[:, :], tinyC[:, :], AF.Ln), after=(nb3,), inc=True)
            nb5 = P.op("scalar", lambda s_: s_.activation(
                tinyC[:, :], tinyC[:, :], AF.Exp, bias=lnc[:, :]),
                after=(nb4, t_lnc), inc=True)
            nb6 = P.op("tensor", lambda t: t.matmul(
                psE2[:, 0:1], ones_f[0:1, :], tinyC[:, :], start=True, stop=True),
                after=(nb5, fr_psE2, t_of), inc=True)
            t_negB = P.op("vector", lambda v: v.tensor_scalar(
                negB[:, :], psE2[:, 0:1], -1.0, None, ALU.mult), after=(nb6,), inc=True)
            fr_psE2 = t_negB

            if DEBUG and s == 0 and c == 0:
                P.dma("sync", io["dbg_qm"], qm[:, :, :], after=ev_qm)
                P.dma("sync", io["dbg_negB"], negB[:, :], after=(t_negB,))
            psSS = [psS[0], psS[1], psA[0], psA[1]]
            fr_psSA = [fr_psS[0], fr_psS[1], ev_qm[-2], ev_qm[-1]]
            itc = 0
            ev_or = []
            dn_tok = []
            t_dd = []
            fr_den_gate = fr_den
            njt = 4 * c + 4
            for hp in range(H // 2):
                pair = (2 * hp, 2 * hp + 1)
                o_deps = {}
                for h in pair:
                    oslot = h % 2
                    if len(ev_or) >= 2:
                        o_deps[h] = (ev_or[-2 + oslot], dn_tok[-2 + oslot])
                    else:
                        o_deps[h] = fr_psO[oslot] or (None, None)
                avs = {pair[0]: None, pair[1]: None}
                pending = []

                def emit_av(args, o_deps=o_deps, avs=avs):
                    h, jt, off, ncols, pslot, tk_ex = args
                    oslot = h % 2
                    o_dep, o_dep2 = o_deps[h]
                    avs[h] = P.op("tensor", lambda t, h=h, jt=jt, off=off, ncols=ncols, pslot=pslot, oslot=oslot, njt=njt: t.matmul(
                        psO[oslot][:, off:off + ncols],
                        ptok[:, jt, h * 65:(h + 1) * 65],
                        probs[pslot][:, 0:ncols],
                        start=(jt == 0), stop=(jt == njt - 1)),
                        after=((tk_ex,) + ((ev_tok[-1], o_dep, o_dep2) if jt == 0 else ())),
                        inc=(jt == njt - 1))
                    fr_probs[pslot] = avs[h]

                for jt in range(njt):
                    off = max(0, jt * 128 - cb)
                    ncols = CH - off
                    exs = []
                    for h in pair:
                        base = (h % 2) * 64
                        sslot = itc % 4
                        itc += 1
                        tk_sc = P.op("tensor", lambda t, h=h, base=base, jt=jt, off=off, ncols=ncols, sslot=sslot: t.matmul(
                            psSS[sslot][:, off:off + ncols],
                            pfm[base:base + 64, h // 2, jt * 128:(jt + 1) * 128],
                            qm[base:base + 64, h // 2, off:off + ncols],
                            start=True, stop=True),
                            after=(fr_psSA[sslot], ev_qm[h]), inc=True)
                        if jt * 128 >= cb:
                            tk_sc = P.op("vector", lambda v, off=off, sslot=sslot: v.tensor_add(
                                psSS[sslot][:, off:off + 128], psSS[sslot][:, off:off + 128],
                                maskT[:, :]), after=(tk_sc, t_mk), inc=True)
                        exs.append((h, sslot, tk_sc))
                    for h, sslot, tk_sc in exs:
                        pslot = sslot
                        tk_ex = P.op("scalar", lambda s_, off=off, ncols=ncols, sslot=sslot, pslot=pslot: s_.activation(
                            probs[pslot][:, 0:ncols], psSS[sslot][:, off:off + ncols],
                            AF.Exp, bias=negB[:, :], scale=0.125),
                            after=(tk_sc, t_negB, fr_probs[pslot]), inc=True)
                        fr_psSA[sslot] = tk_ex
                        pending.append((h, jt, off, ncols, pslot, tk_ex))
                        if len(pending) > 3:
                            emit_av(pending.pop(0))
                while pending:
                    emit_av(pending.pop(0))
                for h in pair:
                    base = (h % 2) * 64
                    oslot = h % 2
                    half = h // 8
                    ev_or.append(P.op("vector", lambda v, h=h, base=base, oslot=oslot: v.tensor_copy(
                        oraw[base:base + 64, h // 2, :], psO[oslot][0:64, :]),
                        after=(avs[h], fr_oraw), inc=True))
                    dn_tok.append(P.op("scalar", lambda s_, h=h, oslot=oslot, half=half: s_.copy(
                        den[64:65][:, (h - half * 8) * CH:(h - half * 8 + 1) * CH],
                        psO[oslot][64:65, :]), after=(avs[h], fr_den_gate), inc=True))
                    if h == 7 or h == 15:
                        half_dma = P.dma("sync",
                                         dend[half * 8:(half + 1) * 8, :].unsqueeze(1),
                                         den[64:65][:, :].rearrange("one (hh t) -> one hh t", hh=8, t=CH),
                                         after=dn_tok[half * 8:(half + 1) * 8] + [fr_dend])
                        t_dd.append(half_dma)
                        fr_den_gate = half_dma
            fr_psO[0] = (ev_or[-2], dn_tok[-2])
            fr_psO[1] = (ev_or[-1], dn_tok[-1])
            fr_psA[0] = fr_psSA[2]
            fr_psA[1] = fr_psSA[3]
            fr_psS[0] = fr_psSA[0]
            fr_psS[1] = fr_psSA[1]

            t_rec = P.op("vector", lambda v: v.reciprocal(dend[:, :], dend[:, :]),
                         after=t_dd, inc=True)
            t_recb = P.op("vector", lambda v: v.tensor_copy(dendb[:, :], dend[:, :]),
                          after=(t_rec, fr_dendb), inc=True)
            if DEBUG and s == 0 and c == 0:
                P.dma("sync", io["dbg_dend"], dend[:, :], after=(t_recb,))
            fr_dend = t_recb

            def backB(s=s, c=c, buf=buf, cb=cb, t_xf=t_xf, t_recb=t_recb,
                      ev_or=ev_or, ev_fm=ev_fm, ev_tok=ev_tok, ev_qm=ev_qm):
                nonlocal fr_psS, fr_psE1, fr_psE2, fr_oraw, fr_xf, fr_xo, fr_den
                nonlocal fr_dendb, fr_rb, fr_r2b
                mul_toks = []
                res_toks = []
                scl_prev = [fr_psE1, fr_psE2]
                psE = [psE1, psE2]
                for ct in range(NCT):
                    eslot = ct % 2
                    t_sc = P.op("tensor", lambda t, ct=ct, eslot=eslot: t.matmul(
                        psE[eslot][:, :], sel[:, ct * 128:(ct + 1) * 128], dendb[:, :],
                        start=True, stop=True),
                        after=(t_recb, t_se, scl_prev[eslot],
                               mul_toks[-2] if len(mul_toks) >= 2 else None), inc=True)
                    scl_prev[eslot] = None
                    mul_toks.append(P.op("vector", lambda v, ct=ct, eslot=eslot: v.tensor_tensor(
                        oraw[:, ct, :], oraw[:, ct, :], psE[eslot][:, :], ALU.mult),
                        after=(t_sc, ev_or[-1]), inc=True))
                fr_dendb = t_sc
                fr_psE1 = mul_toks[-2]
                fr_psE2 = mul_toks[-1]

                if DEBUG and s == 0 and c == 0:
                    dbg_tok = P.dma("sync", io["dbg_oraw"], oraw[:, :, :], after=mul_toks)
                for mt in range(NCT):
                    slot = mt % 2
                    dep = res_toks[-2] if len(res_toks) >= 2 else fr_psS[slot]
                    for ci in range(NCT):
                        tk = P.op("tensor", lambda t, mt=mt, ci=ci, slot=slot: t.matmul(
                            psS[slot][:, :], wm[:, ci, mt * 128:(mt + 1) * 128],
                            oraw[:, ci, :], start=(ci == 0), stop=(ci == NCT - 1)),
                            after=(mul_toks[-1], t_wm, dep) if ci == 0 else (),
                            inc=(ci == NCT - 1))
                    res_toks.append(P.op("vector", lambda v, mt=mt, slot=slot, buf=buf: v.tensor_add(
                        xf[buf][:, mt, :], xf[buf][:, mt, :], psS[slot][:, :]),
                        after=(tk, t_xf), inc=True))
                if DEBUG and s == 0 and c == 0:
                    dbg_r_tok = P.dma("sync", io["dbg_r"], xf[buf][:, :, :], after=res_toks)
                    P.dma("sync", io["dbg_oraw2"], oraw[:, :, :], after=res_toks)
                    res_toks = res_toks + [dbg_r_tok]
                fr_oraw = res_toks[-1]
                fr_psS[0] = res_toks[-2]
                fr_psS[1] = res_toks[-1]

                st_toks = []
                sq_toks = []
                cast_toks = []
                for ct in range(NCT):
                    rs = ct % 2
                    t_rb = P.op("vector", lambda v, ct=ct, rs=rs, buf=buf: v.tensor_copy(
                        rb[rs][:, :], xf[buf][:, ct, :]),
                        after=(res_toks[ct], st_toks[-2] if len(st_toks) >= 2 else None,
                               fr_rb[rs]), inc=True)
                    t_r2 = P.op("scalar", lambda s_, ct=ct, rs=rs, buf=buf: s_.activation(
                        r2b[rs][:, :], xf[buf][:, ct, :], AF.Square),
                        after=(res_toks[ct], sq_toks[-2] if len(sq_toks) >= 2 else None,
                               fr_r2b[rs]), inc=True)
                    st_toks.append(P.op("tensor", lambda t, ct=ct, rs=rs: t.matmul(
                        psE1[0:1, :], ones_b[:, 0:1], rb[rs][:, :],
                        start=(ct == 0), stop=(ct == NCT - 1)),
                        after=(t_rb, t_ob) + ((fr_psE1,) if ct == 0 else ()), inc=True))
                    sq_toks.append(P.op("tensor", lambda t, ct=ct, rs=rs: t.matmul(
                        psE2[0:1, :], ones_b[:, 0:1], r2b[rs][:, :],
                        start=(ct == 0), stop=(ct == NCT - 1)),
                        after=(t_r2,) + ((fr_psE2,) if ct == 0 else ()), inc=True))
                    fr_rb[rs] = st_toks[-1]
                    fr_r2b[rs] = sq_toks[-1]
                e1 = P.op("vector", lambda v: v.tensor_copy(ssum[:, :], psE1[0:1, :]),
                          after=(st_toks[-1],), inc=True)
                e2 = P.op("vector", lambda v: v.tensor_copy(ssq[:, :], psE2[0:1, :]),
                          after=(sq_toks[-1],), inc=True)
                e3 = P.op("tensor", lambda t: t.matmul(
                    psE1[:, :], ones_f[0:1, :], ssum[:, :], start=True, stop=True),
                    after=(e1,), inc=True)
                e4 = P.op("tensor", lambda t: t.matmul(
                    psE2[:, :], ones_f[0:1, :], ssq[:, :], start=True, stop=True),
                    after=(e2,), inc=True)
                e5 = P.op("vector", lambda v: v.tensor_scalar(
                    mb, psE1[:, :], 1.0 / D, None, ALU.mult),
                    after=(e3, t_tp[-1], t_met[-1]), inc=True)
                fr_psE1 = e5
                e6 = P.op("scalar", lambda s_: s_.activation(m2, mb, AF.Square),
                          after=(e5,), inc=True)
                e7 = P.op("vector", lambda v: v.scalar_tensor_tensor(
                    vb, psE2[:, :], 1.0 / D, m2, ALU.mult, ALU.subtract),
                    after=(e4, e6, t_met[-1]), inc=True)
                fr_psE2 = e7
                e8 = P.op("scalar", lambda s_: s_.activation(vb, vb, AF.Ln,
                                                             bias=eps_t[:, :]),
                          after=(e7, t_eps0), inc=True)
                e9 = P.op("scalar", lambda s_: s_.activation(rstd, vb, AF.Exp,
                                                             scale=-0.5),
                          after=(e8,), inc=True)
                out_toks = []
                for ct in range(NCT):
                    u1 = P.op("vector", lambda v, ct=ct, buf=buf: v.tensor_sub(
                        xo[:, ct, :], xf[buf][:, ct, :], mb),
                        after=(e5, res_toks[ct], fr_xo), inc=True)
                    out_toks.append(P.op("vector", lambda v, ct=ct: v.tensor_mul(
                        xo[:, ct, :], xo[:, ct, :], rstd), after=(u1, e9), inc=True))
                fr_xf[buf] = out_toks[-1]
                t_out = P.dma("sync", io["xout"][s, :, :, cb:cb + CH].transpose([1, 0, 2]),
                              xo, after=out_toks)
                fr_xo = t_out
                fr_den = t_out
                if DEBUG and s == 0 and c == NCHUNK - 1:
                    P.dma("sync", io["dbg_pfm"], pfm[:, :, :], after=ev_fm)
                    P.dma("sync", io["dbg_ptok"], ptok[:, :, :], after=ev_tok)

            pend = backB
        pend()
        pend = None
    final_tok = ("d", "sync", P.dn["sync"])
    P.op("sync", lambda e: e.nop(), after=(final_tok,))
    return final_tok



def _host_consts():
    ident = np.eye(128, dtype=np.float32)
    ones_b = np.ones((128, 128), dtype=ml_dtypes.bfloat16)
    ones_f = np.ones((128, 128), dtype=np.float32)
    p = np.arange(128)
    maskT = np.where(p[:, None] <= p[None, :], 0.0, -1e9).astype(np.float32)
    k = np.arange(64)
    maskLK = np.where(k[None, :] <= k[:, None], 0.0, -1e9).astype(np.float32)
    sel = np.zeros((16, NCT * 128), dtype=ml_dtypes.bfloat16)
    for ct in range(NCT):
        for pp in range(128):
            sel[2 * ct + (pp >= 64), ct * 128 + pp] = 1.0
    eb = np.zeros((128, 2), dtype=ml_dtypes.bfloat16)
    eb[0:64, 0] = 1.0
    eb[64:128, 1] = 1.0
    return ident, ones_b, ones_f, maskT, maskLK, sel, eb


def _fm(x):
    B = x.shape[0]
    return np.ascontiguousarray(x.reshape(B, T, NCT, 128).transpose(0, 2, 3, 1))


def _fm_inv(x):
    B = x.shape[0]
    return np.ascontiguousarray(x.transpose(0, 3, 1, 2).reshape(B, T, D))


def _wT(w):
    return np.ascontiguousarray(w.T.reshape(w.shape[1] // 128, 128, w.shape[0]))


def build_l0a_program():
    nc = bass.Bass("TRN2", target_bir_lowering=False, debug=False)
    io = {}
    io["xin"] = nc.dram_tensor("xin", [NSEQ, NCT, 128, T], F32, kind="ExternalInput").ap()
    io["xout"] = nc.dram_tensor("xout", [NSEQ, NCT, 128, T], F32, kind="ExternalOutput").ap()
    io["wproj_t"] = nc.dram_tensor("wproj_t", [NCT, 128, D], F32, kind="ExternalInput").ap()
    io["wmix_t"] = nc.dram_tensor("wmix_t", [NCT, 128, D], F32, kind="ExternalInput").ap()
    io["pre_metric"] = nc.dram_tensor("pre_metric", [64, H, 64], F32, kind="ExternalInput").ap()
    io["ident_f32"] = nc.dram_tensor("ident_f32", [128, 128], F32, kind="ExternalInput").ap()
    io["ones_bf16"] = nc.dram_tensor("ones_bf16", [128, 128], BF16, kind="ExternalInput").ap()
    io["ones_f32"] = nc.dram_tensor("ones_f32", [128, 128], F32, kind="ExternalInput").ap()
    io["maskT_f32"] = nc.dram_tensor("maskT_f32", [128, 128], F32, kind="ExternalInput").ap()
    io["maskLK_f32"] = nc.dram_tensor("maskLK_f32", [64, 64], F32, kind="ExternalInput").ap()
    io["sel_bf16"] = nc.dram_tensor("sel_bf16", [16, NCT * 128], BF16, kind="ExternalInput").ap()
    io["eb_bf16"] = nc.dram_tensor("eb_bf16", [128, 2], BF16, kind="ExternalInput").ap()
    io["eb_f32"] = nc.dram_tensor("eb_f32", [128, 2], F32, kind="ExternalInput").ap()
    if DEBUG:
        io["dbg_pfm"] = nc.dram_tensor("dbg_pfm", [128, NCT, T], BF16, kind="ExternalOutput").ap()
        io["dbg_ptok"] = nc.dram_tensor("dbg_ptok", [128, NTILE, H * 65], BF16, kind="ExternalOutput").ap()
        io["dbg_qm"] = nc.dram_tensor("dbg_qm", [128, 8, CH], BF16, kind="ExternalOutput").ap()
        io["dbg_oraw"] = nc.dram_tensor("dbg_oraw", [128, NCT, CH], BF16, kind="ExternalOutput").ap()
        io["dbg_dend"] = nc.dram_tensor("dbg_dend", [16, CH], F32, kind="ExternalOutput").ap()
        io["dbg_negB"] = nc.dram_tensor("dbg_negB", [128, 1], F32, kind="ExternalOutput").ap()
        io["dbg_r"] = nc.dram_tensor("dbg_r", [128, NCT, CH], F32, kind="ExternalOutput").ap()
        io["dbg_oraw2"] = nc.dram_tensor("dbg_oraw2", [128, NCT, CH], BF16, kind="ExternalOutput").ap()

    stack = ExitStack()
    P = Prog(nc)
    P.make_sems(stack)
    build_l0a(nc, P, stack, io)
    block = stack.enter_context(nc.Block())
    P.finalize(block)
    stack.close()
    return nc


def l0a_in_maps(inputs, core):
    x = inputs["L0_attn_in_sequence_bwc"]
    ident, ones_b, ones_f, maskT, maskLK, sel, eb = _host_consts()
    m = {
        "xin": _fm(np.asarray(x[core * NSEQ:(core + 1) * NSEQ])),
        "wproj_t": _wT(np.asarray(inputs["L0_attn_W_proj"])),
        "wmix_t": _wT(np.asarray(inputs["L0_attn_W_mix"])),
        "pre_metric": np.ascontiguousarray(
            np.asarray(inputs["L0_attn_pre_metric"]).transpose(1, 0, 2)),
        "ident_f32": ident, "ones_bf16": ones_b, "ones_f32": ones_f,
        "maskT_f32": maskT, "maskLK_f32": maskLK, "sel_bf16": sel, "eb_bf16": eb, "eb_f32": eb.astype(np.float32),
    }
    return m


def build_fused_program():
    nc = bass.Bass("TRN2", target_bir_lowering=False, debug=False)
    AD = [NSEQ, NCT, 128, T]
    xin = nc.dram_tensor("xin", AD, F32, kind="ExternalInput").ap()
    x1 = nc.dram_tensor("x1_scratch", AD, F32, kind="Internal").ap()
    x2 = nc.dram_tensor("x2_scratch", AD, F32, kind="Internal").ap()
    x3 = nc.dram_tensor("x3_scratch", AD, F32, kind="Internal").ap()
    xout = nc.dram_tensor("xout", AD, F32, kind="ExternalOutput").ap()
    dt = lambda n, sh, d=F32: nc.dram_tensor(n, sh, d, kind="ExternalInput").ap()
    cm = {
        "ident_f32": dt("ident_f32", [128, 128]),
        "ones_bf16": dt("ones_bf16", [128, 128], BF16),
        "ones_f32": dt("ones_f32", [128, 128]),
        "eb_bf16": dt("eb_bf16", [128, 2], BF16),
    }
    io1 = dict(cm)
    io1.update(xin=xin, xout=x1,
               wproj_t=dt("wproj_t", [NCT, 128, D]),
               wmix_t=dt("wmix_t", [NCT, 128, D]),
               pre_metric=dt("pre_metric", [64, H, 64]),
               maskT_f32=dt("maskT_f32", [128, 128]),
               maskLK_f32=dt("maskLK_f32", [64, 64]),
               sel_bf16=dt("sel_bf16", [16, NCT * 128], BF16),
               eb_f32=dt("eb_f32", [128, 2]))
    io2 = dict(cm)
    io2.update(xin=x1, xout=x2,
               wfc_t=dt("wfc_t", [NCT, 128, 4096]),
               wproj2_t=dt("wproj2_t", [32, 128, 1024]),
               lnw=dt("lnw", [128, 32]), lnb=dt("lnb", [128, 32]))
    io3 = dict(cm)
    io3.update(xin=x2, xout=x3,
               conv3=dt("conv3", [NCT, 128, 3]),
               pe_t=dt("pe_t", [NCT, 128, D]),
               qkv_t=dt("qkv_t", [NCT, 128, 3072]),
               outw_t=dt("outw_t", [NCT, 128, D]),
               mask_strict=dt("mask_strict", [128, 128], BF16),
               mask_incl=dt("mask_incl", [128, 128], BF16),
               sel4=dt("sel4", [4, 2 * 128], BF16))
    io4 = dict(cm)
    io4.update(xin=x3, xout=xout,
               router_t=dt("router_t", [NCT, 128, 8]),
               ebias_b=dt("ebias_b", [128, 8]),
               w1=dt("w1", [8, NCT, 128, 1024]), w3=dt("w3", [8, NCT, 128, 1024]),
               w2=dt("w2", [8, NCT, 128, 1024]),
               sw1=dt("sw1", [NCT, 128, 1024]), sw3=dt("sw3", [NCT, 128, 1024]),
               sw2=dt("sw2", [NCT, 128, 1024]),
               e8sel=dt("e8sel", [8, 8 * 128], BF16))

    outer = ExitStack()
    P = Prog(nc)
    P.make_sems(outer)
    stages = ((build_l0a, io1), (build_l0f, io2), (build_l1a, io3), (build_l1f, io4))[:N_FUSE_STAGES]
    for build, io_ in stages:
        stage_stack = ExitStack()
        build(nc, P, stage_stack, io_)
        stage_stack.close()
        P.barrier()
    block = outer.enter_context(nc.Block())
    P.finalize(block)
    outer.close()
    return nc


def fused_in_maps(inputs, core):
    m = l0a_in_maps(inputs, core)
    m2 = l0f_in_maps(inputs, None); m2.pop("xin")
    m3 = l1a_in_maps(inputs, None); m3.pop("xin")
    m4 = l1f_in_maps(inputs, None); m4.pop("xin")
    for d in (m2, m3, m4):
        m.update(d)
    return m


FUSED = True
N_FUSE_STAGES = 4


def kernel(**inputs):
    cores = list(range(NCORES))
    if FUSED:
        nc = build_fused_program()
        res = run_bass_kernel_spmd(nc, [fused_in_maps(inputs, c) for c in cores],
                                   core_ids=cores)
        return np.concatenate([_fm_inv(r["xout"]) for r in res.results], axis=0)
    nc1 = build_l0a_program()
    res = run_bass_kernel_spmd(nc1, [l0a_in_maps(inputs, c) for c in cores], core_ids=cores)
    x = [r["xout"] for r in res.results]
    nc2 = build_l0f_program()
    res = run_bass_kernel_spmd(nc2, [l0f_in_maps(inputs, x[c]) for c in cores], core_ids=cores)
    x = [r["xout"] for r in res.results]
    nc3 = build_l1a_program()
    res = run_bass_kernel_spmd(nc3, [l1a_in_maps(inputs, x[c]) for c in cores], core_ids=cores)
    x = [r["xout"] for r in res.results]
    nc4 = build_l1f_program()
    res = run_bass_kernel_spmd(nc4, [l1f_in_maps(inputs, x[c]) for c in cores], core_ids=cores)
    return np.concatenate([_fm_inv(r["xout"]) for r in res.results], axis=0)



def build_l0f(nc, P, stack, io):
    E = stack.enter_context
    sb = lambda name, shape, dt=F32: E(nc.sbuf_tensor(name, shape, dt))
    NH = 32

    wq1 = sb("l0f_wq1", [128, NCT, 4096], BF16)
    wq2 = sb("l0f_wq2", [128, NH, 1024], BF16)
    hb = sb("l0f_hb", [128, NH, CH], BF16)
    xb_t = sb("l0f_xb0", [128, NCT, CH], BF16)
    xf_t = sb("l0f_xf", [128, NCT, CH], F32)
    xf = xf_t[:, :, :]
    ones_b = sb("l0f_ones_b", [128, 128], BF16)
    ones_f = sb("l0f_ones_f", [128, 128], F32)
    eps_t = sb("l0f_eps", [128, 1], F32)
    lnw = sb("l0f_lnw", [128, NH], F32)
    lnb = sb("l0f_lnb", [128, NH], F32)
    absacc = sb("l0f_absacc", [128, 2], F32)
    thr = sb("l0f_thr", [128, 2], F32)
    red_t = sb("l0f_red", [128, 1], F32)
    tiny = sb("l0f_tiny", [1, 1], F32)
    u1_t = sb("l0f_u1", [128, 2 * CH], F32)
    u2_t = sb("l0f_u2", [128, 2 * CH], F32)
    mb, m2 = u1_t[:, 0:CH], u1_t[:, CH:2 * CH]
    vb, rstd = u2_t[:, 0:CH], u2_t[:, CH:2 * CH]
    ssum = sb("l0f_ssum", [1, CH], F32)
    ssq = sb("l0f_ssq", [1, CH], F32)
    rb = [sb(f"l0f_rb{i}", [128, CH], BF16) for i in range(2)]
    r2b = [sb(f"l0f_r2b{i}", [128, CH], BF16) for i in range(2)]
    tmp1 = sb("l0f_tmp1", [128, CH], F32)
    tmp2 = sb("l0f_tmp2", [128, CH], F32)

    wld = [hb[:, :, :].bitcast(F32).rearrange("p a b -> p (a b)")[:, i * 4096:(i + 1) * 4096]
           for i in range(2)]
    abs_t = xf_t[:, :, :].rearrange("p a b -> p (a b)")
    sgn_t = xb_t[:, :, :].rearrange("p a b -> p (a b)")

    psA = [E(nc.psum_tensor(f"l0f_psA{i}", [128, CH], F32)) for i in range(2)]
    psB = [E(nc.psum_tensor(f"l0f_psB{i}", [128, CH], F32)) for i in range(4)]
    psE1 = E(nc.psum_tensor("l0f_psE1", [128, CH], F32))
    psE2 = E(nc.psum_tensor("l0f_psE2", [128, CH], F32))

    t_ob = P.dma("sync", ones_b[:, :], io["ones_bf16"])
    t_of = P.dma("sync", ones_f[:, :], io["ones_f32"])
    t_lw = P.dma("sync", lnw[:, :], io["lnw"])
    t_lb = P.dma("sync", lnb[:, :], io["lnb"])
    t_eps0 = P.op("vector", lambda v: v.memset(eps_t[:, :], 1e-5), inc=True)
    t_aa = P.op("vector", lambda v: v.memset(absacc[:, :], 0.0), inc=True)

    tile_srcs = []
    for ci in range(NCT):
        tile_srcs.append((0, io["wfc_t"][ci], wq1[:, ci, :], 4096))
    for g in range(8):
        src = io["wproj2_t"][g * 4:(g + 1) * 4].transpose([1, 0, 2])
        dst = wq2[:, g * 4:(g + 1) * 4, :].rearrange("p a b -> p (a b)")
        tile_srcs.append((1, src, dst, 4096))

    acc_tok = [t_aa, t_aa]
    ld_prev = [None, None]
    for i, (mi, src, dst, ncols) in enumerate(tile_srcs):
        slot = i % 2
        t_ld = P.dma("sync", wld[slot][:, 0:ncols], src, after=(ld_prev[slot],))
        t_red = P.op("vector", lambda v, slot=slot, ncols=ncols: v.tensor_reduce(
            red_t[:, :], wld[slot][:, 0:ncols].rearrange("p (a b) -> p a b", b=512),
            AX.XY, ALU.add, apply_absolute_value=True), after=(t_ld, acc_tok[0]), inc=True)
        acc_tok[0] = P.op("vector", lambda v, mi=mi: v.tensor_add(
            absacc[:, mi:mi + 1], absacc[:, mi:mi + 1], red_t[:, :]),
            after=(t_red,), inc=True)
        ld_prev[slot] = t_red
    thr_toks = []
    for mi in range(2):
        m1 = P.op("tensor", lambda t, mi=mi: t.matmul(
            psE1[0:1, 0:1], ones_f[:, 0:1], absacc[:, mi:mi + 1], start=True, stop=True),
            after=(acc_tok[0], t_of, thr_toks[-1] if thr_toks else None), inc=True)
        m2_ = P.op("vector", lambda v, mi=mi: v.tensor_scalar(
            tiny[:, :], psE1[0:1, 0:1], 0.5 / (4096 * 1024), 0.5e-6, ALU.mult, ALU.add),
            after=(m1,), inc=True)
        m3 = P.op("tensor", lambda t, mi=mi: t.matmul(
            psE2[:, 0:1], ones_f[0:1, :], tiny[:, :], start=True, stop=True),
            after=(m2_,), inc=True)
        thr_toks.append(P.op("vector", lambda v, mi=mi: v.tensor_copy(
            thr[:, mi:mi + 1], psE2[:, 0:1]), after=(m3,), inc=True))

    qz_prev = [None, None]
    last_q = None
    for i, (mi, src, dst, ncols) in enumerate(tile_srcs):
        slot = i % 2
        t_ld = P.dma("sync", wld[slot][:, 0:ncols], src, after=(qz_prev[slot], thr_toks[-1]))
        t_ab = P.op("scalar", lambda s_, slot=slot, ncols=ncols: s_.activation(
            abs_t[:, 0:ncols], wld[slot][:, 0:ncols], AF.Abs), after=(t_ld, last_q), inc=True)
        t_sg = P.op("scalar", lambda s_, slot=slot, ncols=ncols: s_.sign(
            sgn_t[:, 0:ncols], wld[slot][:, 0:ncols]), after=(t_ab,), inc=True)
        last_q = P.op("vector", lambda v, mi=mi, dst=dst, ncols=ncols: v.scalar_tensor_tensor(
            dst, abs_t[:, 0:ncols], thr[:, mi:mi + 1], sgn_t[:, 0:ncols],
            ALU.is_ge, ALU.mult), after=(t_sg,), inc=True)
        qz_prev[slot] = t_sg
    t_wq = last_q
    if DEBUG:
        P.dma("sync", io["dbg_wq1"], wq1[:, :, :], after=(t_wq,))
        P.dma("sync", io["dbg_wq2"], wq2[:, :, :], after=(t_wq,))
        P.dma("sync", io["dbg_thr"], thr[:, :], after=(t_wq,))

    fr_xb0 = None
    fr_xf = None
    fr_psA = [None, None]
    fr_psB = [None, None, None, None]
    fr_psH = [None, None, None, None]
    fr_psE1 = thr_toks[-1]
    fr_psE2 = thr_toks[-1]
    fr_rb = [None, None]
    fr_r2b = [None, None]
    fr_hb = t_wq
    fr_tmp = None
    for s in range(NSEQ):
        for c in range(NCHUNK):
            buf = c % 2
            cb = c * CH
            t_xb = P.dma("gpsimd", xb_t[:, :, :],
                         io["xin"][s, :, :, cb:cb + CH].transpose([1, 0, 2]),
                         after=(fr_xb0, t_wq if s == 0 and c == 0 else None))
            t_xf = P.dma("sync", xf_t[:, :, :],
                         io["xin"][s, :, :, cb:cb + CH].transpose([1, 0, 2]),
                         after=(fr_xf, t_wq if (s == 0 and c == 0) else None))

            gel_toks = []
            st_toks = []
            sq_toks = []
            psH = [psA[0], psA[1], psB[0], psB[1]]
            for ht in range(NH):
                slot = ht % 4
                dep = gel_toks[-4] if len(gel_toks) >= 4 else fr_psH[slot]
                for ci in range(NCT):
                    tk = P.op("tensor", lambda t, ht=ht, ci=ci, slot=slot: t.matmul(
                        psH[slot][:, :], wq1[:, ci, ht * 128:(ht + 1) * 128],
                        xb_t[:, ci, :], start=(ci == 0), stop=(ci == NCT - 1)),
                        after=(t_xb, dep) if ci == 0 else (), inc=(ci == NCT - 1))
                g1 = P.op("scalar", lambda s_, ht=ht, slot=slot: s_.activation(
                    hb[:, ht, :], psH[slot][:, :], AF.Gelu),
                    after=(tk, fr_hb), inc=True)
                gel_toks.append(g1)
                rs = ht % 2
                t_r2 = P.op("scalar", lambda s_, ht=ht, rs=rs: s_.activation(
                    r2b[rs][:, :], hb[:, ht, :], AF.Square),
                    after=(g1, fr_r2b[rs]), inc=True)
                st_toks.append(P.op("tensor", lambda t, ht=ht: t.matmul(
                    psE1[0:1, :], ones_b[:, 0:1], hb[:, ht, :],
                    start=(ht == 0), stop=(ht == NH - 1)),
                    after=(g1, t_ob) + ((fr_psE1,) if ht == 0 else ()), inc=True))
                sq_toks.append(P.op("tensor", lambda t, ht=ht, rs=rs: t.matmul(
                    psE2[0:1, :], ones_b[:, 0:1], r2b[rs][:, :],
                    start=(ht == 0), stop=(ht == NH - 1)),
                    after=(t_r2,) + ((fr_psE2,) if ht == 0 else ()), inc=True))
                fr_r2b[rs] = sq_toks[-1]
            fr_xb0 = tk

            if DEBUG and s == 0 and c == 0:
                P.dma("sync", io["dbg_hb"], hb[:, :, :], after=gel_toks)
            e1 = P.op("vector", lambda v: v.tensor_copy(ssum[:, :], psE1[0:1, :]),
                      after=(st_toks[-1],), inc=True)
            e2 = P.op("vector", lambda v: v.tensor_copy(ssq[:, :], psE2[0:1, :]),
                      after=(sq_toks[-1],), inc=True)
            e3 = P.op("tensor", lambda t: t.matmul(
                psE1[:, :], ones_f[0:1, :], ssum[:, :], start=True, stop=True),
                after=(e1,), inc=True)
            e4 = P.op("tensor", lambda t: t.matmul(
                psE2[:, :], ones_f[0:1, :], ssq[:, :], start=True, stop=True),
                after=(e2,), inc=True)
            e5 = P.op("vector", lambda v: v.tensor_scalar(
                mb, psE1[:, :], 1.0 / 4096, None, ALU.mult), after=(e3,), inc=True)
            fr_psE1 = e5
            e6 = P.op("scalar", lambda s_: s_.activation(m2, mb, AF.Square),
                      after=(e5,), inc=True)
            e7 = P.op("vector", lambda v: v.scalar_tensor_tensor(
                vb, psE2[:, :], 1.0 / 4096, m2, ALU.mult, ALU.subtract),
                after=(e4, e6), inc=True)
            fr_psE2 = e7
            e8 = P.op("scalar", lambda s_: s_.activation(vb, vb, AF.Ln, bias=eps_t[:, :]),
                      after=(e7, t_eps0), inc=True)
            e9 = P.op("scalar", lambda s_: s_.activation(rstd, vb, AF.Exp, scale=-0.5),
                      after=(e8,), inc=True)

            nrm_toks = []
            res_toks = []
            sweep_mm_last = [None, None, None, None]
            for half in range(2):
                for ht in range(NH):
                    if half == 0:
                        ts_ = ht % 2
                        tmp = (tmp1 if ts_ == 0 else tmp2)[:, :]
                        n1 = P.op("vector", lambda v, ht=ht, tmp=tmp: v.tensor_sub(
                            tmp, hb[:, ht, :], mb),
                            after=(e5, gel_toks[ht], sq_toks[ht],
                                   nrm_toks[-2] if len(nrm_toks) >= 2 else fr_tmp), inc=True)
                        n2 = P.op("vector", lambda v, tmp=tmp: v.tensor_mul(tmp, tmp, rstd),
                                  after=(n1, e9), inc=True)
                        nrm_toks.append(P.op("vector", lambda v, ht=ht, tmp=tmp: v.tensor_scalar(
                            hb[:, ht, :], tmp, lnw[:, ht:ht + 1], lnb[:, ht:ht + 1],
                            ALU.mult, ALU.add), after=(n2, t_lw, t_lb), inc=True))
                    for dti in range(4):
                        dt = half * 4 + dti
                        tk = P.op("tensor", lambda t, dt=dt, ht=ht, dti=dti: t.matmul(
                            psB[dti][:, :], wq2[:, ht, dt * 128:(dt + 1) * 128],
                            hb[:, ht, :], start=(ht == 0), stop=(ht == NH - 1)),
                            after=(nrm_toks[ht],
                                   res_toks[dti] if half == 1 else
                                   (gel_toks[-2 + dti] if dti < 2 else fr_psB[dti]))
                                  if ht == 0 else
                                  ((nrm_toks[ht],) if half == 0 else ()),
                            inc=(ht == NH - 1))
                        sweep_mm_last[dti] = tk
                for dti in range(4):
                    dt = half * 4 + dti
                    tok_r = P.op("vector", lambda v, dt=dt, dti=dti: v.tensor_add(
                        xf[:, dt, :], xf[:, dt, :], psB[dti][:, :]),
                        after=(sweep_mm_last[dti], t_xf), inc=True)
                    if half == 0:
                        res_toks.append(tok_r)
                    else:
                        res_toks.append(tok_r)
            fr_tmp = nrm_toks[-1]
            for dti in range(4):
                fr_psB[dti] = res_toks[4 + dti]
            fr_psH[0] = gel_toks[-4]
            fr_psH[1] = gel_toks[-3]
            fr_psH[2] = res_toks[4]
            fr_psH[3] = res_toks[5]
            fr_hb = tk

            st2 = []
            sq2 = []
            for ct in range(NCT):
                rs = ct % 2
                t_rb = P.op("vector", lambda v, ct=ct, rs=rs: v.tensor_copy(
                    rb[rs][:, :], xf[:, ct, :]),
                    after=(res_toks[ct], st2[-2] if len(st2) >= 2 else None,
                           fr_rb[rs]), inc=True)
                t_r2 = P.op("scalar", lambda s_, ct=ct, rs=rs: s_.activation(
                    r2b[rs][:, :], xf[:, ct, :], AF.Square),
                    after=(res_toks[ct], sq2[-2] if len(sq2) >= 2 else None,
                           fr_r2b[rs]), inc=True)
                st2.append(P.op("tensor", lambda t, ct=ct, rs=rs: t.matmul(
                    psE1[0:1, :], ones_b[:, 0:1], rb[rs][:, :],
                    start=(ct == 0), stop=(ct == NCT - 1)),
                    after=(t_rb,) + ((fr_psE1,) if ct == 0 else ()), inc=True))
                sq2.append(P.op("tensor", lambda t, ct=ct, rs=rs: t.matmul(
                    psE2[0:1, :], ones_b[:, 0:1], r2b[rs][:, :],
                    start=(ct == 0), stop=(ct == NCT - 1)),
                    after=(t_r2,) + ((fr_psE2,) if ct == 0 else ()), inc=True))
                fr_rb[rs] = st2[-1]
                fr_r2b[rs] = sq2[-1]
            f1 = P.op("vector", lambda v: v.tensor_copy(ssum[:, :], psE1[0:1, :]),
                      after=(st2[-1],), inc=True)
            f2 = P.op("vector", lambda v: v.tensor_copy(ssq[:, :], psE2[0:1, :]),
                      after=(sq2[-1],), inc=True)
            f3 = P.op("tensor", lambda t: t.matmul(
                psE1[:, :], ones_f[0:1, :], ssum[:, :], start=True, stop=True),
                after=(f1,), inc=True)
            f4 = P.op("tensor", lambda t: t.matmul(
                psE2[:, :], ones_f[0:1, :], ssq[:, :], start=True, stop=True),
                after=(f2,), inc=True)
            f5 = P.op("vector", lambda v: v.tensor_scalar(
                mb, psE1[:, :], 1.0 / D, None, ALU.mult), after=(f3,), inc=True)
            fr_psE1 = f5
            f6 = P.op("scalar", lambda s_: s_.activation(m2, mb, AF.Square),
                      after=(f5,), inc=True)
            f7 = P.op("vector", lambda v: v.scalar_tensor_tensor(
                vb, psE2[:, :], 1.0 / D, m2, ALU.mult, ALU.subtract),
                after=(f4, f6), inc=True)
            fr_psE2 = f7
            f8 = P.op("scalar", lambda s_: s_.activation(vb, vb, AF.Ln, bias=eps_t[:, :]),
                      after=(f7,), inc=True)
            f9 = P.op("scalar", lambda s_: s_.activation(rstd, vb, AF.Exp, scale=-0.5),
                      after=(f8,), inc=True)
            out_toks = []
            for ct in range(NCT):
                u1 = P.op("vector", lambda v, ct=ct: v.tensor_sub(
                    xf[:, ct, :], xf[:, ct, :], mb), after=(f5, res_toks[ct]), inc=True)
                out_toks.append(P.op("vector", lambda v, ct=ct: v.tensor_mul(
                    xf[:, ct, :], xf[:, ct, :], rstd), after=(u1, f9), inc=True))
            t_out = P.dma("sync", io["xout"][s, :, :, cb:cb + CH].transpose([1, 0, 2]),
                          xf, after=out_toks)
            fr_xf = t_out
    final_tok = ("d", "sync", P.dn["sync"])
    P.op("sync", lambda e: e.nop(), after=(final_tok,))
    return final_tok


def build_l0f_program():
    nc = bass.Bass("TRN2", target_bir_lowering=False, debug=False)
    io = {}
    io["xin"] = nc.dram_tensor("xin", [NSEQ, NCT, 128, T], F32, kind="ExternalInput").ap()
    io["xout"] = nc.dram_tensor("xout", [NSEQ, NCT, 128, T], F32, kind="ExternalOutput").ap()
    io["wfc_t"] = nc.dram_tensor("wfc_t", [NCT, 128, 4096], F32, kind="ExternalInput").ap()
    io["wproj2_t"] = nc.dram_tensor("wproj2_t", [32, 128, 1024], F32, kind="ExternalInput").ap()
    io["lnw"] = nc.dram_tensor("lnw", [128, 32], F32, kind="ExternalInput").ap()
    io["lnb"] = nc.dram_tensor("lnb", [128, 32], F32, kind="ExternalInput").ap()
    io["ones_bf16"] = nc.dram_tensor("ones_bf16", [128, 128], BF16, kind="ExternalInput").ap()
    io["ones_f32"] = nc.dram_tensor("ones_f32", [128, 128], F32, kind="ExternalInput").ap()
    if DEBUG:
        io["dbg_wq1"] = nc.dram_tensor("dbg_wq1", [128, NCT, 4096], BF16, kind="ExternalOutput").ap()
        io["dbg_wq2"] = nc.dram_tensor("dbg_wq2", [128, 32, 1024], BF16, kind="ExternalOutput").ap()
        io["dbg_thr"] = nc.dram_tensor("dbg_thr", [128, 2], F32, kind="ExternalOutput").ap()
        io["dbg_hb"] = nc.dram_tensor("dbg_hb", [128, 32, CH], BF16, kind="ExternalOutput").ap()
        io["dbg_hn"] = nc.dram_tensor("dbg_hn", [128, 32, CH], BF16, kind="ExternalOutput").ap()
    stack = ExitStack()
    P = Prog(nc)
    P.make_sems(stack)
    build_l0f(nc, P, stack, io)
    block = stack.enter_context(nc.Block())
    P.finalize(block)
    stack.close()
    return nc


def l0f_in_maps(inputs, x1_core):
    ident, ones_b, ones_f, maskT, maskLK, sel, eb = _host_consts()
    wfc = np.asarray(inputs["L0_ffn_W_fc"])
    wpr = np.asarray(inputs["L0_ffn_W_proj"])
    return {
        "xin": x1_core,
        "wfc_t": np.ascontiguousarray(wfc.T.reshape(NCT, 128, 4096)),
        "wproj2_t": np.ascontiguousarray(wpr.T.reshape(32, 128, 1024)),
        "lnw": np.ascontiguousarray(np.asarray(inputs["L0_ffn_ln_w"]).reshape(32, 128).T),
        "lnb": np.ascontiguousarray(np.asarray(inputs["L0_ffn_ln_b"]).reshape(32, 128).T),
        "ones_bf16": ones_b, "ones_f32": ones_f,
    }



def build_l1a(nc, P, stack, io):
    E = stack.enter_context
    sb = lambda name, shape, dt=F32: E(nc.sbuf_tensor(name, shape, dt))

    qkvT = sb("l1a_qkvT", [128, NCT, 3072], BF16)
    peT = sb("l1a_peT", [128, NCT, D], BF16)
    cw = sb("l1a_cw", [128, NCT, 3], F32)
    ident = sb("l1a_ident", [128, 128], F32)
    ones_b = sb("l1a_ones_b", [128, 128], BF16)
    ones_f = sb("l1a_ones_f", [128, 128], F32)
    m_strict = sb("l1a_mstrict", [128, 128], BF16)
    m_incl = sb("l1a_mincl", [128, 128], BF16)
    sel4 = sb("l1a_sel4", [4, 2 * 128], BF16)
    eb = sb("l1a_eb", [128, 2], BF16)
    eps_t = sb("l1a_eps", [128, 1], F32)
    ln8_t = sb("l1a_ln8", [128, 1], F32)
    xin_t = sb("l1a_xin", [128, NCT, CH + 2], F32)
    xcb = sb("l1a_xcb", [128, NCT, CH], BF16)
    xs = sb("l1a_xs", [128, NCT, CH], BF16)
    outwT = [xcb[:, :, :].rearrange("p a b -> p (a b)").rearrange("p (a b) -> p a b", a=4, b=1024),
             xs[:, :, :].rearrange("p a b -> p (a b)").rearrange("p (a b) -> p a b", a=4, b=1024)]
    qf = sb("l1a_qf", [128, NCT, CH], BF16)
    kf = sb("l1a_kf", [128, NCT, CH], BF16)
    qsh = sb("l1a_qsh", [128, NCT, CH], BF16)
    ksh = sb("l1a_ksh", [128, NCT, CH], BF16)
    scr8 = sb("l1a_scr8", [128, NCT, CH], BF16)
    qv = scr8[:, :, :].rearrange("p a b -> p (a b)")[:, 0:2048].rearrange(
        "p (a b) -> p a b", a=16, b=128)
    kv = scr8[:, :, :].rearrange("p a b -> p (a b)")[:, 2048:4096].rearrange(
        "p (a b) -> p a b", a=16, b=128)
    vtok = sb("l1a_vtok", [128, 4, H * 65], BF16)
    o_buf = sb("l1a_obuf", [128, NCT, T], BF16)
    probs = [sb(f"l1a_pr{i}", [128, CH], BF16) for i in range(4)]
    u_t = sb("l1a_u", [128, 4 * CH], F32)
    mb, m2 = u_t[:, 0:CH], u_t[:, CH:2 * CH]
    vb, rstd = u_t[:, 2 * CH:3 * CH], u_t[:, 3 * CH:4 * CH]
    den = u_t[:, :]
    pnacc = u_t[0:2, 0:CH]
    dend = u_t[0:4, :]
    dendb = sb("l1a_dendb", [4, 4 * CH], BF16)
    ssum = dend[0:1, 0:CH]
    ssq = dend[0:1, CH:2 * CH]
    rb = sb("l1a_rb", [128, CH], BF16)
    r2b = sb("l1a_r2b", [128, CH], BF16)
    tmp0 = sb("l1a_tmp0", [128, CH], F32)
    tinyB = sb("l1a_tinyB", [2, 1], F32)
    tinyC = sb("l1a_tinyC", [1, 1], F32)
    tinyC2 = sb("l1a_tinyC2", [1, 1], F32)
    negB = sb("l1a_negB", [128, 1], F32)

    psA = [E(nc.psum_tensor(f"l1a_psA{i}", [128, CH], F32)) for i in range(2)]
    psS = [E(nc.psum_tensor(f"l1a_psS{i}", [128, CH], F32)) for i in range(2)]
    psO = [E(nc.psum_tensor(f"l1a_psO{i}", [65, CH], F32)) for i in range(2)]
    psE1 = E(nc.psum_tensor("l1a_psE1", [128, CH], F32))
    psE2 = E(nc.psum_tensor("l1a_psE2", [128, CH], F32))

    t_id = P.dma("sync", ident[:, :], io["ident_f32"])
    t_ob = P.dma("sync", ones_b[:, :], io["ones_bf16"])
    t_of = P.dma("sync", ones_f[:, :], io["ones_f32"])
    t_ms = P.dma("sync", m_strict[:, :], io["mask_strict"])
    t_mi = P.dma("sync", m_incl[:, :], io["mask_incl"])
    t_s4 = P.dma("sync", sel4[:, :], io["sel4"])
    t_eb = P.dma("sync", eb[:, :], io["eb_bf16"])
    t_cw = P.dma("sync", cw[:, :, :], io["conv3"].transpose([1, 0, 2]))
    t_qk = P.dma("gpsimd", qkvT[:, :, :], io["qkv_t"].transpose([1, 0, 2]))
    t_pe = P.dma("gpsimd", peT[:, :, :], io["pe_t"].transpose([1, 0, 2]))
    t_eps0 = P.op("vector", lambda v: v.memset(eps_t[:, :], 1e-5), inc=True)
    t_ln8 = P.op("vector", lambda v: v.memset(ln8_t[:, :], float(np.log(8.0))), inc=True)
    vones_ap = vtok[:, :, :].rearrange("p j (h c) -> p j h c", h=H, c=65)[:, :, :, 64:65]
    t_vo = P.op("vector", lambda v: v.memset(vones_ap, 1.0), inc=True)

    fr_xin = None
    fr_psA = [None, None]
    fr_psS = [None, None]
    fr_psO = [None, None]
    fr_psE1 = None
    fr_psE2 = None
    fr_probs = [None, None, None, None]
    fr_den = None
    fr_dend = None
    fr_dendb = None
    fr_obuf = None
    fr_xcb = None
    fr_xs = None
    fr_qf = None
    fr_scr8 = None
    fr_vtok = None
    fr_outw = None
    fr_rb = None
    fr_r2b = None
    fr_u_ph2 = None
    for s in range(NSEQ):
        den_dmas = []
        scale_toks_all = []
        for c in range(NCHUNK):
            cb = c * CH
            lo = max(0, cb - 1)
            hi = min(T, cb + CH + 1)
            dlo = 1 - (cb - lo)
            t_xi = P.dma("sync", xin_t[:, :, dlo:dlo + (hi - lo)],
                         io["xin"][s, :, :, lo:hi].transpose([1, 0, 2]),
                         after=(fr_xin,))
            pads = []
            if c == 0:
                pads.append(P.op("vector", lambda v: v.memset(xin_t[:, :, 0:1], 0.0),
                                 after=(fr_xin,), inc=True))
            if c == NCHUNK - 1:
                pads.append(P.op("vector", lambda v: v.memset(
                    xin_t[:, :, CH + 1:CH + 2], 0.0), after=(fr_xin,), inc=True))

            cv_toks = []
            for ct in range(NCT):
                c1 = P.op("vector", lambda v, ct=ct: v.tensor_scalar(
                    tmp0[:, :], xin_t[:, ct, 0:CH], cw[:, ct, 0:1], None, ALU.mult),
                    after=(t_xi, t_cw, fr_outw,
                           cv_toks[-1] if cv_toks else fr_xcb) + tuple(pads),
                    inc=True)
                c2 = P.op("vector", lambda v, ct=ct: v.scalar_tensor_tensor(
                    tmp0[:, :], xin_t[:, ct, 1:CH + 1], cw[:, ct, 1:2], tmp0[:, :],
                    ALU.mult, ALU.add), after=(c1,), inc=True)
                cv_toks.append(P.op("vector", lambda v, ct=ct: v.scalar_tensor_tensor(
                    xcb[:, ct, :], xin_t[:, ct, 2:CH + 2], cw[:, ct, 2:3], tmp0[:, :],
                    ALU.mult, ALU.add), after=(c2,), inc=True))

            xs_toks = []
            for mt in range(NCT):
                slot = mt % 2
                dep = xs_toks[-2] if len(xs_toks) >= 2 else fr_psA[slot]
                for ci in range(NCT):
                    tk = P.op("tensor", lambda t, mt=mt, ci=ci, slot=slot: t.matmul(
                        psA[slot][:, :], peT[:, ci, mt * 128:(mt + 1) * 128],
                        xcb[:, ci, :], start=(ci == 0), stop=(ci == NCT - 1)),
                        after=(cv_toks[-1], t_pe, dep, fr_xs) if ci == 0 else (),
                        inc=(ci == NCT - 1))
                xs_toks.append(P.op("vector", lambda v, mt=mt, slot=slot: v.tensor_add(
                    xs[:, mt, :], xcb[:, mt, :], psA[slot][:, :]),
                    after=(tk,), inc=True))

            ev_q = []
            ev_k = []
            for mt in range(NCT):
                slot = mt % 2
                dep = ev_q[-2] if len(ev_q) >= 2 else xs_toks[-2 + slot]
                for ci in range(NCT):
                    tk = P.op("tensor", lambda t, mt=mt, ci=ci, slot=slot: t.matmul(
                        psA[slot][:, :], qkvT[:, ci, mt * 128:(mt + 1) * 128],
                        xs[:, ci, :], start=(ci == 0), stop=(ci == NCT - 1)),
                        after=(xs_toks[-1], t_qk, dep, fr_qf) if ci == 0 else (),
                        inc=(ci == NCT - 1))
                ev_q.append(P.op("scalar", lambda v, mt=mt, slot=slot: v.copy(
                    qf[:, mt, :], psA[slot][:, :]), after=(tk,), inc=True))
            for mt in range(NCT):
                slot = mt % 2
                dep = ev_k[-2] if len(ev_k) >= 2 else ev_q[-2 + slot]
                for ci in range(NCT):
                    tk = P.op("tensor", lambda t, mt=mt, ci=ci, slot=slot: t.matmul(
                        psA[slot][:, :], qkvT[:, ci, 1024 + mt * 128:1024 + (mt + 1) * 128],
                        xs[:, ci, :], start=(ci == 0), stop=(ci == NCT - 1)),
                        after=(dep,) if ci == 0 else (), inc=(ci == NCT - 1))
                ev_k.append(P.op("scalar", lambda v, mt=mt, slot=slot: v.copy(
                    kf[:, mt, :], psA[slot][:, :]), after=(tk,), inc=True))
            ev_v = []
            for s4 in range(4):
                for nch in range(2):
                    i = s4 * 2 + nch
                    slot = i % 2
                    dep = ev_v[-2] if len(ev_v) >= 2 else ev_k[-2 + slot]
                    for ci in range(NCT):
                        tk = P.op("tensor", lambda t, s4=s4, nch=nch, ci=ci, slot=slot: t.matmul(
                            psA[slot][:, :], xs[:, ci, s4 * 128:(s4 + 1) * 128],
                            qkvT[:, ci, 2048 + nch * 512:2048 + (nch + 1) * 512],
                            start=(ci == 0), stop=(ci == NCT - 1)),
                            after=(dep, fr_vtok if i == 0 else None) if ci == 0 else (),
                            inc=(ci == NCT - 1))
                    dst = vtok[:, s4, :].rearrange("p (h c2) -> p h c2", h=H, c2=65)[
                        :, nch * 8:(nch + 1) * 8, 0:64]
                    srcp = psA[slot][:, :].rearrange("p (h k) -> p h k", h=8, k=64)
                    ev_v.append(P.op("scalar", lambda v, dst=dst, srcp=srcp: v.copy(dst, srcp),
                                     after=(tk, t_vo), inc=True))
            fr_xs = tk
            fr_psA[0] = ev_v[-2]
            fr_psA[1] = ev_v[-1]

            t_qs0 = P.dma("sync", qsh[0:64, :, :], qf[64:128, :, :], after=ev_q + [fr_scr8])
            t_qs1 = P.dma("sync", qsh[64:128, :, :], qf[0:64, :, :], after=ev_q)
            t_ks0 = P.dma("sync", ksh[0:64, :, :], kf[64:128, :, :], after=ev_k)
            t_ks = P.dma("sync", ksh[64:128, :, :], kf[0:64, :, :], after=ev_k)
            t_qs = t_qs1

            sq1 = P.op("scalar", lambda s_: s_.activation(
                scr8[:, :, :], qf[:, :, :], AF.Square), after=ev_q + [fr_scr8], inc=True)
            nb1 = []
            for blk in range(NCT):
                psE = psE1 if blk % 2 == 0 else psE2
                mmn = P.op("tensor", lambda t, blk=blk, psE=psE: t.matmul(
                    psE[0:2, :], eb[:, :], scr8[:, blk, :], start=True, stop=True),
                    after=(sq1, t_eb, fr_psE1 if blk == 0 else None,
                           fr_psE2 if blk == 1 else None,
                           nb1[-2] if len(nb1) >= 2 else None), inc=True)
                if blk == 0:
                    nb1.append(P.op("vector", lambda v, psE=psE: v.tensor_copy(
                        pnacc, psE[0:2, :]), after=(mmn, fr_u_ph2), inc=True))
                else:
                    nb1.append(P.op("vector", lambda v, psE=psE: v.tensor_tensor(
                        pnacc, pnacc, psE[0:2, :], ALU.max), after=(mmn,), inc=True))
            q1 = P.op("vector", lambda v: v.tensor_reduce(
                tinyB[:, :], pnacc, AX.X, ALU.max), after=(nb1[-1],), inc=True)
            g1 = P.op("gpsimd", lambda g: g.tensor_reduce(
                tinyC[:, :], tinyB[:, :], AX.C, ALU.max), after=(q1,), inc=True)
            sq2 = P.op("scalar", lambda s_: s_.activation(
                scr8[:, :, :], kf[:, :, :], AF.Square), after=ev_k + [nb1[-1], t_qs], inc=True)
            nb2 = []
            for blk in range(NCT):
                psE = psE1 if blk % 2 == 0 else psE2
                mmn = P.op("tensor", lambda t, blk=blk, psE=psE: t.matmul(
                    psE[0:2, :], eb[:, :], scr8[:, blk, :], start=True, stop=True),
                    after=(sq2, nb1[-2 + (blk % 2)] if blk < 2 else None,
                           nb2[-2] if len(nb2) >= 2 else None), inc=True)
                if blk == 0:
                    nb2.append(P.op("vector", lambda v, psE=psE: v.tensor_copy(
                        pnacc, psE[0:2, :]), after=(mmn, q1), inc=True))
                else:
                    nb2.append(P.op("vector", lambda v, psE=psE: v.tensor_tensor(
                        pnacc, pnacc, psE[0:2, :], ALU.max), after=(mmn,), inc=True))
            fr_psE1 = nb2[-2]
            fr_psE2 = nb2[-1]
            k1 = P.op("vector", lambda v: v.tensor_reduce(
                tinyB[:, :], pnacc, AX.X, ALU.max), after=(nb2[-1],), inc=True)
            g2 = P.op("gpsimd", lambda g: g.tensor_reduce(
                tinyC2[:, :], tinyB[:, :], AX.C, ALU.max), after=(k1, g1), inc=True)
            l1_ = P.op("scalar", lambda s_: s_.activation(tinyC[:, :], tinyC[:, :], AF.Ln),
                       after=(g1, sq2), inc=True)
            l2_ = P.op("scalar", lambda s_: s_.activation(tinyC2[:, :], tinyC2[:, :], AF.Ln),
                       after=(g2,), inc=True)
            l3_ = P.op("vector", lambda v: v.tensor_add(tinyC[:, :], tinyC[:, :], tinyC2[:, :]),
                       after=(l1_, l2_), inc=True)
            l4_ = P.op("vector", lambda v: v.tensor_scalar(
                tinyC[:, :], tinyC[:, :], 0.5, -float(np.log(8.0)), ALU.mult, ALU.add),
                after=(l3_,), inc=True)
            l5_ = P.op("scalar", lambda s_: s_.activation(tinyC[:, :], tinyC[:, :], AF.Exp),
                       after=(l4_,), inc=True)
            nb6 = P.op("tensor", lambda t: t.matmul(
                psE2[:, 0:1], ones_f[0:1, :], tinyC[:, :], start=True, stop=True),
                after=(l5_, fr_psE2, t_of), inc=True)
            t_negB = P.op("vector", lambda v: v.tensor_scalar(
                negB[:, :], psE2[:, 0:1], -1.0, None, ALU.mult), after=(nb6,), inc=True)
            fr_psE2 = t_negB

            psSS = [psS[0], psS[1], psA[0], psA[1]]
            fr_psSA = [fr_psS[0], fr_psS[1], fr_psA[0], fr_psA[1]]
            itc = 0
            dn_tok = []
            ev_o = []
            den_local = []
            fr_den_gate = fr_den
            for s4 in range(4):
                h = 4 * c + s4
                base = (h % 2) * 64
                rp = []
                base_dep = (t_qs0, t_qs1, t_ks0, t_ks, sq2, l1_, ev_q[-1],
                            fr_scr8 if s4 == 0 else None,
                            dn_tok[-1] if dn_tok else None)
                for i3, (dv, srcbuf, lo, par) in enumerate((
                        (qv, qf, True, 0), (qv, qsh, True, 1),
                        (qv, qsh, False, 0), (qv, qf, False, 1),
                        (kv, kf, True, 0), (kv, ksh, True, 1),
                        (kv, ksh, False, 0), (kv, kf, False, 1))):
                    prange = slice(0, 64) if lo else slice(64, 128)
                    dst = dv[prange, par::2, :]
                    srca = srcbuf[prange, :, s4 * 128:(s4 + 1) * 128]
                    if i3 % 2 == 0:
                        rp.append(P.op("scalar", lambda v, dst=dst, srca=srca: v.copy(dst, srca),
                                       after=base_dep, inc=True))
                    else:
                        rp.append(P.op("vector", lambda v, dst=dst, srca=srca: v.tensor_copy(dst, srca),
                                       after=base_dep, inc=True))
                for qc in range(4):
                    oslot = (s4 * 4 + qc) % 2
                    if len(ev_o) >= 2:
                        o_dep, o_dep2 = ev_o[-2], dn_tok[-2]
                    else:
                        o_dep, o_dep2 = fr_psO[oslot] or (None, None)
                    av = None
                    pending = []

                    def emit_av(args, s4=s4, oslot=oslot, o_dep=o_dep, o_dep2=o_dep2):
                        nonlocal av
                        ck, pslot, mk = args
                        av = P.op("tensor", lambda t, s4=s4, ck=ck, pslot=pslot, oslot=oslot: t.matmul(
                            psO[oslot][:, :],
                            vtok[:, s4, ck * 65:(ck + 1) * 65],
                            probs[pslot][:, :],
                            start=(ck == 0), stop=(ck == 15)),
                            after=((mk,) + ((ev_v[-1], o_dep, o_dep2) if ck == 0 else ())),
                            inc=(ck == 15))
                        fr_probs[pslot] = av

                    for ckp in range(8):
                        scp = []
                        for ck in (2 * ckp, 2 * ckp + 1):
                            sslot = itc % 4
                            itc += 1
                            pr2 = slice(0, 64) if ck % 2 == 0 else slice(64, 128)
                            tk_sc = P.op("tensor", lambda t, ck=ck, qc=qc, sslot=sslot, pr2=pr2: t.matmul(
                                psSS[sslot][:, :], kv[pr2, ck, :],
                                qv[pr2, 4 * qc:4 * qc + 4, :], start=True, stop=True),
                                after=(fr_psSA[sslot], rp[-1], rp[-2]), inc=True)
                            scp.append((ck, sslot, tk_sc))
                        for ck, sslot, tk_sc in scp:
                            pslot = sslot
                            tk_ex = P.op("scalar", lambda s_, sslot=sslot, pslot=pslot: s_.activation(
                                probs[pslot][:, :], psSS[sslot][:, :],
                                AF.Exp, bias=negB[:, :], scale=0.125),
                                after=(tk_sc, t_negB, fr_probs[pslot]), inc=True)
                            fr_psSA[sslot] = tk_ex
                            dd_ = min(max(ck - 4 * qc, 0), 4)
                            mk = tk_ex
                            if dd_ > 0:
                                pv = probs[pslot][:, 0:dd_ * 128].rearrange(
                                    "p (a b) -> p a b", a=dd_, b=128)
                                mk = P.op("vector", lambda v, pv=pv, dd_=dd_: v.tensor_mul(
                                    pv, pv, m_strict[:, :].unsqueeze(1).broadcast_to([128, dd_, 128])),
                                    after=(mk, t_ms), inc=True)
                            if dd_ < 4:
                                nsl = 4 - dd_
                                pv = probs[pslot][:, dd_ * 128:CH].rearrange(
                                    "p (a b) -> p a b", a=nsl, b=128)
                                mk = P.op("vector", lambda v, pv=pv, nsl=nsl: v.tensor_mul(
                                    pv, pv, m_incl[:, :].unsqueeze(1).broadcast_to([128, nsl, 128])),
                                    after=(mk, t_mi), inc=True)
                            pending.append((ck, pslot, mk))
                        while len(pending) > 3:
                            emit_av(pending.pop(0))
                    while pending:
                        emit_av(pending.pop(0))
                    o_dst = o_buf[base:base + 64, h // 2, :].rearrange(
                        "p (j cc) -> p j cc", j=128, cc=16)[
                        :, :, 4 * qc:4 * qc + 4].transpose([0, 2, 1])
                    ev_o.append(P.op("vector", lambda v, o_dst=o_dst, oslot=oslot: v.tensor_copy(
                        o_dst, psO[oslot][0:64, :].rearrange("p (s j) -> p s j", s=4, j=128)),
                        after=(av, fr_obuf), inc=True))
                    dn_dst = den[64:65].rearrange(
                        "o (j cc) -> o j cc", j=128, cc=16)[
                        :, :, 4 * qc:4 * qc + 4].transpose([0, 2, 1])
                    dn_tok.append(P.op("scalar", lambda s_, dn_dst=dn_dst, oslot=oslot: s_.copy(
                        dn_dst, psO[oslot][64:65, :].rearrange("o (s j) -> o s j", s=4, j=128)),
                        after=(av, fr_den_gate), inc=True))
                half_dma = P.dma("sync",
                                 dend[s4:s4 + 1, :],
                                 den[64:65][:, 0:4 * CH],
                                 after=dn_tok[-4:] + [fr_dend])
                den_local.append(half_dma)
                fr_den_gate = half_dma
            fr_psO[0] = (ev_o[-2], dn_tok[-2])
            fr_psO[1] = (ev_o[-1], dn_tok[-1])
            den_dmas += den_local
            fr_den = den_local[-1]
            fr_scr8 = av
            fr_psS[0] = fr_psSA[0]
            fr_psS[1] = fr_psSA[1]
            fr_psA[0] = fr_psSA[2]
            fr_psA[1] = fr_psSA[3]

            t_rec = P.op("vector", lambda v: v.reciprocal(dend[:, :], dend[:, :]),
                         after=den_local, inc=True)
            t_recb = P.op("vector", lambda v: v.tensor_copy(dendb[:, :], dend[:, :]),
                          after=(t_rec, fr_dendb), inc=True)
            fr_dend = t_recb
            scl_prev = [fr_psE1, fr_psE2]
            psE = [psE1, psE2]
            mul_toks = []
            for i2 in range(2):
                ct = 2 * c + i2
                for qc in range(4):
                    eslot = (i2 * 4 + qc) % 2
                    t_sc = P.op("tensor", lambda t, i2=i2, qc=qc, eslot=eslot: t.matmul(
                        psE[eslot][:, :], sel4[:, i2 * 128:(i2 + 1) * 128],
                        dendb[0:4, qc * CH:(qc + 1) * CH], start=True, stop=True),
                        after=(t_recb, t_s4, scl_prev[eslot],
                               mul_toks[-2] if len(mul_toks) >= 2 else None), inc=True)
                    scl_prev[eslot] = None
                    mul_toks.append(P.op("vector", lambda v, ct=ct, qc=qc, eslot=eslot: v.tensor_tensor(
                        o_buf[:, ct, qc * CH:(qc + 1) * CH],
                        o_buf[:, ct, qc * CH:(qc + 1) * CH],
                        psE[eslot][:, :], ALU.mult),
                        after=(t_sc, ev_o[-1]), inc=True))
            fr_dendb = t_sc
            fr_psE1 = mul_toks[-2]
            fr_psE2 = mul_toks[-1]
            scale_toks_all += mul_toks
            fr_xin = cv_toks[-1]

        t_ow = [P.dma("gpsimd", outwT[i2],
                      io["outw_t"][i2 * 4:(i2 + 1) * 4].transpose([1, 0, 2]),
                      after=(cv_toks[-1] if i2 == 0 else None, fr_xs, fr_outw))
                for i2 in range(2)]
        out_last = None
        for c in range(NCHUNK):
            cb = c * CH
            t_xf = P.dma("sync", xin_t[:, :, 1:CH + 1],
                         io["xin"][s, :, :, cb:cb + CH].transpose([1, 0, 2]),
                         after=(fr_xin, out_last))
            res_toks = []
            for mt in range(NCT):
                slot = mt % 2
                dep = res_toks[-2] if len(res_toks) >= 2 else fr_psA[slot]
                for ci in range(NCT):
                    tk = P.op("tensor", lambda t, mt=mt, ci=ci, slot=slot, cb=cb: t.matmul(
                        psA[slot][:, :],
                        outwT[ci // 4][:, ci % 4, mt * 128:(mt + 1) * 128],
                        o_buf[:, ci, cb:cb + CH], start=(ci == 0), stop=(ci == NCT - 1)),
                        after=(scale_toks_all[-1], t_ow[0], t_ow[1], dep) if ci == 0 else (),
                        inc=(ci == NCT - 1))
                res_toks.append(P.op("vector", lambda v, mt=mt, slot=slot: v.tensor_add(
                    xin_t[:, mt, 1:CH + 1], xin_t[:, mt, 1:CH + 1], psA[slot][:, :]),
                    after=(tk, t_xf), inc=True))
            fr_psA[0] = res_toks[-2]
            fr_psA[1] = res_toks[-1]
            st2 = []
            sq2s = []
            for ct in range(NCT):
                t_rb = P.op("vector", lambda v, ct=ct: v.tensor_copy(
                    rb[:, :], xin_t[:, ct, 1:CH + 1]),
                    after=(res_toks[ct], st2[-1] if st2 else None, fr_rb), inc=True)
                t_r2 = P.op("scalar", lambda s_, ct=ct: s_.activation(
                    r2b[:, :], xin_t[:, ct, 1:CH + 1], AF.Square),
                    after=(res_toks[ct], sq2s[-1] if sq2s else None, fr_r2b), inc=True)
                st2.append(P.op("tensor", lambda t, ct=ct: t.matmul(
                    psE1[0:1, :], ones_b[:, 0:1], rb[:, :],
                    start=(ct == 0), stop=(ct == NCT - 1)),
                    after=(t_rb, t_ob) + ((fr_psE1,) if ct == 0 else ()), inc=True))
                sq2s.append(P.op("tensor", lambda t, ct=ct: t.matmul(
                    psE2[0:1, :], ones_b[:, 0:1], r2b[:, :],
                    start=(ct == 0), stop=(ct == NCT - 1)),
                    after=(t_r2,) + ((fr_psE2,) if ct == 0 else ()), inc=True))
                fr_rb = st2[-1]
                fr_r2b = sq2s[-1]
            f1 = P.op("vector", lambda v: v.tensor_copy(ssum, psE1[0:1, :]),
                      after=(st2[-1], fr_dend if c == 0 else None), inc=True)
            f2 = P.op("vector", lambda v: v.tensor_copy(ssq, psE2[0:1, :]),
                      after=(sq2s[-1],), inc=True)
            f3 = P.op("tensor", lambda t: t.matmul(
                psE1[:, :], ones_f[0:1, :], ssum, start=True, stop=True),
                after=(f1,), inc=True)
            f4 = P.op("tensor", lambda t: t.matmul(
                psE2[:, :], ones_f[0:1, :], ssq, start=True, stop=True),
                after=(f2,), inc=True)
            f5 = P.op("vector", lambda v: v.tensor_scalar(
                mb, psE1[:, :], 1.0 / D, None, ALU.mult),
                after=(f3, scale_toks_all[-1] if c == 0 else None), inc=True)
            fr_psE1 = f5
            f6 = P.op("scalar", lambda s_: s_.activation(m2, mb, AF.Square),
                      after=(f5,), inc=True)
            f7 = P.op("vector", lambda v: v.scalar_tensor_tensor(
                vb, psE2[:, :], 1.0 / D, m2, ALU.mult, ALU.subtract),
                after=(f4, f6), inc=True)
            fr_psE2 = f7
            f8 = P.op("scalar", lambda s_: s_.activation(vb, vb, AF.Ln, bias=eps_t[:, :]),
                      after=(f7, t_eps0), inc=True)
            f9 = P.op("scalar", lambda s_: s_.activation(rstd, vb, AF.Exp, scale=-0.5),
                      after=(f8,), inc=True)
            out_toks = []
            for ct in range(NCT):
                u1 = P.op("vector", lambda v, ct=ct: v.tensor_sub(
                    xin_t[:, ct, 1:CH + 1], xin_t[:, ct, 1:CH + 1], mb),
                    after=(f5, res_toks[ct]), inc=True)
                out_toks.append(P.op("vector", lambda v, ct=ct: v.tensor_mul(
                    xin_t[:, ct, 1:CH + 1], xin_t[:, ct, 1:CH + 1], rstd),
                    after=(u1, f9), inc=True))
            t_out = P.dma("sync", io["xout"][s, :, :, cb:cb + CH].transpose([1, 0, 2]),
                          xin_t[:, :, 1:CH + 1], after=out_toks)
            out_last = t_out
        fr_xin = out_last
        fr_obuf = tk
        fr_outw = tk
        fr_xcb = tk
        fr_den = out_last
        fr_u_ph2 = out_last
    final_tok = ("d", "sync", P.dn["sync"])
    P.op("sync", lambda e: e.nop(), after=(final_tok,))
    return final_tok


def build_l1a_program():
    nc = bass.Bass("TRN2", target_bir_lowering=False, debug=False)
    io = {}
    io["xin"] = nc.dram_tensor("xin", [NSEQ, NCT, 128, T], F32, kind="ExternalInput").ap()
    io["xout"] = nc.dram_tensor("xout", [NSEQ, NCT, 128, T], F32, kind="ExternalOutput").ap()
    io["conv3"] = nc.dram_tensor("conv3", [NCT, 128, 3], F32, kind="ExternalInput").ap()
    io["pe_t"] = nc.dram_tensor("pe_t", [NCT, 128, D], F32, kind="ExternalInput").ap()
    io["qkv_t"] = nc.dram_tensor("qkv_t", [NCT, 128, 3072], F32, kind="ExternalInput").ap()
    io["outw_t"] = nc.dram_tensor("outw_t", [NCT, 128, D], F32, kind="ExternalInput").ap()
    io["ident_f32"] = nc.dram_tensor("ident_f32", [128, 128], F32, kind="ExternalInput").ap()
    io["ones_bf16"] = nc.dram_tensor("ones_bf16", [128, 128], BF16, kind="ExternalInput").ap()
    io["ones_f32"] = nc.dram_tensor("ones_f32", [128, 128], F32, kind="ExternalInput").ap()
    io["mask_strict"] = nc.dram_tensor("mask_strict", [128, 128], BF16, kind="ExternalInput").ap()
    io["mask_incl"] = nc.dram_tensor("mask_incl", [128, 128], BF16, kind="ExternalInput").ap()
    io["sel4"] = nc.dram_tensor("sel4", [4, 2 * 128], BF16, kind="ExternalInput").ap()
    io["eb_bf16"] = nc.dram_tensor("eb_bf16", [128, 2], BF16, kind="ExternalInput").ap()
    stack = ExitStack()
    P = Prog(nc)
    P.make_sems(stack)
    build_l1a(nc, P, stack, io)
    block = stack.enter_context(nc.Block())
    P.finalize(block)
    stack.close()
    return nc


def l1a_in_maps(inputs, x2_core):
    ident, ones_b, ones_f, maskT, maskLK, sel, eb = _host_consts()
    cw = np.asarray(inputs["L1_attn_conv_w"])[:, 0, :, 1]
    j = np.arange(128)
    strict = (j[:, None] < j[None, :]).astype(ml_dtypes.bfloat16)
    incl = (j[:, None] <= j[None, :]).astype(ml_dtypes.bfloat16)
    m_strict = strict
    m_incl = incl
    sel4 = np.zeros((4, 2 * 128), dtype=ml_dtypes.bfloat16)
    for i2 in range(2):
        for pp in range(128):
            sel4[2 * i2 + (pp >= 64), i2 * 128 + pp] = 1.0
    return {
        "xin": x2_core,
        "conv3": np.ascontiguousarray(cw.reshape(NCT, 128, 3)),
        "pe_t": _wT(np.asarray(inputs["L1_attn_pe_w"])),
        "qkv_t": _wT(np.asarray(inputs["L1_attn_qkv_w"])),
        "outw_t": _wT(np.asarray(inputs["L1_attn_out_w"])),
        "ident_f32": ident, "ones_bf16": ones_b, "ones_f32": ones_f,
        "mask_strict": np.ascontiguousarray(m_strict),
        "mask_incl": np.ascontiguousarray(m_incl),
        "sel4": sel4, "eb_bf16": eb,
    }



def build_l1f(nc, P, stack, io):
    E = stack.enter_context
    sb = lambda name, shape, dt=F32: E(nc.sbuf_tensor(name, shape, dt))
    NE = 8

    xb = sb("l1f_xb", [128, NCT, T], BF16)
    yacc = sb("l1f_yacc", [128, NCT, T], F32)
    w1s = [sb(f"l1f_w1_{i}", [128, NCT, 1024], BF16) for i in range(2)]
    w3s = [sb(f"l1f_w3_{i}", [128, NCT, 1024], BF16) for i in range(2)]
    w2s = sb("l1f_w2", [128, NCT, 1024], BF16)
    xf = w2s[:, :, :].rearrange("p a b -> p (a b)").bitcast(F32).rearrange(
        "p (a b) -> p a b", a=NCT, b=CH)
    G = sb("l1f_G", [128, NCT, CH], BF16)
    sil0 = sb("l1f_sil0", [128, CH], BF16)
    rout = sb("l1f_rout", [128, 8, 128], F32)
    logits = rout[:, 0, :].rearrange("p (a b) -> p a b", a=16, b=8)
    scores = rout[:, 1, :].rearrange("p (a b) -> p a b", a=16, b=8)
    sbpb = rout[:, 2, :].rearrange("p (a b) -> p a b", a=16, b=8)
    oh1 = rout[:, 3, :].rearrange("p (a b) -> p a b", a=16, b=8)
    oh2 = rout[:, 4, :].rearrange("p (a b) -> p a b", a=16, b=8)
    cumt = rout[:, 5, :].rearrange("p (a b) -> p a b", a=16, b=8)
    cumt2 = rout[:, 6, :].rearrange("p (a b) -> p a b", a=16, b=8)
    gts = rout[:, 7, :].rearrange("p (a b) -> p a b", a=16, b=8)
    red16 = sb("l1f_red16", [128, 16], F32)
    m8 = sb("l1f_m8", [128, 16, 8], F32)
    gT = yacc[0:8, 0, :]
    gTb = sb("l1f_gTb", [8, T], BF16)
    e8sel = sb("l1f_e8sel", [8, NE * 128], BF16)
    rw = sb("l1f_rw", [128, NCT, 8], F32)
    ebias = sb("l1f_ebias", [128, 8], F32)
    ones_b = sb("l1f_ones_b", [128, 128], BF16)
    ones_f = sb("l1f_ones_f", [128, 128], F32)
    ident = sb("l1f_ident", [128, 128], F32)
    eps_t = sb("l1f_eps", [128, 1], F32)
    u1_t = sb("l1f_u1", [128, 2 * CH], F32)
    u2_t = sb("l1f_u2", [128, 2 * CH], F32)
    mb, m2 = u1_t[:, 0:CH], u1_t[:, CH:2 * CH]
    vb, rstd = u2_t[:, 0:CH], u2_t[:, CH:2 * CH]
    _routf = rout[:, :, :].rearrange("p a b -> p (a b)")
    ssum = _routf[0:1, 0:CH]
    ssq = _routf[0:1, CH:2 * CH]
    rb_t = sb("l1f_rb0", [128, CH], BF16)
    r2b_t = sb("l1f_r2b0", [128, CH], BF16)
    rb = [rb_t, rb_t]
    r2b = [r2b_t, r2b_t]
    sil = [sil0, rb_t[:, :]]

    psA = [E(nc.psum_tensor(f"l1f_psA{i}", [128, CH], F32)) for i in range(4)]
    psY = [E(nc.psum_tensor(f"l1f_psY{i}", [128, CH], F32)) for i in range(2)]
    psE1 = E(nc.psum_tensor("l1f_psE1", [128, CH], F32))
    psE2 = E(nc.psum_tensor("l1f_psE2", [128, CH], F32))

    t_ob = P.dma("sync", ones_b[:, :], io["ones_bf16"])
    t_of = P.dma("sync", ones_f[:, :], io["ones_f32"])
    t_id = P.dma("sync", ident[:, :], io["ident_f32"])
    t_es = P.dma("sync", e8sel[:, :], io["e8sel"])
    t_rw = P.dma("sync", rw[:, :, :], io["router_t"].transpose([1, 0, 2]))
    t_eb = P.dma("sync", ebias[:, :], io["ebias_b"])
    t_eps0 = P.op("vector", lambda v: v.memset(eps_t[:, :], 1e-5), inc=True)

    def wload(dst, e, which, after=()):
        if e == 8:
            src = io[("sw1", "sw3", "sw2")[which]].transpose([1, 0, 2])
        else:
            src = io[("w1", "w3", "w2")[which]][e].transpose([1, 0, 2])
        return P.dma("gpsimd", dst[:, :, :], src, after=after)

    fr_xb = None
    fr_w1 = [None, None]
    fr_w3 = [None, None]
    fr_w2 = None
    fr_G = None
    fr_sil = [None, None]
    fr_psA = [None, None, None, None]
    fr_psY = [None, None]
    fr_psE1 = None
    fr_psE2 = None
    fr_yacc = None
    fr_gT = None
    fr_rout = None
    fr_rb = [None, None]
    fr_r2b = [None, None]
    t_w = {}
    for s in range(NSEQ):
        t_xb = P.dma("gpsimd", xb[:, :, :],
                     io["xin"][s, :, :, :].transpose([1, 0, 2]), after=(fr_xb,))

        lg_toks = []
        t_xr_last = None
        for rc in range(4):
            t_xr = P.dma("sync", xf[:, :, :],
                         io["xin"][s, :, :, rc * CH:(rc + 1) * CH].transpose([1, 0, 2]),
                         after=(fr_w2 if rc == 0 else None, t_xr_last))
            lg_local = []
            for tl in range(4):
                tt = rc * 4 + tl
                slot = tt % 2
                dep = lg_toks[-2] if len(lg_toks) >= 2 else fr_psA[slot]
                for ci in range(NCT):
                    tk = P.op("tensor", lambda t, tt=tt, tl=tl, ci=ci, slot=slot: t.matmul(
                        psA[slot][:, 0:8], xf[:, ci, tl * 128:(tl + 1) * 128],
                        rw[:, ci, :], start=(ci == 0), stop=(ci == NCT - 1)),
                        after=(t_xr, t_rw, dep, fr_rout) if ci == 0 else (),
                        inc=(ci == NCT - 1))
                lg_toks.append(P.op("vector", lambda v, tt=tt, slot=slot: v.tensor_copy(
                    logits[:, tt, :], psA[slot][:, 0:8]), after=(tk,), inc=True))
            t_xr_last = lg_toks[-1]
        fr_w2 = lg_toks[-1]
        r1 = P.op("vector", lambda v: v.tensor_reduce(
            red16[:, :], logits[:, :, :], AX.X, ALU.max), after=lg_toks, inc=True)
        r2 = P.op("vector", lambda v: v.tensor_sub(
            scores[:, :, :], logits[:, :, :],
            red16[:, :].unsqueeze(2).broadcast_to([128, 16, 8])), after=(r1,), inc=True)
        r3 = P.op("scalar", lambda s_: s_.activation(scores[:, :, :], scores[:, :, :], AF.Exp),
                  after=(r2,), inc=True)
        r4 = P.op("vector", lambda v: v.tensor_reduce(
            red16[:, :], scores[:, :, :], AX.X, ALU.add), after=(r3,), inc=True)
        r5 = P.op("vector", lambda v: v.reciprocal(red16[:, :], red16[:, :]),
                  after=(r4,), inc=True)
        r6 = P.op("vector", lambda v: v.tensor_mul(
            scores[:, :, :], scores[:, :, :],
            red16[:, :].unsqueeze(2).broadcast_to([128, 16, 8])), after=(r5,), inc=True)
        r7 = P.op("vector", lambda v: v.tensor_add(
            sbpb[:, :, :], scores[:, :, :],
            ebias[:, :].unsqueeze(1).broadcast_to([128, 16, 8])), after=(r6, t_eb), inc=True)
        m8t = []
        for tt in range(16):
            m8t.append(P.op("vector", lambda v, tt=tt: v.max(
                m8[:, tt, :], sbpb[:, tt, :]), after=(r7,), inc=True))

        def onehot(dst, val_col, excl, after):
            t1 = P.op("vector", lambda v: v.tensor_tensor(
                dst, sbpb[:, :, :],
                m8[:, :, val_col:val_col + 1].broadcast_to([128, 16, 8]),
                ALU.is_equal), after=after, inc=True)
            if excl is not None:
                t1a = P.op("vector", lambda v: v.tensor_mul(
                    cumt[:, :, :], dst, excl), after=(t1,), inc=True)
                t1 = P.op("vector", lambda v: v.tensor_sub(
                    dst, dst, cumt[:, :, :]), after=(t1a,), inc=True)
            t2 = P.op("vector", lambda v: v.tensor_copy(cumt[:, :, :], dst), after=(t1,), inc=True)
            for sh, (a, b) in enumerate(((cumt, cumt2), (cumt2, cumt))):
                shift = 1 << sh
                t3 = P.op("vector", lambda v, a=a, b=b, shift=shift: v.tensor_copy(
                    b[:, :, 0:shift], a[:, :, 0:shift]), after=(t2,), inc=True)
                t4 = P.op("vector", lambda v, a=a, b=b, shift=shift: v.tensor_add(
                    b[:, :, shift:8], a[:, :, shift:8], a[:, :, 0:8 - shift]),
                    after=(t3,), inc=True)
                t2 = t4
            t5 = P.op("vector", lambda v: v.tensor_copy(
                cumt2[:, :, 0:4], cumt[:, :, 0:4]), after=(t2,), inc=True)
            t6 = P.op("vector", lambda v: v.tensor_add(
                cumt2[:, :, 4:8], cumt[:, :, 4:8], cumt[:, :, 0:4]), after=(t5,), inc=True)
            t7 = P.op("vector", lambda v: v.tensor_scalar(
                cumt2[:, :, :], cumt2[:, :, :], 1.0, None, ALU.is_equal),
                after=(t6,), inc=True)
            t8 = P.op("vector", lambda v: v.tensor_mul(dst, dst, cumt2[:, :, :]),
                      after=(t7,), inc=True)
            return t8

        o1t = onehot(oh1[:, :, :], 0, None, tuple(m8t) + (r7, fr_rout))
        o2t = onehot(oh2[:, :, :], 1, oh1[:, :, :], (o1t,))
        g1 = P.op("vector", lambda v: v.tensor_mul(gts[:, :, :], scores[:, :, :], oh1[:, :, :]),
                  after=(o1t, r6), inc=True)
        g2 = P.op("vector", lambda v: v.scalar_tensor_tensor(
            cumt[:, :, :], scores[:, :, :], 1.0, oh2[:, :, :], ALU.mult, ALU.mult),
            after=(o2t, g1), inc=True)
        g3 = P.op("vector", lambda v: v.tensor_add(
            gts[:, :, :], gts[:, :, :], cumt[:, :, :]), after=(g2,), inc=True)
        tr_toks = []
        for tt in range(16):
            tslot = tt % 2
            psE = psE1 if tslot == 0 else psE2
            tk = P.op("tensor", lambda t, tt=tt, psE=psE: t.transpose(
                psE[0:8, 0:128], gts[:, tt, :], ident[:, :]),
                after=(g3, t_id, fr_psE1 if tt == 0 else None,
                       fr_psE2 if tt == 1 else None,
                       tr_toks[-2] if len(tr_toks) >= 2 else None), inc=True)
            tr_toks.append(P.op("vector", lambda v, tt=tt, psE=psE: v.tensor_copy(
                gT[:, tt * 128:(tt + 1) * 128], psE[0:8, 0:128]),
                after=(tk, fr_gT), inc=True))
        t_gTb = P.op("vector", lambda v: v.tensor_copy(gTb[:, :], gT[:, :]),
                     after=(tr_toks[-1],), inc=True)
        fr_psE1 = tr_toks[-2]
        fr_psE2 = tr_toks[-1]
        fr_rout = t_gTb
        if DEBUG and s == 0:
            P.dma("sync", io["dbg_gTb"], gTb[:, :], after=(t_gTb,))
            P.dma("sync", io["dbg_scores"],
                  rout[:, 1, :], after=(t_gTb,))

        if s == 0:
            t_w[(0, 0)] = wload(w1s[0], 0, 0, after=(fr_w1[0],))
            t_w[(0, 1)] = wload(w3s[0], 0, 1, after=(fr_w3[0],))
        yadd_toks = []
        for e in range(9):
            eslot = e % 2
            if e < 8:
                t_w[(e + 1, 0)] = wload(w1s[1 - eslot], e + 1, 0, after=(fr_w1[1 - eslot],))
                t_w[(e + 1, 1)] = wload(w3s[1 - eslot], e + 1, 1, after=(fr_w3[1 - eslot],))
            t_w[(e, 2)] = wload(w2s, e, 2, after=(fr_w2,))
            y_this = []
            if DEBUG and s == 0 and e == 1:
                P.dma("sync", io["dbg_w11"], w1s[eslot][:, :, :], after=(t_w[(1, 0)],))
            for c in range(NCHUNK):
                cb = c * CH
                if e < 8:
                    bcm = P.op("tensor", lambda t, e=e, cb=cb: t.matmul(
                        psE1[:, :], e8sel[:, e * 128:(e + 1) * 128], gTb[:, cb:cb + CH],
                        start=True, stop=True), after=(t_gTb, t_es, fr_psE1), inc=True)
                g_toks = []
                for ht in range(NCT):
                    s1_ = (ht % 2) * 2
                    s3_ = s1_ + 1
                    dep1 = g_toks[-2] if len(g_toks) >= 2 else fr_psA[s1_]
                    for ci in range(NCT):
                        tk1 = P.op("tensor", lambda t, ht=ht, ci=ci, s1_=s1_, eslot=eslot, cb=cb: t.matmul(
                            psA[s1_][:, :], w1s[eslot][:, ci, ht * 128:(ht + 1) * 128],
                            xb[:, ci, cb:cb + CH], start=(ci == 0), stop=(ci == NCT - 1)),
                            after=(t_w[(e, 0)], t_xb, dep1) if ci == 0 else (),
                            inc=(ci == NCT - 1))
                    dep3 = g_toks[-2] if len(g_toks) >= 2 else fr_psA[s3_]
                    for ci in range(NCT):
                        tk3 = P.op("tensor", lambda t, ht=ht, ci=ci, s3_=s3_, eslot=eslot, cb=cb: t.matmul(
                            psA[s3_][:, :], w3s[eslot][:, ci, ht * 128:(ht + 1) * 128],
                            xb[:, ci, cb:cb + CH], start=(ci == 0), stop=(ci == NCT - 1)),
                            after=(t_w[(e, 1)], dep3) if ci == 0 else (),
                            inc=(ci == NCT - 1))
                    ss = ht % 2
                    dbg_h = None
                    if DEBUG and s == 0 and e == 1 and c == 0 and ht == 0:
                        cpy1 = P.op("vector", lambda v, s1_=s1_: v.tensor_copy(
                            u1_t[:, 0:CH], psA[s1_][:, :]), after=(tk1,), inc=True)
                        d1 = P.dma("sync", io["dbg_H1"], u1_t[:, 0:CH], after=(cpy1,))
                        cpy2 = P.op("vector", lambda v, s3_=s3_: v.tensor_copy(
                            u1_t[:, CH:2 * CH], psA[s3_][:, :]), after=(tk3, d1), inc=True)
                        d2 = P.dma("sync", io["dbg_H3"], u1_t[:, CH:2 * CH], after=(cpy2,))
                        dbg_h = d2
                    tsl = P.op("scalar", lambda s_, ht=ht, s1_=s1_, ss=ss: s_.activation(
                        sil[ss][:, :], psA[s1_][:, :], AF.Silu),
                        after=(tk1, fr_sil[ss], dbg_h,
                               fr_rb[0] if ss == 1 and ht == 1 and c == 0 and e == 0 else None),
                        inc=True)
                    if DEBUG and s == 0 and e == 1 and c == 0 and ht == 0:
                        d3 = P.dma("sync", io["dbg_sil"], sil[ss][:, :], after=(tsl,))
                        tsl = P.op("vector", lambda v: v.nop(), after=(d3, tsl), inc=True)
                    if e < 8:
                        gm = P.op("vector", lambda v, ht=ht, s3_=s3_, ss=ss: v.tensor_tensor(
                            G[:, ht, :], sil[ss][:, :], psA[s3_][:, :], ALU.mult),
                            after=(tsl, tk3, fr_G), inc=True)
                        gm = P.op("vector", lambda v, ht=ht: v.tensor_tensor(
                            G[:, ht, :], G[:, ht, :], psE1[:, :], ALU.mult),
                            after=(gm, bcm), inc=True)
                    else:
                        gm = P.op("vector", lambda v, ht=ht, s3_=s3_, ss=ss: v.tensor_tensor(
                            G[:, ht, :], sil[ss][:, :], psA[s3_][:, :], ALU.mult),
                            after=(tsl, tk3, fr_G), inc=True)
                    fr_sil[ss] = gm
                    g_toks.append(gm)
                if e < 8:
                    fr_psE1 = g_toks[-1]
                dbg_pending = None
                if DEBUG and s == 0 and e == 1 and c == 0:
                    dbg_pending = P.dma("sync", io["dbg_G1"], G[:, :, :], after=g_toks)
                for dt in range(NCT):
                    yslot = dt % 2
                    dep = yadd_toks[-2] if len(yadd_toks) >= 2 else fr_psY[yslot]
                    for ht in range(NCT):
                        tk = P.op("tensor", lambda t, dt=dt, ht=ht, yslot=yslot: t.matmul(
                            psY[yslot][:, :], w2s[:, ht, dt * 128:(dt + 1) * 128],
                            G[:, ht, :], start=(ht == 0), stop=(ht == NCT - 1)),
                            after=(g_toks[ht], t_w[(e, 2)], dep) if True else (),
                            inc=(ht == NCT - 1))
                    if e == 0:
                        yadd_toks.append(P.op("vector", lambda v, dt=dt, cb=cb, yslot=yslot: v.tensor_copy(
                            yacc[:, dt, cb:cb + CH], psY[yslot][:, :]),
                            after=(tk, fr_yacc, t_gTb), inc=True))
                    else:
                        yadd_toks.append(P.op("vector", lambda v, dt=dt, cb=cb, yslot=yslot: v.tensor_add(
                            yacc[:, dt, cb:cb + CH], yacc[:, dt, cb:cb + CH], psY[yslot][:, :]),
                            after=(tk,), inc=True))
                y_this.append(yadd_toks[-1])
                if dbg_pending is not None:
                    fr_G = P.op("vector", lambda v: v.nop(), after=(dbg_pending, tk), inc=True)
                else:
                    fr_G = tk
                fr_psA = [g_toks[-2], g_toks[-2], g_toks[-1], g_toks[-1]]
                fr_psY[0] = yadd_toks[-2]
                fr_psY[1] = yadd_toks[-1]
            fr_w1[eslot] = tk1
            fr_w3[eslot] = tk3
            fr_w2 = tk
        fr_xb = tk1

        if DEBUG and s == 0:
            dbg_y = P.dma("sync", io["dbg_yacc"], yacc[:, :, :], after=(yadd_toks[-1],))
            yadd_toks = yadd_toks + [dbg_y]
        out_last = None
        for c in range(NCHUNK):
            cb = c * CH
            t_xf = P.dma("sync", xf[:, :, :],
                         io["xin"][s, :, :, cb:cb + CH].transpose([1, 0, 2]),
                         after=(fr_w2 if c == 0 else None, out_last))
            res_toks = []
            for ct in range(NCT):
                res_toks.append(P.op("vector", lambda v, ct=ct, cb=cb: v.tensor_add(
                    xf[:, ct, :], xf[:, ct, :], yacc[:, ct, cb:cb + CH]),
                    after=(t_xf, yadd_toks[-1]), inc=True))
            st2 = []
            sq2s = []
            for ct in range(NCT):
                rs = ct % 2
                t_rb = P.op("vector", lambda v, ct=ct: v.tensor_copy(
                    rb[0][:, :], xf[:, ct, :]),
                    after=(res_toks[ct], st2[-1] if st2 else None,
                           fr_rb[0], fr_sil[1] if ct == 0 and c == 0 else None), inc=True)
                t_r2 = P.op("scalar", lambda s_, ct=ct: s_.activation(
                    r2b[0][:, :], xf[:, ct, :], AF.Square),
                    after=(res_toks[ct], sq2s[-1] if sq2s else None,
                           fr_r2b[0]), inc=True)
                st2.append(P.op("tensor", lambda t, ct=ct: t.matmul(
                    psE1[0:1, :], ones_b[:, 0:1], rb[0][:, :],
                    start=(ct == 0), stop=(ct == NCT - 1)),
                    after=(t_rb, t_ob) + ((fr_psE1,) if ct == 0 else ()), inc=True))
                sq2s.append(P.op("tensor", lambda t, ct=ct: t.matmul(
                    psE2[0:1, :], ones_b[:, 0:1], r2b[0][:, :],
                    start=(ct == 0), stop=(ct == NCT - 1)),
                    after=(t_r2,) + ((fr_psE2,) if ct == 0 else ()), inc=True))
                fr_rb[0] = st2[-1]
                fr_r2b[0] = sq2s[-1]
            f1 = P.op("vector", lambda v: v.tensor_copy(ssum[:, :], psE1[0:1, :]),
                      after=(st2[-1],), inc=True)
            f2 = P.op("vector", lambda v: v.tensor_copy(ssq[:, :], psE2[0:1, :]),
                      after=(sq2s[-1],), inc=True)
            f3 = P.op("tensor", lambda t: t.matmul(
                psE1[:, :], ones_f[0:1, :], ssum[:, :], start=True, stop=True),
                after=(f1, t_of), inc=True)
            f4 = P.op("tensor", lambda t: t.matmul(
                psE2[:, :], ones_f[0:1, :], ssq[:, :], start=True, stop=True),
                after=(f2,), inc=True)
            f5 = P.op("vector", lambda v: v.tensor_scalar(
                mb, psE1[:, :], 1.0 / D, None, ALU.mult), after=(f3,), inc=True)
            fr_psE1 = f5
            f6 = P.op("scalar", lambda s_: s_.activation(m2, mb, AF.Square),
                      after=(f5,), inc=True)
            f7 = P.op("vector", lambda v: v.scalar_tensor_tensor(
                vb, psE2[:, :], 1.0 / D, m2, ALU.mult, ALU.subtract),
                after=(f4, f6), inc=True)
            fr_psE2 = f7
            f8 = P.op("scalar", lambda s_: s_.activation(vb, vb, AF.Ln, bias=eps_t[:, :]),
                      after=(f7, t_eps0), inc=True)
            f9 = P.op("scalar", lambda s_: s_.activation(rstd, vb, AF.Exp, scale=-0.5),
                      after=(f8,), inc=True)
            out_toks = []
            for ct in range(NCT):
                u1 = P.op("vector", lambda v, ct=ct: v.tensor_sub(
                    xf[:, ct, :], xf[:, ct, :], mb), after=(f5, res_toks[ct]), inc=True)
                out_toks.append(P.op("vector", lambda v, ct=ct: v.tensor_mul(
                    xf[:, ct, :], xf[:, ct, :], rstd), after=(u1, f9), inc=True))
            t_out = P.dma("sync", io["xout"][s, :, :, cb:cb + CH].transpose([1, 0, 2]),
                          xf[:, :, :], after=out_toks)
            out_last = t_out
        fr_w2 = out_last
        fr_rout = out_last
        fr_yacc = res_toks[-1]
        fr_gT = res_toks[-1]
    final_tok = ("d", "sync", P.dn["sync"])
    P.op("sync", lambda e: e.nop(), after=(final_tok,))
    return final_tok


def build_l1f_program():
    nc = bass.Bass("TRN2", target_bir_lowering=False, debug=False)
    io = {}
    io["xin"] = nc.dram_tensor("xin", [NSEQ, NCT, 128, T], F32, kind="ExternalInput").ap()
    io["xout"] = nc.dram_tensor("xout", [NSEQ, NCT, 128, T], F32, kind="ExternalOutput").ap()
    io["router_t"] = nc.dram_tensor("router_t", [NCT, 128, 8], F32, kind="ExternalInput").ap()
    io["ebias_b"] = nc.dram_tensor("ebias_b", [128, 8], F32, kind="ExternalInput").ap()
    io["w1"] = nc.dram_tensor("w1", [8, NCT, 128, 1024], F32, kind="ExternalInput").ap()
    io["w3"] = nc.dram_tensor("w3", [8, NCT, 128, 1024], F32, kind="ExternalInput").ap()
    io["w2"] = nc.dram_tensor("w2", [8, NCT, 128, 1024], F32, kind="ExternalInput").ap()
    io["sw1"] = nc.dram_tensor("sw1", [NCT, 128, 1024], F32, kind="ExternalInput").ap()
    io["sw3"] = nc.dram_tensor("sw3", [NCT, 128, 1024], F32, kind="ExternalInput").ap()
    io["sw2"] = nc.dram_tensor("sw2", [NCT, 128, 1024], F32, kind="ExternalInput").ap()
    io["e8sel"] = nc.dram_tensor("e8sel", [8, 8 * 128], BF16, kind="ExternalInput").ap()
    io["ident_f32"] = nc.dram_tensor("ident_f32", [128, 128], F32, kind="ExternalInput").ap()
    io["ones_bf16"] = nc.dram_tensor("ones_bf16", [128, 128], BF16, kind="ExternalInput").ap()
    io["ones_f32"] = nc.dram_tensor("ones_f32", [128, 128], F32, kind="ExternalInput").ap()
    if DEBUG:
        io["dbg_gTb"] = nc.dram_tensor("dbg_gTb", [8, T], BF16, kind="ExternalOutput").ap()
        io["dbg_yacc"] = nc.dram_tensor("dbg_yacc", [128, NCT, T], F32, kind="ExternalOutput").ap()
        io["dbg_scores"] = nc.dram_tensor("dbg_scores", [128, 128], F32, kind="ExternalOutput").ap()
        io["dbg_w11"] = nc.dram_tensor("dbg_w11", [128, NCT, 1024], BF16, kind="ExternalOutput").ap()
        io["dbg_G1"] = nc.dram_tensor("dbg_G1", [128, NCT, CH], BF16, kind="ExternalOutput").ap()
        io["dbg_H1"] = nc.dram_tensor("dbg_H1", [128, CH], F32, kind="ExternalOutput").ap()
        io["dbg_H3"] = nc.dram_tensor("dbg_H3", [128, CH], F32, kind="ExternalOutput").ap()
        io["dbg_sil"] = nc.dram_tensor("dbg_sil", [128, CH], BF16, kind="ExternalOutput").ap()
    stack = ExitStack()
    P = Prog(nc)
    P.make_sems(stack)
    build_l1f(nc, P, stack, io)
    block = stack.enter_context(nc.Block())
    P.finalize(block)
    stack.close()
    return nc


def l1f_in_maps(inputs, x3_core):
    ident, ones_b, ones_f, maskT, maskLK, sel, eb = _host_consts()
    e8sel = np.zeros((8, 8 * 128), dtype=ml_dtypes.bfloat16)
    for e in range(8):
        e8sel[e, e * 128:(e + 1) * 128] = 1.0
    rt = np.asarray(inputs["L1_ffn_router_w"])
    return {
        "xin": x3_core,
        "router_t": np.ascontiguousarray(rt.reshape(NCT, 128, 8)),
        "ebias_b": np.ascontiguousarray(np.broadcast_to(
            np.asarray(inputs["L1_ffn_expert_bias"], np.float32)[None, :], (128, 8))),
        "w1": np.ascontiguousarray(np.asarray(inputs["L1_ffn_w1"]).reshape(8, NCT, 128, 1024)),
        "w3": np.ascontiguousarray(np.asarray(inputs["L1_ffn_w3"]).reshape(8, NCT, 128, 1024)),
        "w2": np.ascontiguousarray(np.asarray(inputs["L1_ffn_w2"]).reshape(8, NCT, 128, 1024)),
        "sw1": np.ascontiguousarray(np.asarray(inputs["L1_ffn_sw1"]).reshape(NCT, 128, 1024)),
        "sw3": np.ascontiguousarray(np.asarray(inputs["L1_ffn_sw3"]).reshape(NCT, 128, 1024)),
        "sw2": np.ascontiguousarray(np.asarray(inputs["L1_ffn_sw2"]).reshape(NCT, 128, 1024)),
        "e8sel": e8sel,
        "ident_f32": ident, "ones_bf16": ones_b, "ones_f32": ones_f,
    }
